# version 02
# baseline (speedup 1.0000x reference)
import contextlib
import numpy as np
import ml_dtypes

import concourse.bass as bass
import concourse.mybir as mybir
from concourse.bass_utils import run_bass_kernel_spmd

F32 = mybir.dt.float32
BF16 = mybir.dt.bfloat16
AF = mybir.ActivationFunctionType
ALU = mybir.AluOpType
AX = mybir.AxisListType

D = 1024
NCH = 8
B = 2
S = 8192
NCORES = 8
T = 2048
TT = 512
NH = 16
HD = 64
NF = 256
BF = ml_dtypes.bfloat16
ENGINES = ("sync", "scalar", "vector", "tensor", "gpsimd")
DMA_ENGINES = ("sync", "scalar", "gpsimd")


def cdiv(a, b):
    return -(-a // b)


def bcast(ap2d, n, tw):
    return ap2d.rearrange("p (o t) -> p o t", o=1).broadcast_to((ap2d.shape[0], n, tw))


def chunks(ap2d, n, tw):
    return ap2d.rearrange("p (k t) -> p k t", k=n)



class _Op:
    __slots__ = ("engine", "fn", "idx", "is_dma", "slot", "slot_seq", "mark", "deps")

    def __init__(self, idx, engine, fn, is_dma, slot):
        self.idx = idx
        self.engine = engine
        self.fn = fn
        self.is_dma = is_dma
        self.slot = slot
        self.slot_seq = 0
        self.mark = None
        self.deps = ()


SAFE_EDGES = False
EDGE_ENGINES = {"vector", "scalar"}


def _need_edge(d, o):
    if SAFE_EDGES:
        return True
    if d.engine != o.engine:
        return True
    if d.engine in EDGE_ENGINES:
        return True
    return False


class Prog:
    def __init__(self, nc, ctx):
        self.nc = nc
        self.ctx = ctx
        self.ops = []
        self.last_writer = {}
        self.readers = {}
        self.prog_sem = {e: ctx.enter_context(nc.semaphore(f"pg_{e}")) for e in ENGINES}
        self.slot_sems = {}
        self.slot_counts = {}

    def op(self, engine, fn, reads=(), writes=(), deps=()):
        return self._record(engine, fn, tuple(reads), tuple(writes), False, None, deps)

    def dma(self, engine, out, in_, reads=(), writes=(), deps=(), slot=None):
        assert engine in DMA_ENGINES
        if slot is None:
            slot = id(writes[0]) if writes else id(out)
        return self._record(engine, lambda eng: eng.dma_start(out=out, in_=in_),
                            tuple(reads), tuple(writes), True, slot, deps)

    def _record(self, engine, fn, reads, writes, is_dma, slot, extra_deps):
        o = _Op(len(self.ops), engine, fn, is_dma, slot)
        deps = set()
        for r in reads:
            w = self.last_writer.get(id(r))
            if w is not None:
                deps.add(w)
        for w in writes:
            for rd in self.readers.get(id(w), ()):
                deps.add(rd)
            lw = self.last_writer.get(id(w))
            if lw is not None:
                deps.add(lw)
        for d in extra_deps:
            if d is not None:
                deps.add(d)
        deps.discard(o)
        o.deps = tuple(sorted(deps, key=lambda q: q.idx))
        self.ops.append(o)
        for r in reads:
            self.readers.setdefault(id(r), []).append(o)
        for w in writes:
            self.last_writer[id(w)] = o
            self.readers[id(w)] = []
        if is_dma:
            if slot not in self.slot_sems:
                self.slot_sems[slot] = self.ctx.enter_context(
                    self.nc.semaphore(f"dma{len(self.slot_sems)}"))
                self.slot_counts[slot] = 0
            self.slot_counts[slot] += 1
            o.slot_seq = self.slot_counts[slot]
        return o

    def finish(self, engine="gpsimd"):
        last_per_engine = {}
        for o in self.ops:
            last_per_engine[o.engine] = o
        deps = [o for o in self.ops if o.is_dma]
        deps += [v for k, v in last_per_engine.items() if k != engine]
        sems = [(self.slot_sems[s], 16 * self.slot_counts[s]) for s in self.slot_sems]

        def fn(eng):
            for sem, val in sems:
                eng.wait_ge(sem, val)
            return None
        self._record(engine, fn, (), (), False, None, tuple(deps))

    def emit(self, block):
        marks = {e: 0 for e in ENGINES}
        flagged = set()
        for o in self.ops:
            for d in o.deps:
                if (not d.is_dma) and _need_edge(d, o):
                    flagged.add(d)
        for o in self.ops:
            if o in flagged:
                marks[o.engine] += 1
                o.mark = marks[o.engine]
        per_engine = {e: [o for o in self.ops if o.engine == e] for e in ENGINES}
        watermark = {e: {} for e in ENGINES}

        def make_body(eng_name):
            def run(eng_obj):
                wm = watermark[eng_name]
                for o in per_engine[eng_name]:
                    for d in o.deps:
                        if d.is_dma:
                            sem, val = self.slot_sems[d.slot], 16 * d.slot_seq
                        elif _need_edge(d, o):
                            sem, val = self.prog_sem[d.engine], d.mark
                            if val is None:
                                continue
                        else:
                            continue
                        if wm.get(id(sem), -1) < val:
                            eng_obj.wait_ge(sem, val)
                            wm[id(sem)] = val
                    inst = o.fn(eng_obj)
                    if o.is_dma:
                        inst.then_inc(self.slot_sems[o.slot], 16)
                    elif o.mark is not None:
                        if inst is None:
                            raise RuntimeError("marked op returned no instruction")
                        inst.then_inc(self.prog_sem[o.engine], 1)
            return run

        block.sync(make_body("sync"))
        block.scalar(make_body("scalar"))
        block.vector(make_body("vector"))
        block.tensor(make_body("tensor"))
        block.gpsimd(make_body("gpsimd"))


def mm_group(p, psum_ap, pairs, reads, writes, deps=()):
    def fn(t):
        inst = None
        n = len(pairs)
        for i, (l, r) in enumerate(pairs):
            inst = t.matmul(psum_ap, l, r, start=(i == 0), stop=(i == n - 1))
        return inst
    return p.op("tensor", fn, reads=reads, writes=writes, deps=deps)



def host_constants():
    c = {}
    c["c_ones_row_bf"] = np.ones((1, 128), dtype=BF)
    c["c_ones_col_bf"] = np.ones((128, 1), dtype=BF)
    sel = np.zeros((128, 8 * 16), dtype=BF)
    selT = np.zeros((16, 8 * 128), dtype=BF)
    for ch in range(8):
        for pp in range(128):
            j = 2 * ch + pp // 64
            sel[pp, ch * 16 + j] = 1
            selT[j, ch * 128 + pp] = 1
    c["c_hsel"] = sel
    c["c_hselT"] = selT
    c["c_eps"] = np.array([[1e-5]], dtype=np.float32)
    return c


def declare_consts(nc):
    cap = {}
    for name, arr in host_constants().items():
        dt = BF16 if arr.dtype == BF else F32
        cap[name] = nc.dram_tensor(name, list(arr.shape), dt, kind="ExternalInput").ap()
    return cap


class NS:
    pass


def load_common(p, sb, cap):
    cm = NS()
    cm.onesb = sb("onesb", [1, 128], BF16)
    cm.onescb = sb("onescb", [128, 1], BF16)
    cm.hsel = sb("hsel", [128, 8 * 16], BF16)
    cm.hselT = sb("hselT", [16, 8 * 128], BF16)
    cm.epsc = sb("epsc", [1, 1], F32)
    p.dma("sync", cm.onesb[:, :], cap["c_ones_row_bf"], writes=[cm.onesb])
    p.dma("sync", cm.onescb[:, :], cap["c_ones_col_bf"], writes=[cm.onescb])
    p.dma("sync", cm.hsel[:, :], cap["c_hsel"], writes=[cm.hsel])
    p.dma("sync", cm.hselT[:, :], cap["c_hselT"], writes=[cm.hselT])
    p.dma("sync", cm.epsc[:, :], cap["c_eps"], writes=[cm.epsc])
    return cm


def load_w1024(p, w_ap, stage, dst, ncols=D):
    half = NCH // 2
    for hh in range(2):
        p.dma("sync", stage[:, 0:half * ncols].rearrange("p (c n) -> p c n", c=half),
              w_ap[hh * half * 128:(hh + 1) * half * 128, :]
              .rearrange("(c p) n -> p c n", p=128), writes=[stage])
        p.op("vector", lambda v, hh=hh: v.tensor_copy(
            dst[:, hh * half * ncols:(hh + 1) * half * ncols],
            stage[:, 0:half * ncols]),
            reads=[stage], writes=[dst])


def emit_ln3(p, cm, bf_, tw, src3, src_res, dst3, dst_res, gain=None, beta=None):
    dt_, sqbf = bf_.dt_, bf_.sqbf
    r1, r1b, pr1, pbc = bf_.r1, bf_.r1b, bf_.pr1, bf_.pbc
    p.op("vector", lambda v: v.tensor_copy(chunks(sqbf[:, 0:NCH * tw], NCH, tw), src3),
         reads=[src_res], writes=[sqbf])
    mm_group(p, pr1[0:1, 0:tw],
             [(cm.onescb[:, 0:1], sqbf[:, k * tw:(k + 1) * tw]) for k in range(NCH)],
             reads=[sqbf, cm.onescb], writes=[pr1])
    p.op("scalar", lambda s: s.activation(r1b[0:1, 0:tw], pr1[0:1, 0:tw], AF.Copy,
                                          scale=1.0 / D),
         reads=[pr1], writes=[r1b])
    mm_group(p, pbc[:, 0:tw], [(cm.onesb[0:1, 0:128], r1b[0:1, 0:tw])],
             reads=[r1b, cm.onesb], writes=[pbc])
    p.op("vector", lambda v: v.tensor_tensor(chunks(dt_[:, 0:NCH * tw], NCH, tw),
                                             src3, bcast(pbc[:, 0:tw], NCH, tw),
                                             op=ALU.subtract),
         reads=[src_res, pbc], writes=[dt_])
    p.op("scalar", lambda s: s.activation(sqbf[:, 0:NCH * tw], dt_[:, 0:NCH * tw],
                                          AF.Square),
         reads=[dt_], writes=[sqbf])
    mm_group(p, pr1[0:1, 0:tw],
             [(cm.onescb[:, 0:1], sqbf[:, k * tw:(k + 1) * tw]) for k in range(NCH)],
             reads=[sqbf, cm.onescb], writes=[pr1])
    p.op("scalar", lambda s: s.activation(r1[0:1, 0:tw], pr1[0:1, 0:tw], AF.Ln,
                                          scale=1.0 / D, bias=cm.epsc[0:1, 0:1]),
         reads=[pr1, cm.epsc], writes=[r1])
    p.op("scalar", lambda s: s.activation(r1b[0:1, 0:tw], r1[0:1, 0:tw], AF.Exp,
                                          scale=-0.5),
         reads=[r1], writes=[r1b])
    mm_group(p, pbc[:, 0:tw], [(cm.onesb[0:1, 0:128], r1b[0:1, 0:tw])],
             reads=[r1b, cm.onesb], writes=[pbc])
    if gain is None:
        p.op("vector", lambda v: v.tensor_tensor(dst3,
                                                 chunks(dt_[:, 0:NCH * tw], NCH, tw),
                                                 bcast(pbc[:, 0:tw], NCH, tw),
                                                 op=ALU.mult),
             reads=[dt_, pbc], writes=[dst_res])
    else:
        p.op("vector", lambda v: v.tensor_tensor(chunks(dt_[:, 0:NCH * tw], NCH, tw),
                                                 chunks(dt_[:, 0:NCH * tw], NCH, tw),
                                                 bcast(pbc[:, 0:tw], NCH, tw),
                                                 op=ALU.mult),
             reads=[dt_, pbc], writes=[dt_])
        for c in range(NCH):
            out_ap = dst3[:, c:c + 1, :].rearrange("p o t -> p (o t)")
            p.op("vector", lambda v, c=c, out_ap=out_ap: v.tensor_scalar(
                out_ap, dt_[:, c * tw:(c + 1) * tw],
                gain[:, c:c + 1], beta[:, c:c + 1], op0=ALU.mult, op1=ALU.add),
                reads=[dt_, gain, beta], writes=[dst_res])


def emit_ln(p, cm, bf_, tw, dst=None, src=None, gain=None, beta=None):
    xt = bf_.xt if src is None else src
    xnb = bf_.xnb if dst is None else dst
    emit_ln3(p, cm, bf_, tw,
             chunks(xt[:, 0:NCH * tw], NCH, tw), xt,
             chunks(xnb[:, 0:NCH * tw], NCH, tw), xnb, gain=gain, beta=beta)


def emit_headnorm(p, cm, bf_, src_f32, dst_bf, tw, eps_col):
    rows, rowbf, prow, pbc = bf_.rows, bf_.rowbf, bf_.prow, bf_.pbc
    p.op("scalar", lambda s: s.activation(bf_.sqbf[:, 0:NCH * tw], src_f32[:, 0:NCH * tw],
                                          AF.Square),
         reads=[src_f32], writes=[bf_.sqbf])
    mm_group(p, prow[0:16, 0:tw],
             [(cm.hsel[:, ch * 16:(ch + 1) * 16], bf_.sqbf[:, ch * tw:(ch + 1) * tw])
              for ch in range(NCH)],
             reads=[bf_.sqbf, cm.hsel], writes=[prow])
    p.op("scalar", lambda s: s.activation(rows[0:16, 0:tw], prow[0:16, 0:tw], AF.Ln),
         reads=[prow], writes=[rows])
    p.op("scalar", lambda s: s.activation(rows[0:16, 0:tw], rows[0:16, 0:tw], AF.Exp,
                                          scale=0.5),
         reads=[rows], writes=[rows])
    p.op("vector", lambda v: v.tensor_scalar(rows[0:16, 0:tw], rows[0:16, 0:tw],
                                             eps_col, None, op0=ALU.add),
         reads=[rows], writes=[rows])
    p.op("vector", lambda v: v.reciprocal(rows[0:16, 0:tw], rows[0:16, 0:tw]),
         reads=[rows], writes=[rows])
    p.op("vector", lambda v: v.tensor_copy(rowbf[0:16, 0:tw], rows[0:16, 0:tw]),
         reads=[rows], writes=[rowbf])
    mm_group(p, pbc[:, 0:tw],
             [(cm.hselT[0:16, 0:128], rowbf[0:16, 0:tw])],
             reads=[rowbf, cm.hselT], writes=[pbc])


def emit_headnorm_pc(p, cm, bf_, src_f32, dst_bf, tw, eps_col):
    rows, rowbf, prow = bf_.rows, bf_.rowbf, bf_.prow
    p.op("scalar", lambda s: s.activation(bf_.sqbf[:, 0:NCH * tw], src_f32[:, 0:NCH * tw],
                                          AF.Square),
         reads=[src_f32], writes=[bf_.sqbf])
    mm_group(p, prow[0:16, 0:tw],
             [(cm.hsel[:, ch * 16:(ch + 1) * 16], bf_.sqbf[:, ch * tw:(ch + 1) * tw])
              for ch in range(NCH)],
             reads=[bf_.sqbf, cm.hsel], writes=[prow])
    p.op("scalar", lambda s: s.activation(rows[0:16, 0:tw], prow[0:16, 0:tw], AF.Ln),
         reads=[prow], writes=[rows])
    p.op("scalar", lambda s: s.activation(rows[0:16, 0:tw], rows[0:16, 0:tw], AF.Exp,
                                          scale=0.5),
         reads=[rows], writes=[rows])
    p.op("vector", lambda v: v.tensor_scalar(rows[0:16, 0:tw], rows[0:16, 0:tw],
                                             eps_col, None, op0=ALU.add),
         reads=[rows], writes=[rows])
    p.op("vector", lambda v: v.reciprocal(rows[0:16, 0:tw], rows[0:16, 0:tw]),
         reads=[rows], writes=[rows])
    p.op("vector", lambda v: v.tensor_copy(rowbf[0:16, 0:tw], rows[0:16, 0:tw]),
         reads=[rows], writes=[rowbf])
    for ch in range(NCH):
        pb = bf_.pbc
        mm_group(p, pb[:, 0:tw],
                 [(cm.hselT[0:16, ch * 128:(ch + 1) * 128], rowbf[0:16, 0:tw])],
                 reads=[rowbf, cm.hselT], writes=[pb])
        p.op("vector", lambda v, ch=ch, pb=pb: v.tensor_tensor(
            dst_bf[:, ch * tw:(ch + 1) * tw], src_f32[:, ch * tw:(ch + 1) * tw],
            pb[:, 0:tw], op=ALU.mult),
            reads=[src_f32, pb], writes=[dst_bf])



def build_performer(n_own, n_batch):
    nc = bass.Bass("TRN2", target_bir_lowering=False)
    xb = nc.dram_tensor("xb_fm", [D, n_batch], F32, kind="ExternalInput").ap()
    xo = nc.dram_tensor("xo_fm", [D, n_own], F32, kind="ExternalInput").ap()
    Wq = nc.dram_tensor("Wq", [D, D], F32, kind="ExternalInput").ap()
    Wk = nc.dram_tensor("Wk", [D, D], F32, kind="ExternalInput").ap()
    Wv = nc.dram_tensor("Wv", [D, D], F32, kind="ExternalInput").ap()
    Wo = nc.dram_tensor("Wo", [D, D], F32, kind="ExternalInput").ap()
    bq = nc.dram_tensor("bq", [128, NCH], F32, kind="ExternalInput").ap()
    bk = nc.dram_tensor("bk", [128, NCH], F32, kind="ExternalInput").ap()
    bv = nc.dram_tensor("bv", [1, D], F32, kind="ExternalInput").ap()
    bo = nc.dram_tensor("bo", [128, NCH], F32, kind="ExternalInput").ap()
    projT2 = nc.dram_tensor("projT2", [128, NH * NF], F32, kind="ExternalInput").ap()
    cap = declare_consts(nc)
    xout = nc.dram_tensor("xout_fm", [D, n_own], F32, kind="ExternalOutput").ap()

    ntb = cdiv(n_batch, TT)
    ntq = cdiv(n_own, TT)

    with contextlib.ExitStack() as ctx:
        def sb(name, shape, dt=F32):
            return ctx.enter_context(nc.sbuf_tensor(name, shape, dt))

        def psd(name, shape):
            return ctx.enter_context(nc.psum_tensor(name, shape, F32))

        bf_ = NS()
        bf_.xt = sb("xt", [128, NCH * TT])
        bf_.dt_ = sb("dt", [128, NCH * TT])
        bf_.sqbf = sb("sqbf", [128, NCH * TT], BF16)
        bf_.xnb = sb("xnb", [128, NCH * TT], BF16)
        bf_.r1 = sb("r1", [1, TT])
        bf_.r1b = sb("r1b", [1, TT], BF16)
        bf_.rows = sb("rows", [16, TT])
        bf_.rowbf = sb("rowbf", [16, TT], BF16)
        kfm = bf_.dt_
        knb = sb("knb", [128, NCH * TT], BF16)
        kpt = sb("kpt", [128, NH * NF], BF16)
        vtok = sb("vtok", [128, NH * (HD + 1)], BF16)
        kvs = sb("kvs", [128, NH * 2 * (HD + 1)])
        kvb = sb("kvb", [128, NH * 2 * (HD + 1)], BF16)
        qpb = sb("qpb", [128, 2 * TT], BF16)
        att = sb("att", [128, NCH * TT], BF16)
        rbs = sb("rbs", [64, TT])
        epsq = sb("epsq", [16, 1])
        epsn = sb("epsn", [1, 1])
        wst = sb("wst", [128, (NCH // 2) * D])
        wqb = sb("wqb", [128, NCH * D], BF16)
        wkb = sb("wkb", [128, NCH * D], BF16)
        wvb = sb("wvb", [128, NCH * D], BF16)
        wob = sb("wob", [128, NCH * D], BF16)
        pjb = sb("pjb", [128, NH * NF], BF16)
        bqs = sb("bqs", [128, NCH])
        bks = sb("bks", [128, NCH])
        bos = sb("bos", [128, NCH])
        bvb = sb("bvb", [1, D], BF16)

        bf_.pr1 = psd("pr1", [1, TT])
        bf_.pbc = psd("pbc", [128, TT])
        bf_.prow = psd("prow", [16, TT])
        pm0 = psd("pm0", [128, TT])
        pm1 = psd("pm1", [128, TT])
        pv = psd("pv", [128, D])
        pq65 = psd("pq65", [65, TT])

        p = Prog(nc, ctx)
        cm = load_common(p, sb, cap)
        p.op("vector", lambda v: v.memset(epsq[:, :], 1e-8), writes=[epsq])

        for w_ap, dst in [(Wq, wqb), (Wk, wkb), (Wv, wvb), (Wo, wob)]:
            load_w1024(p, w_ap, wst, dst)
        p.dma("sync", wst[:, 0:NH * NF], projT2, writes=[wst])
        p.op("vector", lambda v: v.tensor_copy(pjb[:, :], wst[:, 0:NH * NF]),
             reads=[wst], writes=[pjb])
        p.dma("sync", bqs[:, :], bq, writes=[bqs])
        p.dma("sync", bks[:, :], bk, writes=[bks])
        p.dma("sync", bos[:, :], bo, writes=[bos])
        p.dma("sync", wst[0:1, 0:D], bv, writes=[wst])
        p.op("vector", lambda v: v.tensor_copy(bvb[:, :], wst[0:1, 0:D]),
             reads=[wst], writes=[bvb])

        def proj_1024(src_bf, wdst, bias_cols, out_f32, tw, pms):
            for m in range(NCH):
                pm = pms[m % 2]
                mm_group(p, pm[:, 0:tw],
                         [(wdst[:, k * D + m * 128:k * D + (m + 1) * 128],
                           src_bf[:, k * tw:(k + 1) * tw]) for k in range(NCH)],
                         reads=[wdst, src_bf], writes=[pm])
                p.op("vector", lambda v, m=m, pm=pm: v.tensor_scalar(
                    out_f32[:, m * tw:(m + 1) * tw], pm[:, 0:tw],
                    bias_cols[:, m:m + 1], None, op0=ALU.add),
                    reads=[pm, bias_cols], writes=[out_f32])

        for i in range(ntb):
            tw = TT
            t0 = i * TT
            p.dma("sync", bf_.xt[:, 0:NCH * tw].rearrange("p (c t) -> p c t", c=NCH),
                  xb[:, t0:t0 + tw].rearrange("(c p) t -> p c t", p=128),
                  writes=[bf_.xt])
            emit_ln(p, cm, bf_, tw)
            proj_1024(bf_.xnb, wkb, bks, kfm, tw, (pm0, pm1))
            emit_headnorm_pc(p, cm, bf_, kfm, knb, tw, epsq[0:16, 0:1])
            for tc in range(tw // 128):
                pairs = []
                for k in range(NCH):
                    for nhalf in range(2):
                        pass
                for half in range(2):
                    mm_group(p, pv[:, half * 512:(half + 1) * 512],
                             [(bf_.xnb[:, k * tw + tc * 128:k * tw + tc * 128 + 128],
                               wvb[:, k * D + half * 512:k * D + (half + 1) * 512])
                              for k in range(NCH)]
                             + [(cm.onesb[0:1, 0:128], bvb[0:1, half * 512:(half + 1) * 512])],
                             reads=[bf_.xnb, wvb, cm.onesb, bvb], writes=[pv])
                p.op("vector", lambda v: v.memset(vtok[:, :], 1.0), writes=[vtok])
                p.op("vector", lambda v: v.tensor_copy(
                    vtok[:, :].rearrange("p (h f) -> p h f", h=NH)[:, :, 0:HD],
                    pv[:, 0:D].rearrange("p (h f) -> p h f", h=NH)),
                    reads=[pv], writes=[vtok])
                for h in range(NH):
                    ch = h // 2
                    base = 64 * (h % 2)
                    pm = pm0 if h % 2 == 0 else pm1
                    mm_group(p, pm[:, 0:NF],
                             [(knb[base:base + 64,
                                   ch * tw + tc * 128:ch * tw + tc * 128 + 128],
                               pjb[base:base + 64, h * NF:(h + 1) * NF])],
                             reads=[knb, pjb], writes=[pm])
                    p.op("vector", lambda v, h=h, pm=pm: v.tensor_scalar(
                        kpt[:, h * NF:(h + 1) * NF], pm[:, 0:NF], 0.0, 0.1,
                        op0=ALU.max, op1=ALU.add),
                        reads=[pm], writes=[kpt])
                for h in range(NH):
                    pm = pm0 if h % 2 == 0 else pm1
                    mm_group(p, pm[:, 0:(HD + 1)],
                             [(kpt[:, h * NF:h * NF + 128],
                               vtok[:, h * (HD + 1):(h + 1) * (HD + 1)])],
                             reads=[kpt, vtok], writes=[pm])
                    mm_group(p, pm[:, (HD + 1):2 * (HD + 1)],
                             [(kpt[:, h * NF + 128:(h + 1) * NF],
                               vtok[:, h * (HD + 1):(h + 1) * (HD + 1)])],
                             reads=[kpt, vtok], writes=[pm])
                    if i == 0 and tc == 0:
                        p.op("vector", lambda v, h=h, pm=pm: v.tensor_copy(
                            kvs[:, h * 130:(h + 1) * 130], pm[:, 0:130]),
                            reads=[pm], writes=[kvs])
                    else:
                        p.op("vector", lambda v, h=h, pm=pm: v.tensor_tensor(
                            kvs[:, h * 130:(h + 1) * 130],
                            kvs[:, h * 130:(h + 1) * 130],
                            pm[:, 0:130], op=ALU.add),
                            reads=[pm, kvs], writes=[kvs])
        p.op("vector", lambda v: v.tensor_copy(kvb[:, :], kvs[:, :]),
             reads=[kvs], writes=[kvb])

        for j in range(ntq):
            tw = TT
            t0 = j * TT
            p.dma("sync", bf_.xt[:, 0:NCH * tw].rearrange("p (c t) -> p c t", c=NCH),
                  xo[:, t0:t0 + tw].rearrange("(c p) t -> p c t", p=128),
                  writes=[bf_.xt])
            emit_ln(p, cm, bf_, tw)
            proj_1024(bf_.xnb, wqb, bqs, kfm, tw, (pm0, pm1))
            emit_headnorm_pc(p, cm, bf_, kfm, knb, tw, epsq[0:16, 0:1])
            for h in range(NH):
                ch = h // 2
                base = 64 * (h % 2)
                for fc in range(2):
                    pm = pm0 if fc == 0 else pm1
                    mm_group(p, pm[:, 0:tw],
                             [(pjb[base:base + 64, h * NF + fc * 128:h * NF + (fc + 1) * 128],
                               knb[base:base + 64, ch * tw:(ch + 1) * tw])],
                             reads=[pjb, knb], writes=[pm])
                    p.op("vector", lambda v, fc=fc, pm=pm: v.tensor_scalar(
                        qpb[:, fc * tw:(fc + 1) * tw], pm[:, 0:tw], 0.0, 0.1,
                        op0=ALU.max, op1=ALU.add),
                        reads=[pm], writes=[qpb])
                mm_group(p, pq65[0:65, 0:tw],
                         [(kvb[:, h * 130:h * 130 + 65], qpb[:, 0:tw]),
                          (kvb[:, h * 130 + 65:(h + 1) * 130], qpb[:, tw:2 * tw])],
                         reads=[kvb, qpb], writes=[pq65])
                p.op("vector", lambda v: v.tensor_scalar(
                    bf_.r1[0:1, 0:tw], pq65[64:65, 0:tw], 1e-6, None, op0=ALU.max),
                    reads=[pq65], writes=[bf_.r1])
                p.op("vector", lambda v: v.reciprocal(bf_.r1[0:1, 0:tw], bf_.r1[0:1, 0:tw]),
                     reads=[bf_.r1], writes=[bf_.r1])
                p.op("vector", lambda v: v.tensor_copy(bf_.r1b[0:1, 0:tw], bf_.r1[0:1, 0:tw]),
                     reads=[bf_.r1], writes=[bf_.r1b])
                mm_group(p, bf_.pbc[:, 0:tw], [(cm.onesb[0:1, 0:128], bf_.r1b[0:1, 0:tw])],
                         reads=[bf_.r1b, cm.onesb], writes=[bf_.pbc])
                p.op("scalar", lambda s: s.copy(rbs[0:64, 0:tw], bf_.pbc[0:64, 0:tw]),
                     reads=[bf_.pbc], writes=[rbs])
                p.op("vector", lambda v, h=h, ch=ch, base=base: v.tensor_tensor(
                    att[base:base + 64, ch * tw:(ch + 1) * tw],
                    pq65[0:64, 0:tw], rbs[0:64, 0:tw], op=ALU.mult),
                    reads=[pq65, rbs], writes=[att])
            for m in range(NCH):
                pm = pm0 if m % 2 == 0 else pm1
                mm_group(p, pm[:, 0:tw],
                         [(wob[:, k * D + m * 128:k * D + (m + 1) * 128],
                           att[:, k * tw:(k + 1) * tw]) for k in range(NCH)],
                         reads=[wob, att], writes=[pm])
                p.op("vector", lambda v, m=m, pm=pm: v.scalar_tensor_tensor(
                    bf_.xt[:, m * tw:(m + 1) * tw], pm[:, 0:tw],
                    bos[:, m:m + 1], bf_.xt[:, m * tw:(m + 1) * tw],
                    op0=ALU.add, op1=ALU.add),
                    reads=[pm, bos, bf_.xt], writes=[bf_.xt])
            p.dma("sync", xout[:, t0:t0 + tw].rearrange("(c p) t -> p c t", p=128),
                  bf_.xt[:, 0:NCH * tw].rearrange("p (c t) -> p c t", c=NCH),
                  reads=[bf_.xt], writes=[xout])
        p.finish()
        with nc.Block() as block:
            p.emit(block)
    return nc



def _fm(a):
    return np.ascontiguousarray(np.asarray(a).T)


def _cols(b):
    return np.ascontiguousarray(np.asarray(b).reshape(NCH, 128).T)


def run_performer(x_full, w):
    consts = host_constants()
    projT2 = np.asarray(w["projection"]).transpose(0, 2, 1)
    projT2 = projT2.transpose(1, 0, 2).reshape(HD, NH * NF)
    projT2 = np.ascontiguousarray(np.concatenate([projT2, projT2], axis=0),
                                  dtype=np.float32)
    nc = build_performer(T, S)
    in_maps = []
    for c in range(NCORES):
        b = c // 4
        r = c % 4
        m = {
            "xb_fm": _fm(x_full[b]),
            "xo_fm": _fm(x_full[b][r * T:(r + 1) * T]),
            "Wq": np.asarray(w["Wq"]), "Wk": np.asarray(w["Wk"]),
            "Wv": np.asarray(w["Wv"]), "Wo": np.asarray(w["Wo"]),
            "bq": _cols(w["bq"]),
            "bk": _cols(w["bk"]),
            "bv": np.asarray(w["bv"]).reshape(1, D),
            "bo": _cols(w["bo"]),
            "projT2": projT2,
        }
        m.update(consts)
        in_maps.append(m)
    res = run_bass_kernel_spmd(nc, in_maps, core_ids=list(range(NCORES)))
    out = np.empty((B, S, D), dtype=np.float32)
    for c in range(NCORES):
        b, r = c // 4, c % 4
        out[b, r * T:(r + 1) * T] = res.results[c]["xout_fm"].T
    return out



def build_rwkv(n_own):
    nc = bass.Bass("TRN2", target_bir_lowering=False)
    xo = nc.dram_tensor("xo_fm", [D, n_own], F32, kind="ExternalInput").ap()
    xprev = nc.dram_tensor("xprev_cols", [128, NCH], F32, kind="ExternalInput").ap()
    flag = nc.dram_tensor("flag", [128, 1], F32, kind="ExternalInput").ap()
    WrT = nc.dram_tensor("WrT", [D, D], F32, kind="ExternalInput").ap()
    WkT = nc.dram_tensor("WkT", [D, D], F32, kind="ExternalInput").ap()
    WvT = nc.dram_tensor("WvT", [D, D], F32, kind="ExternalInput").ap()
    mur = nc.dram_tensor("mur", [128, NCH], F32, kind="ExternalInput").ap()
    muk = nc.dram_tensor("muk", [128, NCH], F32, kind="ExternalInput").ap()
    cap = declare_consts(nc)
    xout = nc.dram_tensor("xout_fm", [D, n_own], F32, kind="ExternalOutput").ap()
    nt = cdiv(n_own, TT)

    with contextlib.ExitStack() as ctx:
        def sb(name, shape, dt=F32):
            return ctx.enter_context(nc.sbuf_tensor(name, shape, dt))

        def psd(name, shape):
            return ctx.enter_context(nc.psum_tensor(name, shape, F32))

        bf_ = NS()
        bf_.xt = sb("xt", [128, NCH * TT])
        bf_.dt_ = sb("dt", [128, NCH * TT])
        bf_.sqbf = sb("sqbf", [128, NCH * TT], BF16)
        bf_.xnb = sb("xnb", [128, NCH * TT], BF16)
        bf_.r1 = sb("r1", [1, TT])
        bf_.r1b = sb("r1b", [1, TT], BF16)
        xnf = sb("xnf", [128, NCH * TT])
        xsh = sb("xsh", [128, NCH * TT])
        xrb = sb("xrb", [128, NCH * TT], BF16)
        xkb = sb("xkb", [128, NCH * TT], BF16)
        rfm = sb("rfm", [128, NCH * TT])
        kab = sb("kab", [128, NCH * TT], BF16)
        carry = sb("carry", [128, NCH])
        wst = sb("wst", [128, 4 * D])
        wrb = sb("wrb", [128, NCH * D], BF16)
        wkb = sb("wkb", [128, NCH * D], BF16)
        wvb = sb("wvb", [128, NCH * D], BF16)
        murc = sb("murc", [128, NCH])
        mukc = sb("mukc", [128, NCH])
        flagc = sb("flagc", [128, 1])
        pvt = sb("pvt", [128, NCH])

        bf_.pr1 = psd("pr1", [1, TT])
        bf_.pbc = psd("pbc", [128, TT])
        pm0 = psd("pm0", [128, TT])
        pm1 = psd("pm1", [128, TT])

        p = Prog(nc, ctx)
        cm = load_common(p, sb, cap)
        for w_ap, dstb in [(WrT, wrb), (WkT, wkb), (WvT, wvb)]:
            load_w1024(p, w_ap, wst, dstb)
        p.dma("sync", murc[:, :], mur, writes=[murc])
        p.dma("sync", mukc[:, :], muk, writes=[mukc])
        p.dma("sync", flagc[:, :], flag, writes=[flagc])
        p.dma("sync", pvt[:, :], xprev, writes=[pvt])

        p.op("vector", lambda v: v.tensor_copy(bf_.xt[:, 0:NCH], pvt[:, :]),
             reads=[pvt], writes=[bf_.xt])
        emit_ln(p, cm, bf_, 1, dst=carry)
        p.op("vector", lambda v: v.tensor_scalar(carry[:, :], carry[:, :],
                                                 flagc[:, 0:1], None, op0=ALU.mult),
             reads=[carry, flagc], writes=[carry])

        for j in range(nt):
            tw = TT
            t0 = j * TT
            p.dma("sync", bf_.xt[:, 0:NCH * tw].rearrange("p (c t) -> p c t", c=NCH),
                  xo[:, t0:t0 + tw].rearrange("(c p) t -> p c t", p=128),
                  writes=[bf_.xt])
            emit_ln(p, cm, bf_, tw, dst=xnf)
            p.op("vector", lambda v: v.tensor_copy(
                chunks(xsh[:, 0:NCH * tw], NCH, tw)[:, :, 1:tw],
                chunks(xnf[:, 0:NCH * tw], NCH, tw)[:, :, 0:tw - 1]),
                reads=[xnf], writes=[xsh])
            p.op("vector", lambda v: v.tensor_copy(
                chunks(xsh[:, 0:NCH * tw], NCH, tw)[:, :, 0:1],
                carry[:, :].rearrange("p (c o) -> p c o", c=NCH)),
                reads=[carry], writes=[xsh])
            p.op("vector", lambda v: v.tensor_copy(
                carry[:, :].rearrange("p (c o) -> p c o", c=NCH),
                chunks(xnf[:, 0:NCH * tw], NCH, tw)[:, :, tw - 1:tw]),
                reads=[xnf], writes=[carry])
            p.op("vector", lambda v: v.tensor_tensor(bf_.dt_[:, 0:NCH * tw],
                                                     xnf[:, 0:NCH * tw],
                                                     xsh[:, 0:NCH * tw],
                                                     op=ALU.subtract),
                 reads=[xnf, xsh], writes=[bf_.dt_])
            for c in range(NCH):
                p.op("vector", lambda v, c=c: v.scalar_tensor_tensor(
                    xrb[:, c * tw:(c + 1) * tw], bf_.dt_[:, c * tw:(c + 1) * tw],
                    murc[:, c:c + 1], xsh[:, c * tw:(c + 1) * tw],
                    op0=ALU.mult, op1=ALU.add),
                    reads=[bf_.dt_, murc, xsh], writes=[xrb])
                p.op("vector", lambda v, c=c: v.scalar_tensor_tensor(
                    xkb[:, c * tw:(c + 1) * tw], bf_.dt_[:, c * tw:(c + 1) * tw],
                    mukc[:, c:c + 1], xsh[:, c * tw:(c + 1) * tw],
                    op0=ALU.mult, op1=ALU.add),
                    reads=[bf_.dt_, mukc, xsh], writes=[xkb])
            for m in range(NCH):
                pm = pm0 if m % 2 == 0 else pm1
                mm_group(p, pm[:, 0:tw],
                         [(wrb[:, k * D + m * 128:k * D + (m + 1) * 128],
                           xrb[:, k * tw:(k + 1) * tw]) for k in range(NCH)],
                         reads=[wrb, xrb], writes=[pm])
                p.op("scalar", lambda s, m=m, pm=pm: s.activation(
                    rfm[:, m * tw:(m + 1) * tw], pm[:, 0:tw], AF.Sigmoid),
                    reads=[pm], writes=[rfm])
            for m in range(NCH):
                pm = pm0 if m % 2 == 0 else pm1
                mm_group(p, pm[:, 0:tw],
                         [(wkb[:, k * D + m * 128:k * D + (m + 1) * 128],
                           xkb[:, k * tw:(k + 1) * tw]) for k in range(NCH)],
                         reads=[wkb, xkb], writes=[pm])
                p.op("vector", lambda v, m=m, pm=pm: v.tensor_scalar(
                    bf_.dt_[:, m * tw:(m + 1) * tw], pm[:, 0:tw], 0.0, None,
                    op0=ALU.max),
                    reads=[pm], writes=[bf_.dt_])
            p.op("vector", lambda v: v.tensor_tensor(
                kab[:, 0:NCH * tw], bf_.dt_[:, 0:NCH * tw], bf_.dt_[:, 0:NCH * tw],
                op=ALU.mult),
                reads=[bf_.dt_], writes=[kab])
            for m in range(NCH):
                pm = pm0 if m % 2 == 0 else pm1
                mm_group(p, pm[:, 0:tw],
                         [(wvb[:, k * D + m * 128:k * D + (m + 1) * 128],
                           kab[:, k * tw:(k + 1) * tw]) for k in range(NCH)],
                         reads=[wvb, kab], writes=[pm])
                p.op("vector", lambda v, m=m, pm=pm: v.tensor_tensor(
                    xsh[:, m * tw:(m + 1) * tw], pm[:, 0:tw],
                    rfm[:, m * tw:(m + 1) * tw], op=ALU.mult),
                    reads=[pm, rfm], writes=[xsh])
            p.op("vector", lambda v: v.tensor_tensor(
                bf_.xt[:, 0:NCH * tw], bf_.xt[:, 0:NCH * tw], xsh[:, 0:NCH * tw],
                op=ALU.add),
                reads=[xsh, bf_.xt], writes=[bf_.xt])
            p.dma("sync", xout[:, t0:t0 + tw].rearrange("(c p) t -> p c t", p=128),
                  bf_.xt[:, 0:NCH * tw].rearrange("p (c t) -> p c t", c=NCH),
                  reads=[bf_.xt], writes=[xout])
        p.finish()
        with nc.Block() as block:
            p.emit(block)
    return nc


def run_rwkv(x_cur, w):
    consts = host_constants()
    WrT = np.ascontiguousarray(np.asarray(w["W_r"]).T)
    WkT = np.ascontiguousarray(np.asarray(w["W_k"]).T)
    WvT = np.ascontiguousarray(np.asarray(w["W_v"]).T)
    nc = build_rwkv(T)
    in_maps = []
    xf = np.asarray(x_cur).reshape(B * S, D)
    for c in range(NCORES):
        r = c % 4
        start = c * T
        if r == 0:
            xp = np.zeros((128, NCH), dtype=np.float32)
            fl = np.zeros((128, 1), dtype=np.float32)
        else:
            xp = _cols(xf[start - 1])
            fl = np.ones((128, 1), dtype=np.float32)
        m = {
            "xo_fm": _fm(xf[start:start + T]),
            "xprev_cols": xp, "flag": fl,
            "WrT": WrT, "WkT": WkT, "WvT": WvT,
            "mur": _cols(w["mu_r"]), "muk": _cols(w["mu_k"]),
        }
        m.update(consts)
        in_maps.append(m)
    res = run_bass_kernel_spmd(nc, in_maps, core_ids=list(range(NCORES)))
    out = np.empty((B * S, D), dtype=np.float32)
    for c in range(NCORES):
        out[c * T:(c + 1) * T] = res.results[c]["xout_fm"].T
    return out.reshape(B, S, D)



def build_moe(n_own, nexp, topk, act, extra_residual=False, ln_gains=False):
    FH = 4096
    NQ = 4
    HL = 1024
    nhalf = cdiv(n_own, HL)
    nc = bass.Bass("TRN2", target_bir_lowering=False)
    xo = nc.dram_tensor("xo_fm", [D, n_own], F32, kind="ExternalInput").ap()
    if extra_residual:
        xb2 = nc.dram_tensor("xb2_fm", [D, n_own], F32, kind="ExternalInput").ap()
    w1 = nc.dram_tensor("w1", [nexp, D, FH], F32, kind="ExternalInput").ap()
    w2 = nc.dram_tensor("w2", [nexp, FH, D], F32, kind="ExternalInput").ap()
    b1c = nc.dram_tensor("b1c", [nexp, 128, FH // 128], F32, kind="ExternalInput").ap()
    if ln_gains:
        g2c_in = nc.dram_tensor("g2c", [128, NCH], F32, kind="ExternalInput").ap()
        be2c_in = nc.dram_tensor("be2c", [128, NCH], F32, kind="ExternalInput").ap()
    if nexp > 1:
        gw = nc.dram_tensor("gw", [D, nexp], F32, kind="ExternalInput").ap()
        gb = nc.dram_tensor("gb", [nexp, 1], F32, kind="ExternalInput").ap()
        b2r = nc.dram_tensor("b2r", [nexp, D], F32, kind="ExternalInput").ap()
        ident_in = nc.dram_tensor("ident_in", [128, 128], F32, kind="ExternalInput").ap()
        esel_in = nc.dram_tensor("esel_in", [nexp, nexp * 128], BF16, kind="ExternalInput").ap()
    else:
        b2cc = nc.dram_tensor("b2cc", [128, NCH], F32, kind="ExternalInput").ap()
    cap = declare_consts(nc)
    xout = nc.dram_tensor("xout_fm", [D, n_own], F32, kind="ExternalOutput").ap()
    actf = {"gelu": AF.Gelu, "silu": AF.Silu, "relu": AF.Relu}[act]

    with contextlib.ExitStack() as ctx:
        def sb(name, shape, dt=F32):
            return ctx.enter_context(nc.sbuf_tensor(name, shape, dt))

        def psd(name, shape):
            return ctx.enter_context(nc.psum_tensor(name, shape, F32))

        bf_ = NS()
        bf_.xt = sb("xt", [128, NCH * HL])
        bf_.dt_ = sb("dt", [128, NCH * TT])
        bf_.sqbf = sb("sqbf", [128, NCH * TT], BF16)
        bf_.r1 = sb("r1", [1, TT])
        bf_.r1b = sb("r1b", [1, TT], BF16)
        xnbh = sb("xnbh", [128, NCH * HL], BF16)
        hq = sb("hq", [128, NCH * HL], BF16)
        osum = sb("osum", [128, NCH * HL])
        wst = sb("wst", [128, 4 * D])
        w1b = sb("w1b", [128, NCH * HL], BF16)
        w2b = sb("w2b", [128, NCH * HL], BF16)
        b1s = sb("b1s", [128, FH // 128])
        if nexp > 1:
            gwb = sb("gwb", [128, NCH * nexp], BF16)
            gbc = sb("gbc", [nexp, 1])
            lgr = sb("lgr", [nexp, TT])
            ew = sb("ew", [128, nexp])
            sel = sb("sel", [128, nexp])
            cols = sb("cols", [128, 4])
            ewT = sb("ewT", [nexp, HL])
            ewTb = sb("ewTb", [nexp, HL], BF16)
            b2b = sb("b2b", [nexp, D], BF16)
            idf = sb("idf", [128, 128])
            esel = sb("esel", [nexp, nexp * 128], BF16)
        else:
            b2s = sb("b2s", [128, NCH])
        if ln_gains:
            g2s = sb("g2s", [128, NCH])
            be2s = sb("be2s", [128, NCH])

        bf_.pr1 = psd("pr1", [1, TT])
        bf_.pbc = psd("pbc", [128, TT])
        pm0 = psd("pm0", [128, TT])
        pm1 = psd("pm1", [128, TT])
        pewb = psd("pewb", [128, HL])
        pl8 = psd("pl8", [max(nexp, 8), TT])

        p = Prog(nc, ctx)
        cm = load_common(p, sb, cap)
        if nexp > 1:
            load_w1024(p, gw, wst, gwb, ncols=nexp)
            p.dma("sync", gbc[:, :], gb, writes=[gbc])
            p.dma("sync", wst[0:nexp, 0:D], b2r, writes=[wst])
            p.op("vector", lambda v: v.tensor_copy(b2b[:, :], wst[0:nexp, 0:D]),
                 reads=[wst], writes=[b2b])
            p.dma("sync", idf[:, :], ident_in, writes=[idf])
            p.dma("sync", esel[:, :], esel_in, writes=[esel])
        else:
            p.dma("sync", b2s[:, :], b2cc, writes=[b2s])
        if ln_gains:
            p.dma("sync", g2s[:, :], g2c_in, writes=[g2s])
            p.dma("sync", be2s[:, :], be2c_in, writes=[be2s])

        for h in range(nhalf):
            h0 = h * HL
            xt3 = chunks(bf_.xt[:, 0:NCH * HL], NCH, HL)
            xn3 = chunks(xnbh[:, 0:NCH * HL], NCH, HL)
            p.dma("sync", xt3, xo[:, h0:h0 + HL].rearrange("(c p) t -> p c t", p=128),
                  writes=[bf_.xt])
            for ts in range(HL // TT):
                emit_ln3(p, cm, bf_, TT,
                         xt3[:, :, ts * TT:(ts + 1) * TT], bf_.xt,
                         xn3[:, :, ts * TT:(ts + 1) * TT], xnbh,
                         gain=g2s if ln_gains else None,
                         beta=be2s if ln_gains else None)
                if nexp > 1:
                    mm_group(p, pl8[0:nexp, 0:TT],
                             [(gwb[:, k * nexp:(k + 1) * nexp],
                               xnbh[:, k * HL + ts * TT:k * HL + (ts + 1) * TT])
                              for k in range(NCH)],
                             reads=[gwb, xnbh], writes=[pl8])
                    p.op("vector", lambda v: v.tensor_scalar(
                        lgr[0:nexp, 0:TT], pl8[0:nexp, 0:TT], gbc[0:nexp, 0:1],
                        None, op0=ALU.add),
                        reads=[pl8, gbc], writes=[lgr])
                    for tc in range(TT // 128):
                        ptr = bf_.pbc[:, 0:nexp]
                        p.op("tensor", lambda t, tc=tc, ptr=ptr: t.transpose(
                            ptr, lgr[0:nexp, tc * 128:(tc + 1) * 128],
                            idf[0:nexp, 0:nexp]),
                            reads=[lgr, idf], writes=[bf_.pbc])
                        p.op("vector", lambda v, ptr=ptr: v.reduce_max(
                            cols[:, 0:1], ptr, axis=AX.X),
                            reads=[bf_.pbc], writes=[cols])
                        p.op("vector", lambda v: v.tensor_scalar(
                            cols[:, 1:2], cols[:, 0:1], -1.0, None, op0=ALU.mult),
                            reads=[cols], writes=[cols])
                        p.op("scalar", lambda s, ptr=ptr: s.activation(
                            ew[:, :], ptr, AF.Exp, bias=cols[:, 1:2]),
                            reads=[bf_.pbc, cols], writes=[ew])
                        p.op("vector", lambda v: v.reduce_sum(
                            cols[:, 2:3], ew[:, :], axis=AX.X),
                            reads=[ew], writes=[cols])
                        if topk == 2:
                            p.op("vector", lambda v, ptr=ptr: v.tensor_scalar(
                                sel[:, :], ptr, cols[:, 0:1], None, op0=ALU.is_ge),
                                reads=[bf_.pbc, cols], writes=[sel])
                            p.op("vector", lambda v, ptr=ptr: v.scalar_tensor_tensor(
                                sel[:, :], sel[:, :], -1e9, ptr,
                                op0=ALU.mult, op1=ALU.add),
                                reads=[sel, bf_.pbc], writes=[sel])
                            p.op("vector", lambda v: v.reduce_max(
                                cols[:, 3:4], sel[:, :], axis=AX.X),
                                reads=[sel], writes=[cols])
                            p.op("vector", lambda v, ptr=ptr: v.tensor_scalar(
                                sel[:, :], ptr, cols[:, 3:4], None, op0=ALU.is_ge),
                                reads=[bf_.pbc, cols], writes=[sel])
                            p.op("vector", lambda v: v.tensor_tensor(
                                ew[:, :], ew[:, :], sel[:, :], op=ALU.mult),
                                reads=[sel, ew], writes=[ew])
                            p.op("vector", lambda v: v.reduce_sum(
                                cols[:, 2:3], ew[:, :], axis=AX.X),
                                reads=[ew], writes=[cols])
                            p.op("vector", lambda v: v.reciprocal(
                                cols[:, 2:3], cols[:, 2:3]),
                                reads=[cols], writes=[cols])
                            p.op("vector", lambda v: v.tensor_scalar(
                                ew[:, :], ew[:, :], cols[:, 2:3], None, op0=ALU.mult),
                                reads=[cols, ew], writes=[ew])
                        else:
                            p.op("vector", lambda v, ptr=ptr: v.tensor_scalar(
                                sel[:, :], ptr, cols[:, 0:1], None, op0=ALU.is_ge),
                                reads=[bf_.pbc, cols], writes=[sel])
                            p.op("vector", lambda v: v.reciprocal(
                                cols[:, 2:3], cols[:, 2:3]),
                                reads=[cols], writes=[cols])
                            p.op("vector", lambda v: v.tensor_tensor(
                                ew[:, :], ew[:, :], sel[:, :], op=ALU.mult),
                                reads=[sel, ew], writes=[ew])
                            p.op("vector", lambda v: v.tensor_scalar(
                                ew[:, :], ew[:, :], cols[:, 2:3], None, op0=ALU.mult),
                                reads=[cols, ew], writes=[ew])
                        p.op("tensor", lambda t, tc=tc: t.transpose(
                            pl8[0:128 if False else nexp, 0:128] if False else pl8[0:nexp, 0:128],
                            ew[:, :], idf[:, :]),
                            reads=[ew, idf], writes=[pl8])
                        p.op("vector", lambda v, ts=ts, tc=tc: v.tensor_copy(
                            ewT[0:nexp, ts * TT + tc * 128:ts * TT + (tc + 1) * 128],
                            pl8[0:nexp, 0:128]),
                            reads=[pl8], writes=[ewT])
                    p.op("vector", lambda v, ts=ts: v.tensor_copy(
                        ewTb[0:nexp, ts * TT:(ts + 1) * TT],
                        ewT[0:nexp, ts * TT:(ts + 1) * TT]),
                        reads=[ewT], writes=[ewTb])

            first = [True]
            for e in range(nexp):
                if nexp > 1:
                    for tt in range(2):
                        mm_group(p, pewb[:, tt * TT:(tt + 1) * TT],
                                 [(esel[0:nexp, e * 128:(e + 1) * 128],
                                   ewTb[0:nexp, tt * TT:(tt + 1) * TT])],
                                 reads=[esel, ewTb], writes=[pewb])
                    p.dma("sync", b1s[:, :], b1c[e:e + 1, :, :]
                          .rearrange("o p q -> (o p) q"), writes=[b1s])
                else:
                    p.dma("sync", b1s[:, :], b1c[0:1, :, :]
                          .rearrange("o p q -> (o p) q"), writes=[b1s])
                for q in range(NQ):
                    w1q = w1[e:e + 1, :, q * HL:(q + 1) * HL].rearrange("o r c -> (o r) c")
                    load_w1024(p, w1q, wst, w1b, ncols=HL)
                    for m in range(NCH):
                        hidx = q * 8 + m
                        for tt in range(2):
                            pm = pm0 if (m * 2 + tt) % 2 == 0 else pm1
                            mm_group(p, pm[:, 0:TT],
                                     [(w1b[:, k * HL + m * 128:k * HL + (m + 1) * 128],
                                       xnbh[:, k * HL + tt * TT:k * HL + (tt + 1) * TT])
                                      for k in range(NCH)],
                                     reads=[w1b, xnbh], writes=[pm])
                            p.op("scalar", lambda s, m=m, tt=tt, pm=pm, hidx=hidx:
                                 s.activation(hq[:, m * HL + tt * TT:m * HL + (tt + 1) * TT],
                                              pm[:, 0:TT], actf,
                                              bias=b1s[:, hidx:hidx + 1]),
                                 reads=[pm, b1s], writes=[hq])
                    if nexp > 1:
                        for tt in range(2):
                            p.op("vector", lambda v, tt=tt: v.tensor_tensor(
                                chunks(hq[:, 0:NCH * HL], NCH, HL)[:, :, tt * TT:(tt + 1) * TT],
                                chunks(hq[:, 0:NCH * HL], NCH, HL)[:, :, tt * TT:(tt + 1) * TT],
                                bcast(pewb[:, tt * TT:(tt + 1) * TT], NCH, TT),
                                op=ALU.mult),
                                reads=[hq, pewb], writes=[hq])
                    w2q = w2[e:e + 1, q * HL:(q + 1) * HL, :].rearrange("o r c -> (o r) c")
                    load_w1024(p, w2q, wst, w2b, ncols=D)
                    for m in range(NCH):
                        for tt in range(2):
                            pm = pm0 if (m * 2 + tt) % 2 == 0 else pm1
                            mm_group(p, pm[:, 0:TT],
                                     [(w2b[:, k * D + m * 128:k * D + (m + 1) * 128],
                                       hq[:, k * HL + tt * TT:k * HL + (tt + 1) * TT])
                                      for k in range(NCH)],
                                     reads=[w2b, hq], writes=[pm])
                            oap = osum[:, m * HL + tt * TT:m * HL + (tt + 1) * TT]
                            if first[0]:
                                p.op("vector", lambda v, oap=oap, pm=pm: v.tensor_copy(
                                    oap, pm[:, 0:TT]), reads=[pm], writes=[osum])
                            else:
                                p.op("vector", lambda v, oap=oap, pm=pm: v.tensor_tensor(
                                    oap, oap, pm[:, 0:TT], op=ALU.add),
                                    reads=[pm, osum], writes=[osum])
                    first[0] = False
            if nexp > 1:
                for m in range(NCH):
                    for tt in range(2):
                        pm = pm0 if (m * 2 + tt) % 2 == 0 else pm1
                        mm_group(p, pm[:, 0:TT],
                                 [(b2b[0:nexp, m * 128:(m + 1) * 128],
                                   ewTb[0:nexp, tt * TT:(tt + 1) * TT])],
                                 reads=[b2b, ewTb], writes=[pm])
                        oap = osum[:, m * HL + tt * TT:m * HL + (tt + 1) * TT]
                        p.op("vector", lambda v, oap=oap, pm=pm: v.tensor_tensor(
                            oap, oap, pm[:, 0:TT], op=ALU.add),
                            reads=[pm, osum], writes=[osum])
                p.op("vector", lambda v: v.tensor_tensor(
                    bf_.xt[:, 0:NCH * HL], bf_.xt[:, 0:NCH * HL],
                    osum[:, 0:NCH * HL], op=ALU.add),
                    reads=[osum, bf_.xt], writes=[bf_.xt])
            else:
                for c in range(NCH):
                    p.op("vector", lambda v, c=c: v.scalar_tensor_tensor(
                        bf_.xt[:, c * HL:(c + 1) * HL], osum[:, c * HL:(c + 1) * HL],
                        b2s[:, c:c + 1], bf_.xt[:, c * HL:(c + 1) * HL],
                        op0=ALU.add, op1=ALU.add),
                        reads=[osum, b2s, bf_.xt], writes=[bf_.xt])
                if extra_residual:
                    p.dma("sync", chunks(osum[:, 0:NCH * HL], NCH, HL),
                          xb2[:, h0:h0 + HL].rearrange("(c p) t -> p c t", p=128),
                          writes=[osum])
                    p.op("vector", lambda v: v.tensor_tensor(
                        bf_.xt[:, 0:NCH * HL], bf_.xt[:, 0:NCH * HL],
                        osum[:, 0:NCH * HL], op=ALU.add),
                        reads=[osum, bf_.xt], writes=[bf_.xt])
            p.dma("sync", xout[:, h0:h0 + HL].rearrange("(c p) t -> p c t", p=128),
                  chunks(bf_.xt[:, 0:NCH * HL], NCH, HL),
                  reads=[bf_.xt], writes=[xout])
        p.finish()
        with nc.Block() as block:
            p.emit(block)
    return nc


def moe_host_consts(nexp):
    ident = np.eye(128, dtype=np.float32)
    esel = np.zeros((nexp, nexp * 128), dtype=BF)
    for e in range(nexp):
        esel[e, e * 128:(e + 1) * 128] = 1
    return ident, esel


def run_moe(x_cur, w, topk, act, names=("gate_w", "gate_b", "w1", "b1", "w2", "b2")):
    gwn, gbn, w1n, b1n, w2n, b2n = names
    nexp = np.asarray(w[w1n]).shape[0]
    consts = host_constants()
    ident, esel = moe_host_consts(nexp)
    b1cols = np.stack([np.asarray(w[b1n][e]).reshape(32, 128).T for e in range(nexp)])
    nc = build_moe(T, nexp, topk, act)
    in_maps = []
    xf = np.asarray(x_cur).reshape(B * S, D)
    for c in range(NCORES):
        m = {
            "xo_fm": _fm(xf[c * T:(c + 1) * T]),
            "w1": np.asarray(w[w1n]), "w2": np.asarray(w[w2n]),
            "b1c": np.ascontiguousarray(b1cols),
            "gw": np.asarray(w[gwn]), "gb": np.asarray(w[gbn]).reshape(nexp, 1),
            "b2r": np.asarray(w[b2n]),
            "ident_in": ident, "esel_in": esel,
        }
        m.update(consts)
        in_maps.append(m)
    res = run_bass_kernel_spmd(nc, in_maps, core_ids=list(range(NCORES)))
    out = np.empty((B * S, D), dtype=np.float32)
    for c in range(NCORES):
        out[c * T:(c + 1) * T] = res.results[c]["xout_fm"].T
    return out.reshape(B, S, D)



def rope_perm():
    perm = np.zeros(D, dtype=np.int64)
    for c in range(NCH):
        base = 128 * c
        cols = ([base + 2 * i for i in range(32)]
                + [base + 64 + 2 * i for i in range(32)]
                + [base + 1 + 2 * i for i in range(32)]
                + [base + 64 + 1 + 2 * i for i in range(32)])
        perm[base:base + 128] = cols
    return perm


def rope_tables():
    inv = 1.0 / (10000.0 ** (np.arange(0, HD, 2, dtype=np.float64) / HD))
    ang = np.arange(S, dtype=np.float64)[:, None] * inv[None, :]
    cos = np.cos(ang).T.astype(np.float32)
    sin = np.sin(ang).T.astype(np.float32)
    return (np.ascontiguousarray(np.concatenate([cos, cos, cos, cos], axis=0)),
            np.ascontiguousarray(np.concatenate([sin, sin, sin, sin], axis=0)))


def hselp_consts():
    sel = np.zeros((128, 8 * 16), dtype=BF)
    for ch in range(8):
        for pp in range(128):
            j = 2 * ch + (pp // 32) % 2
            sel[pp, ch * 16 + j] = 1
    return sel


def emit_rope(p, kfm, robuf, cost, sint, sA, sB, tw):
    for c in range(NCH):
        ev = kfm[0:64, c * tw:(c + 1) * tw]
        od = kfm[64:128, c * tw:(c + 1) * tw]
        a0 = sA[0:64, 0:tw]
        b0 = sB[0:64, 0:tw]
        a1 = sA[64:128, 0:tw]
        b1 = sB[64:128, 0:tw]
        oe = robuf[0:64, c * tw:(c + 1) * tw]
        oo = robuf[64:128, c * tw:(c + 1) * tw]
        p.op("vector", lambda v, ev=ev, a0=a0: v.tensor_tensor(
            a0, ev, cost[0:64, 0:tw], op=ALU.mult), reads=[kfm, cost], writes=[sA])
        p.op("vector", lambda v, od=od, b0=b0: v.tensor_tensor(
            b0, od, sint[64:128, 0:tw], op=ALU.mult), reads=[kfm, sint], writes=[sB])
        p.op("vector", lambda v, a0=a0, b0=b0, oe=oe: v.tensor_tensor(
            oe, a0, b0, op=ALU.subtract), reads=[sA, sB], writes=[robuf])
        p.op("vector", lambda v, ev=ev, a1=a1: v.tensor_tensor(
            a1, ev, sint[0:64, 0:tw], op=ALU.mult), reads=[kfm, sint], writes=[sA])
        p.op("vector", lambda v, od=od, b1=b1: v.tensor_tensor(
            b1, od, cost[64:128, 0:tw], op=ALU.mult), reads=[kfm, cost], writes=[sB])
        p.op("vector", lambda v, a1=a1, b1=b1, oo=oo: v.tensor_tensor(
            oo, a1, b1, op=ALU.add), reads=[sA, sB], writes=[robuf])


def build_l1a_prep(n_own):
    nc = bass.Bass("TRN2", target_bir_lowering=False)
    xo = nc.dram_tensor("xo_fm", [D, n_own], F32, kind="ExternalInput").ap()
    WkP = nc.dram_tensor("WkP", [D, D], F32, kind="ExternalInput").ap()
    Wv = nc.dram_tensor("Wv", [D, D], F32, kind="ExternalInput").ap()
    bkP = nc.dram_tensor("bkP", [128, NCH], F32, kind="ExternalInput").ap()
    bv = nc.dram_tensor("bv", [1, D], F32, kind="ExternalInput").ap()
    g1c = nc.dram_tensor("g1c", [128, NCH], F32, kind="ExternalInput").ap()
    be1c = nc.dram_tensor("be1c", [128, NCH], F32, kind="ExternalInput").ap()
    cosd = nc.dram_tensor("cosd", [128, n_own], F32, kind="ExternalInput").ap()
    sind = nc.dram_tensor("sind", [128, n_own], F32, kind="ExternalInput").ap()
    hselp_in = nc.dram_tensor("hselp_in", [128, 8 * 16], BF16, kind="ExternalInput").ap()
    cap = declare_consts(nc)
    kout = nc.dram_tensor("kout", [D, n_own], BF16, kind="ExternalOutput").ap()
    vout = nc.dram_tensor("vout", [n_own, D], BF16, kind="ExternalOutput").ap()
    mkout = nc.dram_tensor("mkout", [16, 1], F32, kind="ExternalOutput").ap()
    nt = cdiv(n_own, TT)

    with contextlib.ExitStack() as ctx:
        def sb(name, shape, dt=F32):
            return ctx.enter_context(nc.sbuf_tensor(name, shape, dt))

        def psd(name, shape):
            return ctx.enter_context(nc.psum_tensor(name, shape, F32))

        bf_ = NS()
        bf_.xt = sb("xt", [128, NCH * TT])
        bf_.dt_ = sb("dt", [128, NCH * TT])
        bf_.sqbf = sb("sqbf", [128, NCH * TT], BF16)
        bf_.xnb = sb("xnb", [128, NCH * TT], BF16)
        bf_.r1 = sb("r1", [1, TT])
        bf_.r1b = sb("r1b", [1, TT], BF16)
        kfm = sb("kfm", [128, NCH * TT])
        krb = sb("krb", [128, NCH * TT], BF16)
        sA = sb("sA", [128, TT])
        sB = sb("sB", [128, TT])
        vtok = sb("vtok", [128, D], BF16)
        rows = sb("rows", [16, TT])
        mkc = sb("mkc", [16, 1])
        cost = sb("cost", [128, TT])
        sint = sb("sint", [128, TT])
        wst = sb("wst", [128, 4 * D])
        wkb = sb("wkb", [128, NCH * D], BF16)
        wvb = sb("wvb", [128, NCH * D], BF16)
        bks = sb("bks", [128, NCH])
        g1s = sb("g1s", [128, NCH])
        be1s = sb("be1s", [128, NCH])
        bvb = sb("bvb", [1, D], BF16)
        hselp = sb("hselp", [128, 8 * 16], BF16)

        bf_.pr1 = psd("pr1", [1, TT])
        bf_.pbc = psd("pbc", [128, TT])
        pm0 = psd("pm0", [128, TT])
        pm1 = psd("pm1", [128, TT])
        pv = psd("pv", [128, D])
        prow = psd("prow", [16, TT])

        p = Prog(nc, ctx)
        cm = load_common(p, sb, cap)
        load_w1024(p, WkP, wst, wkb)
        load_w1024(p, Wv, wst, wvb)
        p.dma("sync", bks[:, :], bkP, writes=[bks])
        p.dma("sync", g1s[:, :], g1c, writes=[g1s])
        p.dma("sync", be1s[:, :], be1c, writes=[be1s])
        p.dma("sync", wst[0:1, 0:D], bv, writes=[wst])
        p.op("vector", lambda v: v.tensor_copy(bvb[:, :], wst[0:1, 0:D]),
             reads=[wst], writes=[bvb])
        p.dma("sync", hselp[:, :], hselp_in, writes=[hselp])
        p.op("vector", lambda v: v.memset(mkc[:, :], 0.0), writes=[mkc])

        for j in range(nt):
            tw = TT
            t0 = j * TT
            p.dma("sync", chunks(bf_.xt[:, 0:NCH * tw], NCH, tw),
                  xo[:, t0:t0 + tw].rearrange("(c p) t -> p c t", p=128),
                  writes=[bf_.xt])
            p.dma("sync", cost[:, 0:tw], cosd[:, t0:t0 + tw], writes=[cost])
            p.dma("sync", sint[:, 0:tw], sind[:, t0:t0 + tw], writes=[sint])
            emit_ln(p, cm, bf_, tw, gain=g1s, beta=be1s)
            for m in range(NCH):
                pm = pm0 if m % 2 == 0 else pm1
                mm_group(p, pm[:, 0:tw],
                         [(wkb[:, k * D + m * 128:k * D + (m + 1) * 128],
                           bf_.xnb[:, k * tw:(k + 1) * tw]) for k in range(NCH)],
                         reads=[wkb, bf_.xnb], writes=[pm])
                p.op("vector", lambda v, m=m, pm=pm: v.tensor_scalar(
                    kfm[:, m * tw:(m + 1) * tw], pm[:, 0:tw],
                    bks[:, m:m + 1], None, op0=ALU.add),
                    reads=[pm, bks], writes=[kfm])
            emit_rope(p, kfm, krb, cost, sint, sA, sB, tw)
            p.dma("sync", kout[:, t0:t0 + tw].rearrange("(c p) t -> p c t", p=128),
                  chunks(krb[:, 0:NCH * tw], NCH, tw), reads=[krb], writes=[kout])
            p.op("scalar", lambda s: s.activation(bf_.sqbf[:, 0:NCH * tw],
                                                  krb[:, 0:NCH * tw], AF.Square),
                 reads=[krb], writes=[bf_.sqbf])
            mm_group(p, prow[0:16, 0:tw],
                     [(hselp[:, ch * 16:(ch + 1) * 16], bf_.sqbf[:, ch * tw:(ch + 1) * tw])
                      for ch in range(NCH)],
                     reads=[bf_.sqbf, hselp], writes=[prow])
            p.op("scalar", lambda s: s.activation(rows[0:16, 0:tw], prow[0:16, 0:tw], AF.Ln),
                 reads=[prow], writes=[rows])
            p.op("scalar", lambda s: s.activation(rows[0:16, 0:tw], rows[0:16, 0:tw],
                                                  AF.Exp, scale=0.5),
                 reads=[rows], writes=[rows])
            p.op("vector", lambda v: v.reduce_max(rows[0:16, 0:1], rows[0:16, 0:tw],
                                                  axis=AX.X),
                 reads=[rows], writes=[rows])
            p.op("vector", lambda v: v.tensor_tensor(mkc[0:16, 0:1], mkc[0:16, 0:1],
                                                     rows[0:16, 0:1], op=ALU.max),
                 reads=[rows, mkc], writes=[mkc])
            for tc in range(tw // 128):
                for half in range(2):
                    mm_group(p, pv[:, half * 512:(half + 1) * 512],
                             [(bf_.xnb[:, k * tw + tc * 128:k * tw + tc * 128 + 128],
                               wvb[:, k * D + half * 512:k * D + (half + 1) * 512])
                              for k in range(NCH)]
                             + [(cm.onesb[0:1, 0:128], bvb[0:1, half * 512:(half + 1) * 512])],
                             reads=[bf_.xnb, wvb, cm.onesb, bvb], writes=[pv])
                p.op("vector", lambda v: v.tensor_copy(vtok[:, :], pv[:, 0:D]),
                     reads=[pv], writes=[vtok])
                p.dma("sync", vout[t0 + tc * 128:t0 + (tc + 1) * 128, :], vtok[:, :],
                      reads=[vtok], writes=[vout])
        p.dma("sync", mkout, mkc[0:16, 0:1], reads=[mkc], writes=[mkout])
        p.finish()
        with nc.Block() as block:
            p.emit(block)
    return nc


def build_l1a_attn(n_own, n_seq):
    nc = bass.Bass("TRN2", target_bir_lowering=False)
    xo = nc.dram_tensor("xo_fm", [D, n_own], F32, kind="ExternalInput").ap()
    kfull = nc.dram_tensor("kfull", [D, n_seq], BF16, kind="ExternalInput").ap()
    vgrp = nc.dram_tensor("vgrp", [NH, n_seq, HD], BF16, kind="ExternalInput").ap()
    mkin = nc.dram_tensor("mkin", [16, 4], F32, kind="ExternalInput").ap()
    WqP = nc.dram_tensor("WqP", [D, D], F32, kind="ExternalInput").ap()
    Wo = nc.dram_tensor("Wo", [D, D], F32, kind="ExternalInput").ap()
    bqP = nc.dram_tensor("bqP", [128, NCH], F32, kind="ExternalInput").ap()
    bo = nc.dram_tensor("bo", [128, NCH], F32, kind="ExternalInput").ap()
    g1c = nc.dram_tensor("g1c", [128, NCH], F32, kind="ExternalInput").ap()
    be1c = nc.dram_tensor("be1c", [128, NCH], F32, kind="ExternalInput").ap()
    cosd = nc.dram_tensor("cosd", [128, n_own], F32, kind="ExternalInput").ap()
    sind = nc.dram_tensor("sind", [128, n_own], F32, kind="ExternalInput").ap()
    sel65_in = nc.dram_tensor("sel65_in", [128, NH * 65], BF16, kind="ExternalInput").ap()
    ident_in = nc.dram_tensor("ident_in", [128, 128], F32, kind="ExternalInput").ap()
    cap = declare_consts(nc)
    xout = nc.dram_tensor("xout_fm", [D, n_own], F32, kind="ExternalOutput").ap()
    nt = cdiv(n_own, TT)
    nkc = n_seq // 128

    with contextlib.ExitStack() as ctx:
        def sb(name, shape, dt=F32):
            return ctx.enter_context(nc.sbuf_tensor(name, shape, dt))

        def psd(name, shape, dt=F32):
            return ctx.enter_context(nc.psum_tensor(name, shape, dt))

        bf_ = NS()
        bf_.xt = sb("xt", [128, NCH * TT])
        bf_.dt_ = sb("dt", [128, NCH * TT])
        bf_.sqbf = sb("sqbf", [128, NCH * TT], BF16)
        bf_.xnb = sb("xnb", [128, NCH * TT], BF16)
        bf_.r1 = sb("r1", [1, TT])
        bf_.r1b = sb("r1b", [1, TT], BF16)
        zt = sb("zt", [128, NCH * TT])
        kfm = bf_.dt_
        krb = sb("krb", [128, NCH * TT])
        sA = sb("sA", [128, TT])
        sB = sb("sB", [128, TT])
        qodd = sb("qodd", [64, TT])
        qaug = sb("qaug", [65, NH * TT], BF16)
        kaug = sb("kaug", [65, 0 + n_seq], BF16)
        vh = sb("vh", [128, (n_seq // 128) * 64], BF16)
        ptb = sb("ptb", [128, 1024], BF16)
        att = sb("att", [128, NCH * TT], BF16)
        rbs = sb("rbs", [64, TT])
        cost = sb("cost", [128, TT])
        sint = sb("sint", [128, TT])
        wst = sb("wst", [128, 4 * D])
        wqb = sb("wqb", [128, NCH * D], BF16)
        wob = sb("wob", [128, NCH * D], BF16)
        bqs = sb("bqs", [128, NCH])
        bos = sb("bos", [128, NCH])
        g1s = sb("g1s", [128, NCH])
        be1s = sb("be1s", [128, NCH])
        sel65 = sb("sel65", [128, NH * 65], BF16)
        mkrow = sb("mkrow", [1, 16])
        mks = sb("mks", [16, 4])
        idf = sb("idf", [128, 128])

        bf_.pr1 = psd("pr1", [1, TT])
        bf_.pbc = psd("pbc", [128, TT])
        pm0 = psd("pm0", [128, TT])
        pm1 = psd("pm1", [128, TT])
        psc = psd("psc", [128, 1024])
        pv65 = psd("pv65", [65, TT])

        p = Prog(nc, ctx)
        cm = load_common(p, sb, cap)
        load_w1024(p, WqP, wst, wqb)
        load_w1024(p, Wo, wst, wob)
        p.dma("sync", bqs[:, :], bqP, writes=[bqs])
        p.dma("sync", bos[:, :], bo, writes=[bos])
        p.dma("sync", g1s[:, :], g1c, writes=[g1s])
        p.dma("sync", be1s[:, :], be1c, writes=[be1s])
        p.dma("sync", sel65[:, :], sel65_in, writes=[sel65])
        p.dma("sync", idf[:, :], ident_in, writes=[idf])
        p.dma("sync", mks[:, :], mkin, writes=[mks])
        p.op("vector", lambda v: v.tensor_reduce(mks[0:16, 0:1], mks[0:16, 0:4],
                                                 axis=AX.X, op=ALU.max),
             reads=[mks], writes=[mks])
        p.op("tensor", lambda t: t.transpose(bf_.pbc[0:1, 0:16], mks[0:16, 0:1],
                                             idf[0:16, 0:16]),
             reads=[mks, idf], writes=[bf_.pbc])
        p.op("vector", lambda v: v.tensor_copy(mkrow[0:1, 0:16], bf_.pbc[0:1, 0:16]),
             reads=[bf_.pbc], writes=[mkrow])
        p.op("vector", lambda v: v.memset(kaug[64:65, :], 1.0), writes=[kaug])


        for j in range(nt):
            tw = TT
            t0 = j * TT
            p.dma("sync", chunks(bf_.xt[:, 0:NCH * tw], NCH, tw),
                  xo[:, t0:t0 + tw].rearrange("(c p) t -> p c t", p=128),
                  writes=[bf_.xt])
            p.dma("sync", cost[:, 0:tw], cosd[:, t0:t0 + tw], writes=[cost])
            p.dma("sync", sint[:, 0:tw], sind[:, t0:t0 + tw], writes=[sint])
            emit_ln(p, cm, bf_, tw, dst=zt)
            emit_ln(p, cm, bf_, tw, src=zt, gain=g1s, beta=be1s)
            for m in range(NCH):
                pm = pm0 if m % 2 == 0 else pm1
                mm_group(p, pm[:, 0:tw],
                         [(wqb[:, k * D + m * 128:k * D + (m + 1) * 128],
                           bf_.xnb[:, k * tw:(k + 1) * tw]) for k in range(NCH)],
                         reads=[wqb, bf_.xnb], writes=[pm])
                p.op("vector", lambda v, m=m, pm=pm: v.tensor_scalar(
                    kfm[:, m * tw:(m + 1) * tw], pm[:, 0:tw],
                    bqs[:, m:m + 1], None, op0=ALU.add),
                    reads=[pm, bqs], writes=[kfm])
            emit_rope(p, kfm, krb, cost, sint, sA, sB, tw)
            p.op("scalar", lambda s: s.activation(bf_.sqbf[:, 0:NCH * tw],
                                                  krb[:, 0:NCH * tw], AF.Square),
                 reads=[krb], writes=[bf_.sqbf])
            for h in range(NH):
                ch, par = h // 2, h % 2
                if par == 0:
                    p.op("vector", lambda v, ch=ch: v.tensor_copy(
                        qodd[0:64, 0:tw], krb[64:128, ch * tw:(ch + 1) * tw]),
                        reads=[krb], writes=[qodd])
                p.op("vector", lambda v, h=h, ch=ch, par=par: v.tensor_scalar(
                    qaug[0:32, h * tw:(h + 1) * tw],
                    krb[32 * par:32 * par + 32, ch * tw:(ch + 1) * tw],
                    0.125, None, op0=ALU.mult),
                    reads=[krb], writes=[qaug])
                p.op("vector", lambda v, h=h, par=par: v.tensor_scalar(
                    qaug[32:64, h * tw:(h + 1) * tw],
                    qodd[32 * par:32 * par + 32, 0:tw],
                    0.125, None, op0=ALU.mult),
                    reads=[qodd], writes=[qaug])
                mm_group(p, pv65[0:65, 0:tw],
                         [(sel65[:, h * 65:(h + 1) * 65],
                           bf_.sqbf[:, ch * tw:(ch + 1) * tw])],
                         reads=[sel65, bf_.sqbf], writes=[pv65])
                p.op("scalar", lambda s: s.activation(rbs[0:1, 0:tw],
                                                      pv65[64:65, 0:tw], AF.Ln),
                     reads=[pv65], writes=[rbs])
                p.op("scalar", lambda s: s.activation(rbs[0:1, 0:tw], rbs[0:1, 0:tw],
                                                      AF.Exp, scale=0.5),
                     reads=[rbs], writes=[rbs])
                p.op("vector", lambda v, h=h: v.tensor_scalar(
                    qaug[64:65, h * tw:(h + 1) * tw], rbs[0:1, 0:tw],
                    mkrow[0:1, h:h + 1], -0.125, op0=ALU.mult, op1=ALU.mult),
                    reads=[rbs, mkrow], writes=[qaug])
            for h in range(NH):
                ch, par = h // 2, h % 2
                p.dma("sync", kaug[0:32, :],
                      kfull[ch * 128 + 32 * par:ch * 128 + 32 * par + 32, :],
                      writes=[kaug], slot="kaug")
                p.dma("sync", kaug[32:64, :],
                      kfull[ch * 128 + 64 + 32 * par:ch * 128 + 96 + 32 * par, :],
                      writes=[kaug], slot="kaug")
                for qsp in range(4):
                    nq = nkc // 4
                    p.dma("sync",
                          vh[:, :].rearrange("p (kc g) -> p kc g", g=64)[
                              :, qsp * nq:(qsp + 1) * nq, :],
                          vgrp[h, :, :].rearrange("(kc p) g -> p kc g", p=128)[
                              :, qsp * nq:(qsp + 1) * nq, :],
                          writes=[vh], slot="vh")
                for kcp in range(nkc // 2):
                    for half in range(2):
                        kc = kcp * 2 + half
                        def mmsc(t, kc=kc, half=half):
                            return t.matmul(psc[:, half * TT:(half + 1) * TT],
                                            kaug[0:65, kc * 128:(kc + 1) * 128],
                                            qaug[0:65, h * tw:(h + 1) * tw],
                                            start=True, stop=True)
                        p.op("tensor", mmsc, reads=[kaug, qaug], writes=[psc])
                    p.op("scalar", lambda s: s.activation(ptb[:, 0:1024],
                                                          psc[:, 0:1024], AF.Exp),
                         reads=[psc], writes=[ptb])
                    for half in range(2):
                        kc = kcp * 2 + half
                        first = (kcp == 0 and half == 0)
                        last = (kcp == nkc // 2 - 1 and half == 1)
                        def mmpv(t, kc=kc, half=half, first=first, last=last):
                            t.matmul(pv65[0:64, 0:tw],
                                     vh[:, kc * 64:(kc + 1) * 64],
                                     ptb[:, half * TT:(half + 1) * TT],
                                     start=first, stop=False,
                                     skip_group_check=True)
                            return t.matmul(pv65[64:65, 0:tw],
                                            cm.onescb[:, 0:1],
                                            ptb[:, half * TT:(half + 1) * TT],
                                            start=first, stop=last,
                                            skip_group_check=True)
                        p.op("tensor", mmpv, reads=[vh, ptb, cm.onescb], writes=[pv65])
                p.op("vector", lambda v: v.reciprocal(bf_.r1[0:1, 0:tw],
                                                      pv65[64:65, 0:tw]),
                     reads=[pv65], writes=[bf_.r1])
                p.op("vector", lambda v: v.tensor_copy(bf_.r1b[0:1, 0:tw],
                                                       bf_.r1[0:1, 0:tw]),
                     reads=[bf_.r1], writes=[bf_.r1b])
                mm_group(p, bf_.pbc[0:64, 0:tw], [(cm.onesb[0:1, 0:64], bf_.r1b[0:1, 0:tw])],
                         reads=[bf_.r1b, cm.onesb], writes=[bf_.pbc])
                p.op("scalar", lambda s: s.copy(rbs[0:64, 0:tw], bf_.pbc[0:64, 0:tw]),
                     reads=[bf_.pbc], writes=[rbs])
                p.op("vector", lambda v, h=h, ch=ch, par=par: v.tensor_tensor(
                    att[64 * par:64 * par + 64, ch * tw:(ch + 1) * tw],
                    pv65[0:64, 0:tw], rbs[0:64, 0:tw], op=ALU.mult),
                    reads=[pv65, rbs], writes=[att])
            for m in range(NCH):
                pm = pm0 if m % 2 == 0 else pm1
                mm_group(p, pm[:, 0:tw],
                         [(wob[:, k * D + m * 128:k * D + (m + 1) * 128],
                           att[:, k * tw:(k + 1) * tw]) for k in range(NCH)],
                         reads=[wob, att], writes=[pm])
                p.op("vector", lambda v, m=m, pm=pm: v.scalar_tensor_tensor(
                    zt[:, m * tw:(m + 1) * tw], pm[:, 0:tw],
                    bos[:, m:m + 1], zt[:, m * tw:(m + 1) * tw],
                    op0=ALU.add, op1=ALU.add),
                    reads=[pm, bos, zt], writes=[zt])
            p.dma("sync", xout[:, t0:t0 + tw].rearrange("(c p) t -> p c t", p=128),
                  chunks(zt[:, 0:NCH * tw], NCH, tw), reads=[zt], writes=[xout])
        p.finish()
        with nc.Block() as block:
            p.emit(block)
    return nc


def sel65_consts():
    sel = np.zeros((128, NH * 65), dtype=BF)
    for h in range(NH):
        par = h % 2
        for blk in (32 * par, 64 + 32 * par):
            sel[blk:blk + 32, h * 65 + 64] = 1
    return sel


def run_l1a(x_cur, w):
    consts = host_constants()
    perm = rope_perm()
    cos64, sin64 = rope_tables()
    WqP = np.ascontiguousarray(np.asarray(w["Wq"])[:, perm])
    WkP = np.ascontiguousarray(np.asarray(w["Wk"])[:, perm])
    bqP = _cols(np.asarray(w["bq"])[perm])
    bkP = _cols(np.asarray(w["bk"])[perm])
    xf = np.asarray(x_cur).reshape(B * S, D)

    nc1 = build_l1a_prep(T)
    in_maps = []
    for c in range(NCORES):
        t0g = c * T
        pos0 = (c % 4) * T
        m = {
            "xo_fm": _fm(xf[t0g:t0g + T]),
            "WkP": WkP, "Wv": np.asarray(w["Wv"]), "bkP": bkP,
            "bv": np.asarray(w["bv"]).reshape(1, D),
            "g1c": _cols(w["g1"]), "be1c": _cols(w["beta1"]),
            "cosd": np.ascontiguousarray(cos64[:, pos0:pos0 + T]),
            "sind": np.ascontiguousarray(sin64[:, pos0:pos0 + T]),
            "hselp_in": hselp_consts(),
        }
        m.update(consts)
        in_maps.append(m)
    res1 = run_bass_kernel_spmd(nc1, in_maps, core_ids=list(range(NCORES)))
    kparts = [res1.results[c]["kout"] for c in range(NCORES)]
    vparts = [res1.results[c]["vout"] for c in range(NCORES)]
    mkparts = [res1.results[c]["mkout"] for c in range(NCORES)]
    kfulls = [np.concatenate(kparts[b * 4:(b + 1) * 4], axis=1) for b in range(B)]
    vfulls = [np.ascontiguousarray(
        np.concatenate(vparts[b * 4:(b + 1) * 4], axis=0)
        .reshape(S, NH, HD).transpose(1, 0, 2)) for b in range(B)]
    mkfulls = [np.concatenate(mkparts[b * 4:(b + 1) * 4], axis=1) for b in range(B)]

    nc2 = build_l1a_attn(T, S)
    in_maps = []
    for c in range(NCORES):
        b = c // 4
        t0g = c * T
        pos0 = (c % 4) * T
        m = {
            "xo_fm": _fm(xf[t0g:t0g + T]),
            "kfull": kfulls[b], "vgrp": vfulls[b], "mkin": mkfulls[b],
            "WqP": WqP, "Wo": np.asarray(w["Wo"]),
            "bqP": bqP, "bo": _cols(w["bo"]),
            "g1c": _cols(w["g1"]), "be1c": _cols(w["beta1"]),
            "cosd": np.ascontiguousarray(cos64[:, pos0:pos0 + T]),
            "sind": np.ascontiguousarray(sin64[:, pos0:pos0 + T]),
            "sel65_in": sel65_consts(),
            "ident_in": np.eye(128, dtype=np.float32),
        }
        m.update(consts)
        in_maps.append(m)
    res2 = run_bass_kernel_spmd(nc2, in_maps, core_ids=list(range(NCORES)))
    z2 = np.empty((B * S, D), dtype=np.float32)
    for c in range(NCORES):
        z2[c * T:(c + 1) * T] = res2.results[c]["xout_fm"].T

    consts2 = host_constants()
    nc3 = build_moe(T, 1, 1, "relu", extra_residual=True, ln_gains=True)
    b1cols = np.asarray(w["b1"]).reshape(32, 128).T[None]
    in_maps = []
    for c in range(NCORES):
        t0g = c * T
        m = {
            "xo_fm": _fm(z2[t0g:t0g + T]),
            "xb2_fm": _fm(xf[t0g:t0g + T]),
            "w1": np.asarray(w["W1"])[None], "w2": np.asarray(w["W2"])[None],
            "b1c": np.ascontiguousarray(b1cols),
            "b2cc": _cols(w["b2"]),
            "g2c": _cols(w["g2"]), "be2c": _cols(w["beta2"]),
        }
        m.update(consts2)
        in_maps.append(m)
    res3 = run_bass_kernel_spmd(nc3, in_maps, core_ids=list(range(NCORES)))
    out = np.empty((B * S, D), dtype=np.float32)
    for c in range(NCORES):
        out[c * T:(c + 1) * T] = res3.results[c]["xout_fm"].T
    return out.reshape(B, S, D)



def scan_consts():
    c = {}
    c["sc_iota_j"] = np.arange(128, dtype=np.float32).reshape(1, 128)
    c["sc_iota127"] = (127.0 - np.arange(128, dtype=np.float32)).reshape(128, 1)
    deltaT = np.zeros((128, 128), dtype=np.float32)
    maskT = np.zeros((128, 128), dtype=np.float32)
    for i in range(128):
        for j_ in range(128):
            if i < j_:
                deltaT[i, j_] = j_ - 1 - i
                maskT[i, j_] = 1.0
    c["sc_deltaT"] = deltaT
    c["sc_maskT"] = maskT
    c["sc_ones_f32"] = np.ones((1, 128), dtype=np.float32)
    return c


def emit_scan_wprep(p, nc, sb, psd, bf_, cm, basd, coefd, wst, pm0, pm1, pbt,
                    names=("q", "k", "v", "o")):
    bb = sb("bb", [128, NCH * 64], BF16)
    ct = sb("ct", [128, 64], BF16)
    basT = sb("basT", [64, D], BF16)
    cqT = sb("cqT", [64, 128], BF16)
    lgs = sb("lgs", [128, D])
    colm = sb("colm", [128, 2])
    out = {}
    p.dma("sync", chunks(wst[:, 0:NCH * 64], NCH, 64),
          basd.rearrange("(c p) n -> p c n", p=128), writes=[wst])
    p.op("vector", lambda v: v.tensor_copy(bb[:, :], wst[:, 0:NCH * 64]),
         reads=[wst], writes=[bb])
    for c in range(NCH):
        p.op("tensor", lambda t, c=c: t.transpose(
            pbt[0:64, 0:128], bb[:, c * 64:(c + 1) * 64], cm.identb[:, 0:128]),
            reads=[bb, cm.identb], writes=[pbt])
        p.op("vector", lambda v, c=c: v.tensor_copy(
            basT[0:64, c * 128:(c + 1) * 128], pbt[0:64, 0:128]),
            reads=[pbt], writes=[basT])
    for nm in names:
        cd = coefd[nm]
        p.dma("sync", wst[:, 0:64], cd, writes=[wst])
        p.op("vector", lambda v: v.tensor_copy(ct[:, :], wst[:, 0:64]),
             reads=[wst], writes=[ct])
        p.op("tensor", lambda t: t.transpose(pbt[0:64, 0:128], ct[:, :],
                                             cm.identb[:, 0:128]),
             reads=[ct, cm.identb], writes=[pbt])
        p.op("vector", lambda v: v.tensor_copy(cqT[0:64, 0:128], pbt[0:64, 0:128]),
             reads=[pbt], writes=[cqT])
        for half in range(2):
            pm = pm0 if half == 0 else pm1
            mm_group(p, pm[:, 0:TT],
                     [(cqT[0:64, 0:128], basT[0:64, half * TT:(half + 1) * TT])],
                     reads=[cqT, basT], writes=[pm])
        if nm == "o":
            owT = sb("owT", [128, D], BF16)
            for half in range(2):
                pm = pm0 if half == 0 else pm1
                p.op("vector", lambda v, half=half, pm=pm, owT=owT: v.tensor_copy(
                    owT[:, half * TT:(half + 1) * TT], pm[:, 0:TT]),
                    reads=[pm], writes=[owT])
            out["o"] = owT
            continue
        p.op("vector", lambda v: v.reduce_max(colm[:, 0:1], pm0[:, 0:TT], axis=AX.X),
             reads=[pm0], writes=[colm])
        p.op("vector", lambda v: v.reduce_max(colm[:, 1:2], pm1[:, 0:TT], axis=AX.X),
             reads=[pm1], writes=[colm])
        p.op("vector", lambda v: v.tensor_tensor(colm[:, 0:1], colm[:, 0:1],
                                                 colm[:, 1:2], op=ALU.max),
             reads=[colm], writes=[colm])
        p.op("vector", lambda v: v.tensor_scalar(colm[:, 0:1], colm[:, 0:1],
                                                 -1.0, None, op0=ALU.mult),
             reads=[colm], writes=[colm])
        for half in range(2):
            pm = pm0 if half == 0 else pm1
            p.op("scalar", lambda s, half=half, pm=pm: s.activation(
                lgs[:, half * TT:(half + 1) * TT], pm[:, 0:TT], AF.Exp,
                bias=colm[:, 0:1]),
                reads=[pm, colm], writes=[lgs])
        p.op("vector", lambda v: v.reduce_sum(colm[:, 1:2], lgs[:, 0:D], axis=AX.X),
             reads=[lgs], writes=[colm])
        p.op("vector", lambda v: v.reciprocal(colm[:, 1:2], colm[:, 1:2]),
             reads=[colm], writes=[colm])
        wsm = sb(f"wsm_{nm}", [128, D], BF16)
        p.op("vector", lambda v, wsm=wsm: v.tensor_scalar(wsm[:, 0:D], lgs[:, 0:D],
                                                 colm[:, 1:2], None, op0=ALU.mult),
             reads=[lgs, colm], writes=[wsm])
        wnb = sb(f"w_{nm}", [128, NCH * 128], BF16)
        p.op("vector", lambda v, wnb=wnb: v.memset(wnb[:, :], 0.0), writes=[wnb])
        for c in range(NCH):
            p.op("tensor", lambda t, c=c, wsm=wsm: t.transpose(
                pbt[:, 0:128], wsm[:, c * 128:(c + 1) * 128], cm.identb[:, 0:128]),
                reads=[wsm, cm.identb], writes=[pbt])
            p.op("vector", lambda v, c=c, wnb=wnb: v.tensor_copy(
                wnb[:, c * 128:(c + 1) * 128], pbt[:, 0:128]),
                reads=[pbt], writes=[wnb])
        out[nm] = wnb
    return out


def emit_decay(p, nc, sb, bf_, cm, decd, iota_j, iota127, deltaT_t, maskT_t):
    dd = NS()
    dd.sc = sb("dsc", [1, 2])
    dd.ldcol = sb("ldcol", [128, 1])
    dd.dprow = sb("dprow", [1, 128], BF16)
    dd.d127col = sb("d127col", [128, 1])
    dd.dp128col = sb("dp128col", [128, 1])
    dd.dmaskT = sb("dmaskT", [128, 128], BF16)
    dd.dmf = sb("dmf", [128, 128])
    p.dma("sync", dd.sc[0:1, 0:1], decd, writes=[dd.sc])
    p.op("scalar", lambda s: s.activation(dd.sc[0:1, 0:1], dd.sc[0:1, 0:1],
                                          AF.Sigmoid),
         reads=[dd.sc], writes=[dd.sc])
    p.op("scalar", lambda s: s.activation(dd.sc[0:1, 1:2], dd.sc[0:1, 0:1], AF.Ln),
         reads=[dd.sc], writes=[dd.sc])
    ld = dd.sc[0:1, 1:2]
    def bcast_ld(t):
        return t.matmul(bf_.pbc[:, 0:1], cm.onesf[0:1, 0:128], ld, start=True,
                        stop=True)
    p.op("tensor", bcast_ld, reads=[dd.sc, cm.onesf], writes=[bf_.pbc])
    p.op("vector", lambda v: v.tensor_copy(dd.ldcol[:, 0:1], bf_.pbc[:, 0:1]),
         reads=[bf_.pbc], writes=[dd.ldcol])
    p.op("scalar", lambda s: s.activation(dd.dprow[0:1, 0:128], iota_j[0:1, 0:128],
                                          AF.Exp, scale=ld),
         reads=["iota_j", dd.sc], writes=[dd.dprow])
    p.op("scalar", lambda s: s.activation(dd.d127col[:, 0:1], iota127[:, 0:1],
                                          AF.Exp, scale=dd.ldcol[:, 0:1]),
         reads=["iota127", dd.ldcol], writes=[dd.d127col])
    p.op("scalar", lambda s: s.activation(dd.dp128col[:, 0:1], dd.ldcol[:, 0:1],
                                          AF.Exp, scale=128.0),
         reads=[dd.ldcol], writes=[dd.dp128col])
    p.op("scalar", lambda s: s.activation(dd.dmf[:, :], deltaT_t[:, :], AF.Exp,
                                          scale=dd.ldcol[:, 0:1]),
         reads=["deltaT", dd.ldcol], writes=[dd.dmf])
    p.op("vector", lambda v: v.tensor_tensor(dd.dmaskT[:, :], dd.dmf[:, :],
                                             maskT_t[:, :], op=ALU.mult),
         reads=[dd.dmf, "maskT"], writes=[dd.dmaskT])
    return dd


def build_scan_a(n_own):
    nc = bass.Bass("TRN2", target_bir_lowering=False)
    xo = nc.dram_tensor("xo_fm", [D, n_own], F32, kind="ExternalInput").ap()
    basd = nc.dram_tensor("basis", [D, 64], F32, kind="ExternalInput").ap()
    qcd = nc.dram_tensor("qcoef", [128, 64], F32, kind="ExternalInput").ap()
    kcd = nc.dram_tensor("kcoef", [128, 64], F32, kind="ExternalInput").ap()
    vcd = nc.dram_tensor("vcoef", [128, 64], F32, kind="ExternalInput").ap()
    ocd = nc.dram_tensor("ocoef", [128, 64], F32, kind="ExternalInput").ap()
    decd = nc.dram_tensor("decay", [1, 1], F32, kind="ExternalInput").ap()
    oscd = nc.dram_tensor("oscol", [128, 1], F32, kind="ExternalInput").ap()
    cap = declare_consts(nc)
    scap = {}
    for name, arr in scan_consts().items():
        scap[name] = nc.dram_tensor(name, list(arr.shape), F32, kind="ExternalInput").ap()
    identb_in = nc.dram_tensor("identb_in", [128, 128], BF16, kind="ExternalInput").ap()
    xout = nc.dram_tensor("xout_fm", [D, n_own], F32, kind="ExternalOutput").ap()
    sout = nc.dram_tensor("sout", [128, 128], F32, kind="ExternalOutput").ap()
    qout = nc.dram_tensor("qout", [128, n_own], BF16, kind="ExternalOutput").ap()
    nt = cdiv(n_own, TT)
    ncc = n_own // 128

    with contextlib.ExitStack() as ctx:
        def sb(name, shape, dt=F32):
            return ctx.enter_context(nc.sbuf_tensor(name, shape, dt))

        def psd(name, shape, dt=F32):
            return ctx.enter_context(nc.psum_tensor(name, shape, dt))

        bf_ = NS()
        bf_.xt = sb("xt", [128, NCH * TT])
        bf_.dt_ = sb("dt", [128, NCH * TT])
        bf_.sqbf = sb("sqbf", [128, NCH * TT], BF16)
        bf_.xnb = sb("xnb", [128, NCH * TT], BF16)
        bf_.r1 = sb("r1", [1, TT])
        bf_.r1b = sb("r1b", [1, TT], BF16)
        qT = sb("qT", [128, n_own], BF16)
        kT = sb("kT", [128, n_own], BF16)
        vT = sb("vT", [128, n_own], BF16)
        rt = sb("rt", [128, n_own], BF16)
        vtk = sb("vtk", [128, 128], BF16)
        ktk = sb("ktk", [128, 128], BF16)
        kdec = sb("kdec", [128, 128], BF16)
        qd = sb("qd", [128, 128], BF16)
        amb = sb("amb", [128, 128], BF16)
        Mst = sb("Mst", [128, 128])
        Mstb = sb("Mstb", [128, 128], BF16)
        wst = sb("wst", [128, 4 * D])
        iotaj = sb("iotaj", [1, 128])
        iota127 = sb("iota127", [128, 1])
        deltaT_t = sb("deltaT_t", [128, 128])
        maskT_t = sb("maskT_t", [128, 128])
        oscol = sb("oscol_sb", [128, 1])

        bf_.pr1 = psd("pr1", [1, TT])
        bf_.pbc = psd("pbc", [128, TT])
        pm0 = psd("pm0", [128, TT])
        pm1 = psd("pm1", [128, TT])
        pbt2 = psd("pbt2", [128, 128], BF16)
        pA = psd("pA", [128, 128])
        pS = psd("pS", [128, 128])
        prt = psd("prt", [128, 128])

        p = Prog(nc, ctx)
        cm = load_common(p, sb, cap)
        cm.identb = sb("identb", [128, 128], BF16)
        cm.onesf = sb("onesf", [1, 128])
        p.dma("sync", cm.identb[:, :], identb_in, writes=[cm.identb])
        p.dma("sync", cm.onesf[:, :], scap["sc_ones_f32"], writes=[cm.onesf])
        p.dma("sync", iotaj[:, :], scap["sc_iota_j"], writes=["iota_j"])
        p.dma("sync", iota127[:, :], scap["sc_iota127"], writes=["iota127"])
        p.dma("sync", deltaT_t[:, :], scap["sc_deltaT"], writes=["deltaT"])
        p.dma("sync", maskT_t[:, :], scap["sc_maskT"], writes=["maskT"])
        p.dma("sync", oscol[:, :], oscd, writes=[oscol])

        wd = emit_scan_wprep(p, nc, sb, psd, bf_, cm, basd,
                             {"q": qcd, "k": kcd, "v": vcd, "o": ocd}, wst,
                             pm0, pm1, pbt2)
        dd = emit_decay(p, nc, sb, bf_, cm, decd, iotaj, iota127,
                        deltaT_t, maskT_t)

        for j in range(nt):
            tw = TT
            t0 = j * TT
            p.dma("sync", chunks(bf_.xt[:, 0:NCH * tw], NCH, tw),
                  xo[:, t0:t0 + tw].rearrange("(c p) t -> p c t", p=128),
                  writes=[bf_.xt])
            emit_ln(p, cm, bf_, tw)
            for nm, dst in [("q", qT), ("k", kT), ("v", vT)]:
                wnb = wd[nm]
                pm = pm0 if nm != "k" else pm1
                mm_group(p, pm[:, 0:tw],
                         [(wnb[:, k * 128:(k + 1) * 128],
                           bf_.xnb[:, k * tw:(k + 1) * tw]) for k in range(NCH)],
                         reads=[wnb, bf_.xnb], writes=[pm])
                p.op("vector", lambda v, dst=dst, t0=t0, tw=tw, pm=pm: v.tensor_copy(
                    dst[:, t0:t0 + tw], pm[:, 0:tw]),
                    reads=[pm], writes=[dst])
        p.op("vector", lambda v: v.memset(Mst[:, :], 0.0), writes=[Mst])
        p.op("vector", lambda v: v.memset(Mstb[:, :], 0.0), writes=[Mstb])
        for cc in range(ncc):
            c0 = cc * 128
            p.op("tensor", lambda t, c0=c0: t.transpose(
                pbt2[:, 0:128], vT[:, c0:c0 + 128], cm.identb[:, 0:128]),
                reads=[vT, cm.identb], writes=[pbt2])
            p.op("vector", lambda v: v.tensor_copy(vtk[:, :], pbt2[:, 0:128]),
                 reads=[pbt2], writes=[vtk])
            p.op("tensor", lambda t, c0=c0: t.transpose(
                pbt2[:, 0:128], kT[:, c0:c0 + 128], cm.identb[:, 0:128]),
                reads=[kT, cm.identb], writes=[pbt2])
            p.op("vector", lambda v: v.tensor_copy(ktk[:, :], pbt2[:, 0:128]),
                 reads=[pbt2], writes=[ktk])
            p.op("vector", lambda v: v.tensor_scalar(
                kdec[:, :], ktk[:, :], dd.d127col[:, 0:1], None, op0=ALU.mult),
                reads=[ktk, dd.d127col], writes=[kdec])
            mm_group(p, pA[:, 0:128],
                     [(kT[:, c0:c0 + 128], qT[:, c0:c0 + 128])],
                     reads=[kT, qT], writes=[pA])
            p.op("vector", lambda v: v.tensor_tensor(
                amb[:, :], pA[:, 0:128], dd.dmaskT[:, :], op=ALU.mult),
                reads=[pA, dd.dmaskT], writes=[amb])
            mm_group(p, bf_.pbc[:, 0:128],
                     [(cm.onesb[0:1, 0:128], dd.dprow[0:1, 0:128])],
                     reads=[cm.onesb, dd.dprow], writes=[bf_.pbc])
            p.op("vector", lambda v, c0=c0: v.tensor_tensor(
                qd[:, :], qT[:, c0:c0 + 128], bf_.pbc[:, 0:128], op=ALU.mult),
                reads=[qT, bf_.pbc], writes=[qd])
            mm_group(p, prt[:, 0:128],
                     [(Mstb[:, :], qd[:, :]), (vtk[:, :], amb[:, :])],
                     reads=[Mstb, qd, vtk, amb], writes=[prt])
            p.op("vector", lambda v, c0=c0: v.tensor_copy(
                rt[:, c0:c0 + 128], prt[:, 0:128]),
                reads=[prt], writes=[rt])
            mm_group(p, pS[:, 0:128], [(kdec[:, :], vtk[:, :])],
                     reads=[kdec, vtk], writes=[pS])
            p.op("vector", lambda v: v.scalar_tensor_tensor(
                Mst[:, :], Mst[:, :], dd.dp128col[:, 0:1], pS[:, 0:128],
                op0=ALU.mult, op1=ALU.add),
                reads=[pS, Mst, dd.dp128col], writes=[Mst])
            p.op("vector", lambda v: v.tensor_copy(Mstb[:, :], Mst[:, :]),
                 reads=[Mst], writes=[Mstb])
        for j in range(nt):
            tw = TT
            t0 = j * TT
            p.dma("sync", chunks(bf_.xt[:, 0:NCH * tw], NCH, tw),
                  xo[:, t0:t0 + tw].rearrange("(c p) t -> p c t", p=128),
                  writes=[bf_.xt])
            for m in range(NCH):
                pm = pm0 if m % 2 == 0 else pm1
                mm_group(p, pm[:, 0:tw],
                         [(wd["o"][:, m * 128:(m + 1) * 128], rt[:, t0:t0 + tw])],
                         reads=[wd["o"], rt], writes=[pm])
                p.op("vector", lambda v, m=m, tw=tw, pm=pm: v.scalar_tensor_tensor(
                    bf_.xt[:, m * tw:(m + 1) * tw], pm[:, 0:tw],
                    oscol[:, 0:1], bf_.xt[:, m * tw:(m + 1) * tw],
                    op0=ALU.mult, op1=ALU.add),
                    reads=[pm, oscol, bf_.xt], writes=[bf_.xt])
            p.dma("sync", xout[:, t0:t0 + tw].rearrange("(c p) t -> p c t", p=128),
                  chunks(bf_.xt[:, 0:NCH * tw], NCH, tw),
                  reads=[bf_.xt], writes=[xout])
        p.dma("sync", sout, Mst[:, :], reads=[Mst], writes=[sout])
        p.dma("sync", qout, qT[:, :], reads=[qT], writes=[qout])
        p.finish()
        with nc.Block() as block:
            p.emit(block)
    return nc


def build_scan_b(n_own):
    nc = bass.Bass("TRN2", target_bir_lowering=False)
    xo = nc.dram_tensor("xo_fm", [D, n_own], F32, kind="ExternalInput").ap()
    qin = nc.dram_tensor("qin", [128, n_own], BF16, kind="ExternalInput").ap()
    sstk = nc.dram_tensor("sstk", [3, 128, 128], F32, kind="ExternalInput").ap()
    basd = nc.dram_tensor("basis", [D, 64], F32, kind="ExternalInput").ap()
    ocd = nc.dram_tensor("ocoef", [128, 64], F32, kind="ExternalInput").ap()
    decd = nc.dram_tensor("decay", [1, 1], F32, kind="ExternalInput").ap()
    oscd = nc.dram_tensor("oscol", [128, 1], F32, kind="ExternalInput").ap()
    iott = nc.dram_tensor("iota_t", [1, n_own], F32, kind="ExternalInput").ap()
    cap = declare_consts(nc)
    scap = {}
    for name, arr in scan_consts().items():
        scap[name] = nc.dram_tensor(name, list(arr.shape), F32, kind="ExternalInput").ap()
    identb_in = nc.dram_tensor("identb_in", [128, 128], BF16, kind="ExternalInput").ap()
    xout = nc.dram_tensor("xout_fm", [D, n_own], F32, kind="ExternalOutput").ap()
    nt = cdiv(n_own, TT)

    with contextlib.ExitStack() as ctx:
        def sb(name, shape, dt=F32):
            return ctx.enter_context(nc.sbuf_tensor(name, shape, dt))

        def psd(name, shape, dt=F32):
            return ctx.enter_context(nc.psum_tensor(name, shape, dt))

        bf_ = NS()
        bf_.xt = sb("xt", [128, NCH * TT])
        bf_.dt_ = sb("dt", [128, NCH * TT])
        bf_.sqbf = sb("sqbf", [128, NCH * TT], BF16)
        bf_.xnb = sb("xnb", [128, NCH * TT], BF16)
        bf_.r1 = sb("r1", [1, TT])
        bf_.r1b = sb("r1b", [1, TT], BF16)
        qT = sb("qT", [128, n_own], BF16)
        qd = sb("qd", [128, TT], BF16)
        rc = sb("rc", [128, TT], BF16)
        Mi = sb("Mi", [128, 128])
        Mib = sb("Mib", [128, 128], BF16)
        stile = sb("stile", [128, 3 * 128])
        wst = sb("wst", [128, 4 * D])
        iotat = sb("iotat", [1, n_own])
        dgrow = sb("dgrow", [1, n_own], BF16)
        ecol = sb("ecol", [128, 3])
        oscol = sb("oscol_sb", [128, 1])

        bf_.pr1 = psd("pr1", [1, TT])
        bf_.pbc = psd("pbc", [128, TT])
        pm0 = psd("pm0", [128, TT])
        pm1 = psd("pm1", [128, TT])
        pbt = psd("pbt", [128, 128], BF16)

        p = Prog(nc, ctx)
        cm = load_common(p, sb, cap)
        cm.identb = sb("identb", [128, 128], BF16)
        cm.onesf = sb("onesf", [1, 128])
        p.dma("sync", cm.identb[:, :], identb_in, writes=[cm.identb])
        p.dma("sync", cm.onesf[:, :], scap["sc_ones_f32"], writes=[cm.onesf])
        p.dma("sync", oscol[:, :], oscd, writes=[oscol])
        p.dma("sync", iotat[:, :], iott, writes=[iotat])
        p.dma("sync", qT[:, :], qin, writes=[qT])
        p.dma("sync", stile[:, :].rearrange("p (s q) -> p s q", s=3),
              sstk.rearrange("s p q -> p s q"), writes=[stile])

        wd = emit_scan_wprep(p, nc, sb, psd, bf_, cm, basd, {"o": ocd}, wst,
                             pm0, pm1, pbt, names=("o",))
        dsc = sb("dsc2", [1, 2])
        ldcol = sb("ldcol2", [128, 1])
        p.dma("sync", dsc[0:1, 0:1], decd, writes=[dsc])
        p.op("scalar", lambda s: s.activation(dsc[0:1, 0:1], dsc[0:1, 0:1], AF.Sigmoid),
             reads=[dsc], writes=[dsc])
        p.op("scalar", lambda s: s.activation(dsc[0:1, 1:2], dsc[0:1, 0:1], AF.Ln),
             reads=[dsc], writes=[dsc])
        p.op("tensor", lambda t: t.matmul(bf_.pbc[:, 0:1], cm.onesf[0:1, 0:128],
                                          dsc[0:1, 1:2], start=True, stop=True),
             reads=[dsc, cm.onesf], writes=[bf_.pbc])
        p.op("vector", lambda v: v.tensor_copy(ldcol[:, 0:1], bf_.pbc[:, 0:1]),
             reads=[bf_.pbc], writes=[ldcol])
        for kk in range(3):
            p.op("scalar", lambda s, kk=kk: s.activation(
                ecol[:, kk:kk + 1], ldcol[:, 0:1], AF.Exp, scale=2048.0 * (kk + 1)),
                reads=[ldcol], writes=[ecol])
        p.op("scalar", lambda s: s.activation(dgrow[0:1, 0:n_own], iotat[0:1, 0:n_own],
                                              AF.Exp, scale=dsc[0:1, 1:2]),
             reads=[iotat, dsc], writes=[dgrow])
        p.op("vector", lambda v: v.tensor_scalar(Mi[:, :], stile[:, 0:128],
                                                 ecol[:, 0:1], None, op0=ALU.mult),
             reads=[stile, ecol], writes=[Mi])
        p.op("vector", lambda v: v.scalar_tensor_tensor(
            Mi[:, :], stile[:, 128:256], ecol[:, 1:2], Mi[:, :],
            op0=ALU.mult, op1=ALU.add),
            reads=[stile, ecol, Mi], writes=[Mi])
        p.op("vector", lambda v: v.scalar_tensor_tensor(
            Mi[:, :], stile[:, 256:384], ecol[:, 2:3], Mi[:, :],
            op0=ALU.mult, op1=ALU.add),
            reads=[stile, ecol, Mi], writes=[Mi])
        p.op("vector", lambda v: v.tensor_copy(Mib[:, :], Mi[:, :]),
             reads=[Mi], writes=[Mib])

        for j in range(nt):
            tw = TT
            t0 = j * TT
            p.dma("sync", chunks(bf_.xt[:, 0:NCH * tw], NCH, tw),
                  xo[:, t0:t0 + tw].rearrange("(c p) t -> p c t", p=128),
                  writes=[bf_.xt])
            mm_group(p, bf_.pbc[:, 0:tw],
                     [(cm.onesb[0:1, 0:128], dgrow[0:1, t0:t0 + tw])],
                     reads=[cm.onesb, dgrow], writes=[bf_.pbc])
            p.op("vector", lambda v, t0=t0, tw=tw: v.tensor_tensor(
                qd[:, 0:tw], qT[:, t0:t0 + tw], bf_.pbc[:, 0:tw], op=ALU.mult),
                reads=[qT, bf_.pbc], writes=[qd])
            mm_group(p, pm0[:, 0:tw], [(Mib[:, :], qd[:, 0:tw])],
                     reads=[Mib, qd], writes=[pm0])
            p.op("vector", lambda v: v.tensor_copy(rc[:, 0:tw], pm0[:, 0:tw]),
                 reads=[pm0], writes=[rc])
            for m in range(NCH):
                pm = pm1
                mm_group(p, pm[:, 0:tw],
                         [(wd["o"][:, m * 128:(m + 1) * 128], rc[:, 0:tw])],
                         reads=[wd["o"], rc], writes=[pm])
                p.op("vector", lambda v, m=m, tw=tw, pm=pm: v.scalar_tensor_tensor(
                    bf_.xt[:, m * tw:(m + 1) * tw], pm[:, 0:tw],
                    oscol[:, 0:1], bf_.xt[:, m * tw:(m + 1) * tw],
                    op0=ALU.mult, op1=ALU.add),
                    reads=[pm, oscol, bf_.xt], writes=[bf_.xt])
            p.dma("sync", xout[:, t0:t0 + tw].rearrange("(c p) t -> p c t", p=128),
                  chunks(bf_.xt[:, 0:NCH * tw], NCH, tw),
                  reads=[bf_.xt], writes=[xout])
        p.finish()
        with nc.Block() as block:
            p.emit(block)
    return nc


def run_scan(x_cur, w):
    consts = host_constants()
    sconsts = scan_consts()
    identb = np.eye(128, dtype=BF)
    xf = np.asarray(x_cur).reshape(B * S, D)
    oscol = np.broadcast_to(np.asarray(w["out_scale"]).reshape(1, 1),
                            (128, 1)).astype(np.float32).copy()
    nc1 = build_scan_a(T)
    in_maps = []
    base = {
        "basis": np.asarray(w["basis"]),
        "qcoef": np.asarray(w["q_coeffs"]), "kcoef": np.asarray(w["k_coeffs"]),
        "vcoef": np.asarray(w["v_coeffs"]), "ocoef": np.asarray(w["o_coeffs"]),
        "decay": np.asarray(w["decay"]).reshape(1, 1),
        "oscol": oscol, "identb_in": identb,
    }
    for c in range(NCORES):
        m = {"xo_fm": _fm(xf[c * T:(c + 1) * T])}
        m.update(base)
        m.update(consts)
        m.update(sconsts)
        in_maps.append(m)
    res1 = run_bass_kernel_spmd(nc1, in_maps, core_ids=list(range(NCORES)))
    xpart = [res1.results[c]["xout_fm"] for c in range(NCORES)]
    souts = [res1.results[c]["sout"] for c in range(NCORES)]
    qouts = [res1.results[c]["qout"] for c in range(NCORES)]

    nc2 = build_scan_b(T)
    in_maps = []
    iota_t = np.arange(T, dtype=np.float32).reshape(1, T)
    for c in range(NCORES):
        r = c % 4
        stk = np.zeros((3, 128, 128), dtype=np.float32)
        for kk in range(1, 4):
            if r - kk >= 0:
                stk[kk - 1] = souts[c - kk]
        m = {
            "xo_fm": np.ascontiguousarray(xpart[c]),
            "qin": qouts[c], "sstk": stk,
            "basis": base["basis"], "ocoef": base["ocoef"],
            "decay": base["decay"], "oscol": oscol,
            "iota_t": iota_t, "identb_in": identb,
        }
        m.update(consts)
        m.update(sconsts)
        in_maps.append(m)
    res2 = run_bass_kernel_spmd(nc2, in_maps, core_ids=list(range(NCORES)))
    out = np.empty((B * S, D), dtype=np.float32)
    for c in range(NCORES):
        out[c * T:(c + 1) * T] = res2.results[c]["xout_fm"].T
    return out.reshape(B, S, D)


def _retry(fn):
    try:
        return fn()
    except Exception:
        return fn()


def kernel(**inputs):
    g = {k: np.ascontiguousarray(np.asarray(v, dtype=np.float32))
         for k, v in inputs.items()}
    x = g["L0_attn_a_x"]
    w = {n: g[f"L0_attn_a_{n}"] for n in
         ["Wq", "bq", "Wk", "bk", "Wv", "bv", "Wo", "bo", "projection"]}
    x = _retry(lambda: run_performer(x, w))
    wf = {"W_r": g["L0_ffn_W_r"], "W_k": g["L0_ffn_W_k"], "W_v": g["L0_ffn_W_v"],
          "mu_r": g["L0_ffn_mu_r"], "mu_k": g["L0_ffn_mu_k"]}
    x = _retry(lambda: run_rwkv(x, wf))
    wm0 = {"gate_w": g["L0_attn_b_gate_w"], "gate_b": g["L0_attn_b_gate_b"],
           "w1": g["L0_attn_b_w1"], "b1": g["L0_attn_b_b1"],
           "w2": g["L0_attn_b_w2"], "b2": g["L0_attn_b_b2"]}
    x = _retry(lambda: run_moe(x, wm0, topk=2, act="gelu"))
    x = _retry(lambda: run_rwkv(x, wf))
    wa = {n: g[f"L1_attn_a_{n}"] for n in
          ["Wq", "bq", "Wk", "bk", "Wv", "bv", "Wo", "bo", "W1", "b1",
           "W2", "b2", "g1", "beta1", "g2", "beta2"]}
    x = _retry(lambda: run_l1a(x, wa))
    wm1 = {"gate_w": g["L1_ffn_Wg"], "gate_b": g["L1_ffn_bg"],
           "w1": g["L1_ffn_W1"], "b1": g["L1_ffn_b1"],
           "w2": g["L1_ffn_W2"], "b2": g["L1_ffn_b2"]}
    x = _retry(lambda: run_moe(x, wm1, topk=1, act="silu"))
    ws = {"basis": g["L1_attn_b_basis"], "q_coeffs": g["L1_attn_b_q_coeffs"],
          "k_coeffs": g["L1_attn_b_k_coeffs"], "v_coeffs": g["L1_attn_b_v_coeffs"],
          "o_coeffs": g["L1_attn_b_o_coeffs"], "decay": g["L1_attn_b_decay"],
          "out_scale": g["L1_attn_b_out_scale"]}
    x = _retry(lambda: run_scan(x, ws))
    x = _retry(lambda: run_moe(x, wm1, topk=1, act="silu"))
    return np.asarray(x, dtype=np.float32)


# version 03 (final)
# speedup vs baseline: 1.5885x; 1.5885x over previous
import contextlib
import numpy as np
import ml_dtypes

import concourse.bass as bass
import concourse.mybir as mybir
from concourse.bass_utils import run_bass_kernel_spmd

F32 = mybir.dt.float32
BF16 = mybir.dt.bfloat16
AF = mybir.ActivationFunctionType
ALU = mybir.AluOpType
AX = mybir.AxisListType

D = 1024
NCH = 8
B = 2
S = 8192
NCORES = 8
T = 2048
TT = 512
NH = 16
HD = 64
NF = 256
BF = ml_dtypes.bfloat16
ENGINES = ("sync", "scalar", "vector", "tensor", "gpsimd")
DMA_ENGINES = ("sync", "scalar", "gpsimd")


def cdiv(a, b):
    return -(-a // b)


def bcast(ap2d, n, tw):
    return ap2d.rearrange("p (o t) -> p o t", o=1).broadcast_to((ap2d.shape[0], n, tw))


def chunks(ap2d, n, tw):
    return ap2d.rearrange("p (k t) -> p k t", k=n)



class _Op:
    __slots__ = ("engine", "fn", "idx", "is_dma", "slot", "slot_seq", "mark", "deps")

    def __init__(self, idx, engine, fn, is_dma, slot):
        self.idx = idx
        self.engine = engine
        self.fn = fn
        self.is_dma = is_dma
        self.slot = slot
        self.slot_seq = 0
        self.mark = None
        self.deps = ()


SAFE_EDGES = False
EDGE_ENGINES = {"vector", "scalar"}


def _need_edge(d, o):
    if SAFE_EDGES:
        return True
    if d.engine != o.engine:
        return True
    if d.engine in EDGE_ENGINES:
        return True
    return False


class Prog:
    def __init__(self, nc, ctx):
        self.nc = nc
        self.ctx = ctx
        self.ops = []
        self.last_writer = {}
        self.readers = {}
        self.prog_sem = {e: ctx.enter_context(nc.semaphore(f"pg_{e}")) for e in ENGINES}
        self.slot_sems = {}
        self.slot_counts = {}

    def op(self, engine, fn, reads=(), writes=(), deps=()):
        return self._record(engine, fn, tuple(reads), tuple(writes), False, None, deps)

    def dma(self, engine, out, in_, reads=(), writes=(), deps=(), slot=None):
        assert engine in DMA_ENGINES
        if slot is None:
            slot = id(writes[0]) if writes else id(out)
        return self._record(engine, lambda eng: eng.dma_start(out=out, in_=in_),
                            tuple(reads), tuple(writes), True, slot, deps)

    def _record(self, engine, fn, reads, writes, is_dma, slot, extra_deps):
        o = _Op(len(self.ops), engine, fn, is_dma, slot)
        deps = set()
        for r in reads:
            w = self.last_writer.get(id(r))
            if w is not None:
                deps.add(w)
        for w in writes:
            for rd in self.readers.get(id(w), ()):
                deps.add(rd)
            lw = self.last_writer.get(id(w))
            if lw is not None:
                deps.add(lw)
        for d in extra_deps:
            if d is not None:
                deps.add(d)
        deps.discard(o)
        o.deps = tuple(sorted(deps, key=lambda q: q.idx))
        self.ops.append(o)
        for r in reads:
            self.readers.setdefault(id(r), []).append(o)
        for w in writes:
            self.last_writer[id(w)] = o
            self.readers[id(w)] = []
        if is_dma:
            if slot not in self.slot_sems:
                self.slot_sems[slot] = self.ctx.enter_context(
                    self.nc.semaphore(f"dma{len(self.slot_sems)}"))
                self.slot_counts[slot] = 0
            self.slot_counts[slot] += 1
            o.slot_seq = self.slot_counts[slot]
        return o

    def finish(self, engine="gpsimd"):
        last_per_engine = {}
        for o in self.ops:
            last_per_engine[o.engine] = o
        deps = [o for o in self.ops if o.is_dma]
        deps += [v for k, v in last_per_engine.items() if k != engine]
        sems = [(self.slot_sems[s], 16 * self.slot_counts[s]) for s in self.slot_sems]

        def fn(eng):
            for sem, val in sems:
                eng.wait_ge(sem, val)
            return None
        self._record(engine, fn, (), (), False, None, tuple(deps))

    def emit(self, block):
        marks = {e: 0 for e in ENGINES}
        flagged = set()
        for o in self.ops:
            for d in o.deps:
                if (not d.is_dma) and _need_edge(d, o):
                    flagged.add(d)
        for o in self.ops:
            if o in flagged:
                marks[o.engine] += 1
                o.mark = marks[o.engine]
        per_engine = {e: [o for o in self.ops if o.engine == e] for e in ENGINES}
        watermark = {e: {} for e in ENGINES}

        def make_body(eng_name):
            def run(eng_obj):
                wm = watermark[eng_name]
                for o in per_engine[eng_name]:
                    for d in o.deps:
                        if d.is_dma:
                            sem, val = self.slot_sems[d.slot], 16 * d.slot_seq
                        elif _need_edge(d, o):
                            sem, val = self.prog_sem[d.engine], d.mark
                            if val is None:
                                continue
                        else:
                            continue
                        if wm.get(id(sem), -1) < val:
                            eng_obj.wait_ge(sem, val)
                            wm[id(sem)] = val
                    inst = o.fn(eng_obj)
                    if o.is_dma:
                        inst.then_inc(self.slot_sems[o.slot], 16)
                    elif o.mark is not None:
                        if inst is None:
                            raise RuntimeError("marked op returned no instruction")
                        inst.then_inc(self.prog_sem[o.engine], 1)
            return run

        block.sync(make_body("sync"))
        block.scalar(make_body("scalar"))
        block.vector(make_body("vector"))
        block.tensor(make_body("tensor"))
        block.gpsimd(make_body("gpsimd"))


def mm_group(p, psum_ap, pairs, reads, writes, deps=()):
    def fn(t):
        inst = None
        n = len(pairs)
        for i, (l, r) in enumerate(pairs):
            inst = t.matmul(psum_ap, l, r, start=(i == 0), stop=(i == n - 1))
        return inst
    return p.op("tensor", fn, reads=reads, writes=writes, deps=deps)



def host_constants():
    c = {}
    c["c_ones_row_bf"] = np.ones((1, 128), dtype=BF)
    c["c_ones_col_bf"] = np.ones((128, 1), dtype=BF)
    sel = np.zeros((128, 8 * 16), dtype=BF)
    selT = np.zeros((16, 8 * 128), dtype=BF)
    for ch in range(8):
        for pp in range(128):
            j = 2 * ch + pp // 64
            sel[pp, ch * 16 + j] = 1
            selT[j, ch * 128 + pp] = 1
    c["c_hsel"] = sel
    c["c_hselT"] = selT
    c["c_eps"] = np.array([[1e-5]], dtype=np.float32)
    return c


def declare_consts(nc):
    cap = {}
    for name, arr in host_constants().items():
        dt = BF16 if arr.dtype == BF else F32
        cap[name] = nc.dram_tensor(name, list(arr.shape), dt, kind="ExternalInput").ap()
    return cap


class NS:
    pass


def load_common(p, sb, cap):
    cm = NS()
    cm.onesb = sb("onesb", [1, 128], BF16)
    cm.onescb = sb("onescb", [128, 1], BF16)
    cm.hsel = sb("hsel", [128, 8 * 16], BF16)
    cm.hselT = sb("hselT", [16, 8 * 128], BF16)
    cm.epsc = sb("epsc", [1, 1], F32)
    p.dma("sync", cm.onesb[:, :], cap["c_ones_row_bf"], writes=[cm.onesb])
    p.dma("sync", cm.onescb[:, :], cap["c_ones_col_bf"], writes=[cm.onescb])
    p.dma("sync", cm.hsel[:, :], cap["c_hsel"], writes=[cm.hsel])
    p.dma("sync", cm.hselT[:, :], cap["c_hselT"], writes=[cm.hselT])
    p.dma("sync", cm.epsc[:, :], cap["c_eps"], writes=[cm.epsc])
    return cm


def load_w1024(p, w_ap, stage, dst, ncols=D, queue="sync", split_cast=False):
    half = NCH // 2
    for hh in range(2):
        p.dma(queue, stage[:, 0:half * ncols].rearrange("p (c n) -> p c n", c=half),
              w_ap[hh * half * 128:(hh + 1) * half * 128, :]
              .rearrange("(c p) n -> p c n", p=128), writes=[stage])
        if split_cast:
            hw = half * ncols // 2
            p.op("vector", lambda v, hh=hh, hw=hw: v.tensor_copy(
                dst[:, hh * half * ncols:hh * half * ncols + hw],
                stage[:, 0:hw]),
                reads=[stage], writes=[dst])
            p.op("scalar", lambda s, hh=hh, hw=hw: s.copy(
                dst[:, hh * half * ncols + hw:(hh + 1) * half * ncols],
                stage[:, hw:half * ncols]),
                reads=[stage], writes=[dst])
        else:
            p.op("vector", lambda v, hh=hh: v.tensor_copy(
                dst[:, hh * half * ncols:(hh + 1) * half * ncols],
                stage[:, 0:half * ncols]),
                reads=[stage], writes=[dst])


def emit_ln3(p, cm, bf_, tw, src3, src_res, dst3, dst_res, gain=None, beta=None):
    dt_, sqbf = bf_.dt_, bf_.sqbf
    r1, r1b, pr1, pbc = bf_.r1, bf_.r1b, bf_.pr1, bf_.pbc
    p.op("vector", lambda v: v.tensor_copy(chunks(sqbf[:, 0:NCH * tw], NCH, tw), src3),
         reads=[src_res], writes=[sqbf])
    mm_group(p, pr1[0:1, 0:tw],
             [(cm.onescb[:, 0:1], sqbf[:, k * tw:(k + 1) * tw]) for k in range(NCH)],
             reads=[sqbf, cm.onescb], writes=[pr1])
    p.op("scalar", lambda s: s.activation(r1b[0:1, 0:tw], pr1[0:1, 0:tw], AF.Copy,
                                          scale=1.0 / D),
         reads=[pr1], writes=[r1b])
    mm_group(p, pbc[:, 0:tw], [(cm.onesb[0:1, 0:128], r1b[0:1, 0:tw])],
             reads=[r1b, cm.onesb], writes=[pbc])
    p.op("vector", lambda v: v.tensor_tensor(chunks(dt_[:, 0:NCH * tw], NCH, tw),
                                             src3, bcast(pbc[:, 0:tw], NCH, tw),
                                             op=ALU.subtract),
         reads=[src_res, pbc], writes=[dt_])
    p.op("scalar", lambda s: s.activation(sqbf[:, 0:NCH * tw], dt_[:, 0:NCH * tw],
                                          AF.Square),
         reads=[dt_], writes=[sqbf])
    mm_group(p, pr1[0:1, 0:tw],
             [(cm.onescb[:, 0:1], sqbf[:, k * tw:(k + 1) * tw]) for k in range(NCH)],
             reads=[sqbf, cm.onescb], writes=[pr1])
    p.op("scalar", lambda s: s.activation(r1[0:1, 0:tw], pr1[0:1, 0:tw], AF.Ln,
                                          scale=1.0 / D, bias=cm.epsc[0:1, 0:1]),
         reads=[pr1, cm.epsc], writes=[r1])
    p.op("scalar", lambda s: s.activation(r1b[0:1, 0:tw], r1[0:1, 0:tw], AF.Exp,
                                          scale=-0.5),
         reads=[r1], writes=[r1b])
    mm_group(p, pbc[:, 0:tw], [(cm.onesb[0:1, 0:128], r1b[0:1, 0:tw])],
             reads=[r1b, cm.onesb], writes=[pbc])
    if gain is None:
        p.op("vector", lambda v: v.tensor_tensor(dst3,
                                                 chunks(dt_[:, 0:NCH * tw], NCH, tw),
                                                 bcast(pbc[:, 0:tw], NCH, tw),
                                                 op=ALU.mult),
             reads=[dt_, pbc], writes=[dst_res])
    else:
        p.op("vector", lambda v: v.tensor_tensor(chunks(dt_[:, 0:NCH * tw], NCH, tw),
                                                 chunks(dt_[:, 0:NCH * tw], NCH, tw),
                                                 bcast(pbc[:, 0:tw], NCH, tw),
                                                 op=ALU.mult),
             reads=[dt_, pbc], writes=[dt_])
        for c in range(NCH):
            out_ap = dst3[:, c:c + 1, :].rearrange("p o t -> p (o t)")
            p.op("vector", lambda v, c=c, out_ap=out_ap: v.tensor_scalar(
                out_ap, dt_[:, c * tw:(c + 1) * tw],
                gain[:, c:c + 1], beta[:, c:c + 1], op0=ALU.mult, op1=ALU.add),
                reads=[dt_, gain, beta], writes=[dst_res])


def emit_ln(p, cm, bf_, tw, dst=None, src=None, gain=None, beta=None):
    xt = bf_.xt if src is None else src
    xnb = bf_.xnb if dst is None else dst
    emit_ln3(p, cm, bf_, tw,
             chunks(xt[:, 0:NCH * tw], NCH, tw), xt,
             chunks(xnb[:, 0:NCH * tw], NCH, tw), xnb, gain=gain, beta=beta)


def emit_headnorm(p, cm, bf_, src_f32, dst_bf, tw, eps_col):
    rows, rowbf, prow, pbc = bf_.rows, bf_.rowbf, bf_.prow, bf_.pbc
    p.op("scalar", lambda s: s.activation(bf_.sqbf[:, 0:NCH * tw], src_f32[:, 0:NCH * tw],
                                          AF.Square),
         reads=[src_f32], writes=[bf_.sqbf])
    mm_group(p, prow[0:16, 0:tw],
             [(cm.hsel[:, ch * 16:(ch + 1) * 16], bf_.sqbf[:, ch * tw:(ch + 1) * tw])
              for ch in range(NCH)],
             reads=[bf_.sqbf, cm.hsel], writes=[prow])
    p.op("scalar", lambda s: s.activation(rows[0:16, 0:tw], prow[0:16, 0:tw], AF.Ln),
         reads=[prow], writes=[rows])
    p.op("scalar", lambda s: s.activation(rows[0:16, 0:tw], rows[0:16, 0:tw], AF.Exp,
                                          scale=0.5),
         reads=[rows], writes=[rows])
    p.op("vector", lambda v: v.tensor_scalar(rows[0:16, 0:tw], rows[0:16, 0:tw],
                                             eps_col, None, op0=ALU.add),
         reads=[rows], writes=[rows])
    p.op("vector", lambda v: v.reciprocal(rows[0:16, 0:tw], rows[0:16, 0:tw]),
         reads=[rows], writes=[rows])
    p.op("vector", lambda v: v.tensor_copy(rowbf[0:16, 0:tw], rows[0:16, 0:tw]),
         reads=[rows], writes=[rowbf])
    mm_group(p, pbc[:, 0:tw],
             [(cm.hselT[0:16, 0:128], rowbf[0:16, 0:tw])],
             reads=[rowbf, cm.hselT], writes=[pbc])


def emit_headnorm_pc(p, cm, bf_, src_f32, dst_bf, tw, eps_col):
    rows, rowbf, prow = bf_.rows, bf_.rowbf, bf_.prow
    p.op("scalar", lambda s: s.activation(bf_.sqbf[:, 0:NCH * tw], src_f32[:, 0:NCH * tw],
                                          AF.Square),
         reads=[src_f32], writes=[bf_.sqbf])
    mm_group(p, prow[0:16, 0:tw],
             [(cm.hsel[:, ch * 16:(ch + 1) * 16], bf_.sqbf[:, ch * tw:(ch + 1) * tw])
              for ch in range(NCH)],
             reads=[bf_.sqbf, cm.hsel], writes=[prow])
    p.op("scalar", lambda s: s.activation(rows[0:16, 0:tw], prow[0:16, 0:tw], AF.Ln),
         reads=[prow], writes=[rows])
    p.op("scalar", lambda s: s.activation(rows[0:16, 0:tw], rows[0:16, 0:tw], AF.Exp,
                                          scale=0.5),
         reads=[rows], writes=[rows])
    p.op("vector", lambda v: v.tensor_scalar(rows[0:16, 0:tw], rows[0:16, 0:tw],
                                             eps_col, None, op0=ALU.add),
         reads=[rows], writes=[rows])
    p.op("vector", lambda v: v.reciprocal(rows[0:16, 0:tw], rows[0:16, 0:tw]),
         reads=[rows], writes=[rows])
    p.op("vector", lambda v: v.tensor_copy(rowbf[0:16, 0:tw], rows[0:16, 0:tw]),
         reads=[rows], writes=[rowbf])
    for ch in range(NCH):
        pb = bf_.pbc
        mm_group(p, pb[:, 0:tw],
                 [(cm.hselT[0:16, ch * 128:(ch + 1) * 128], rowbf[0:16, 0:tw])],
                 reads=[rowbf, cm.hselT], writes=[pb])
        p.op("vector", lambda v, ch=ch, pb=pb: v.tensor_tensor(
            dst_bf[:, ch * tw:(ch + 1) * tw], src_f32[:, ch * tw:(ch + 1) * tw],
            pb[:, 0:tw], op=ALU.mult),
            reads=[src_f32, pb], writes=[dst_bf])



def build_performer(n_own, n_batch):
    nc = bass.Bass("TRN2", target_bir_lowering=False)
    xb = nc.dram_tensor("xb_fm", [D, n_batch], F32, kind="ExternalInput").ap()
    xo = nc.dram_tensor("xo_fm", [D, n_own], F32, kind="ExternalInput").ap()
    Wq = nc.dram_tensor("Wq", [D, D], F32, kind="ExternalInput").ap()
    Wk = nc.dram_tensor("Wk", [D, D], F32, kind="ExternalInput").ap()
    Wv = nc.dram_tensor("Wv", [D, D], F32, kind="ExternalInput").ap()
    Wo = nc.dram_tensor("Wo", [D, D], F32, kind="ExternalInput").ap()
    bq = nc.dram_tensor("bq", [128, NCH], F32, kind="ExternalInput").ap()
    bk = nc.dram_tensor("bk", [128, NCH], F32, kind="ExternalInput").ap()
    bv = nc.dram_tensor("bv", [1, D], F32, kind="ExternalInput").ap()
    bo = nc.dram_tensor("bo", [128, NCH], F32, kind="ExternalInput").ap()
    projT2 = nc.dram_tensor("projT2", [128, NH * NF], F32, kind="ExternalInput").ap()
    cap = declare_consts(nc)
    xout = nc.dram_tensor("xout_fm", [D, n_own], F32, kind="ExternalOutput").ap()

    ntb = cdiv(n_batch, TT)
    ntq = cdiv(n_own, TT)

    with contextlib.ExitStack() as ctx:
        def sb(name, shape, dt=F32):
            return ctx.enter_context(nc.sbuf_tensor(name, shape, dt))

        def psd(name, shape):
            return ctx.enter_context(nc.psum_tensor(name, shape, F32))

        bf_ = NS()
        bf_.xt = sb("xt", [128, NCH * TT])
        bf_.dt_ = sb("dt", [128, NCH * TT])
        bf_.sqbf = sb("sqbf", [128, NCH * TT], BF16)
        bf_.xnb = sb("xnb", [128, NCH * TT], BF16)
        bf_.r1 = sb("r1", [1, TT])
        bf_.r1b = sb("r1b", [1, TT], BF16)
        bf_.rows = sb("rows", [16, TT])
        bf_.rowbf = sb("rowbf", [16, TT], BF16)
        kfm = bf_.dt_
        knb = sb("knb", [128, NCH * TT], BF16)
        kpt = sb("kpt", [128, NH * NF], BF16)
        vtok = sb("vtok", [128, NH * (HD + 1)], BF16)
        kvs = sb("kvs", [128, NH * 2 * (HD + 1)])
        kvb = sb("kvb", [128, NH * 2 * (HD + 1)], BF16)
        qpb = sb("qpb", [128, 2 * TT], BF16)
        att = sb("att", [128, NCH * TT], BF16)
        rbs = sb("rbs", [64, TT])
        epsq = sb("epsq", [16, 1])
        epsn = sb("epsn", [1, 1])
        wst = sb("wst", [128, (NCH // 2) * D])
        wqb = sb("wqb", [128, NCH * D], BF16)
        wkb = sb("wkb", [128, NCH * D], BF16)
        wvb = sb("wvb", [128, NCH * D], BF16)
        wob = sb("wob", [128, NCH * D], BF16)
        pjb = sb("pjb", [128, NH * NF], BF16)
        bqs = sb("bqs", [128, NCH])
        bks = sb("bks", [128, NCH])
        bos = sb("bos", [128, NCH])
        bvb = sb("bvb", [1, D], BF16)

        bf_.pr1 = psd("pr1", [1, TT])
        bf_.pbc = psd("pbc", [128, TT])
        bf_.prow = psd("prow", [16, TT])
        pm0 = psd("pm0", [128, TT])
        pm1 = psd("pm1", [128, TT])
        pv = psd("pv", [128, D])
        pq65 = psd("pq65", [65, TT])

        p = Prog(nc, ctx)
        cm = load_common(p, sb, cap)
        p.op("vector", lambda v: v.memset(epsq[:, :], 1e-8), writes=[epsq])

        for w_ap, dst in [(Wq, wqb), (Wk, wkb), (Wv, wvb), (Wo, wob)]:
            load_w1024(p, w_ap, wst, dst)
        p.dma("sync", wst[:, 0:NH * NF], projT2, writes=[wst])
        p.op("vector", lambda v: v.tensor_copy(pjb[:, :], wst[:, 0:NH * NF]),
             reads=[wst], writes=[pjb])
        p.dma("sync", bqs[:, :], bq, writes=[bqs])
        p.dma("sync", bks[:, :], bk, writes=[bks])
        p.dma("sync", bos[:, :], bo, writes=[bos])
        p.dma("sync", wst[0:1, 0:D], bv, writes=[wst])
        p.op("vector", lambda v: v.tensor_copy(bvb[:, :], wst[0:1, 0:D]),
             reads=[wst], writes=[bvb])

        def proj_1024(src_bf, wdst, bias_cols, out_f32, tw, pms):
            for m in range(NCH):
                pm = pms[m % 2]
                mm_group(p, pm[:, 0:tw],
                         [(wdst[:, k * D + m * 128:k * D + (m + 1) * 128],
                           src_bf[:, k * tw:(k + 1) * tw]) for k in range(NCH)],
                         reads=[wdst, src_bf], writes=[pm])
                p.op("vector", lambda v, m=m, pm=pm: v.tensor_scalar(
                    out_f32[:, m * tw:(m + 1) * tw], pm[:, 0:tw],
                    bias_cols[:, m:m + 1], None, op0=ALU.add),
                    reads=[pm, bias_cols], writes=[out_f32])

        for i in range(ntb):
            tw = TT
            t0 = i * TT
            p.dma("sync", bf_.xt[:, 0:NCH * tw].rearrange("p (c t) -> p c t", c=NCH),
                  xb[:, t0:t0 + tw].rearrange("(c p) t -> p c t", p=128),
                  writes=[bf_.xt])
            emit_ln(p, cm, bf_, tw)
            proj_1024(bf_.xnb, wkb, bks, kfm, tw, (pm0, pm1))
            emit_headnorm_pc(p, cm, bf_, kfm, knb, tw, epsq[0:16, 0:1])
            for tc in range(tw // 128):
                pairs = []
                for k in range(NCH):
                    for nhalf in range(2):
                        pass
                for half in range(2):
                    mm_group(p, pv[:, half * 512:(half + 1) * 512],
                             [(bf_.xnb[:, k * tw + tc * 128:k * tw + tc * 128 + 128],
                               wvb[:, k * D + half * 512:k * D + (half + 1) * 512])
                              for k in range(NCH)]
                             + [(cm.onesb[0:1, 0:128], bvb[0:1, half * 512:(half + 1) * 512])],
                             reads=[bf_.xnb, wvb, cm.onesb, bvb], writes=[pv])
                p.op("vector", lambda v: v.memset(vtok[:, :], 1.0), writes=[vtok])
                p.op("vector", lambda v: v.tensor_copy(
                    vtok[:, :].rearrange("p (h f) -> p h f", h=NH)[:, :, 0:HD],
                    pv[:, 0:D].rearrange("p (h f) -> p h f", h=NH)),
                    reads=[pv], writes=[vtok])
                for h in range(NH):
                    ch = h // 2
                    base = 64 * (h % 2)
                    pm = pm0 if h % 2 == 0 else pm1
                    mm_group(p, pm[:, 0:NF],
                             [(knb[base:base + 64,
                                   ch * tw + tc * 128:ch * tw + tc * 128 + 128],
                               pjb[base:base + 64, h * NF:(h + 1) * NF])],
                             reads=[knb, pjb], writes=[pm])
                    p.op("vector", lambda v, h=h, pm=pm: v.tensor_scalar(
                        kpt[:, h * NF:(h + 1) * NF], pm[:, 0:NF], 0.0, 0.1,
                        op0=ALU.max, op1=ALU.add),
                        reads=[pm], writes=[kpt])
                for h in range(NH):
                    pm = pm0 if h % 2 == 0 else pm1
                    mm_group(p, pm[:, 0:(HD + 1)],
                             [(kpt[:, h * NF:h * NF + 128],
                               vtok[:, h * (HD + 1):(h + 1) * (HD + 1)])],
                             reads=[kpt, vtok], writes=[pm])
                    mm_group(p, pm[:, (HD + 1):2 * (HD + 1)],
                             [(kpt[:, h * NF + 128:(h + 1) * NF],
                               vtok[:, h * (HD + 1):(h + 1) * (HD + 1)])],
                             reads=[kpt, vtok], writes=[pm])
                    if i == 0 and tc == 0:
                        p.op("vector", lambda v, h=h, pm=pm: v.tensor_copy(
                            kvs[:, h * 130:(h + 1) * 130], pm[:, 0:130]),
                            reads=[pm], writes=[kvs])
                    else:
                        p.op("vector", lambda v, h=h, pm=pm: v.tensor_tensor(
                            kvs[:, h * 130:(h + 1) * 130],
                            kvs[:, h * 130:(h + 1) * 130],
                            pm[:, 0:130], op=ALU.add),
                            reads=[pm, kvs], writes=[kvs])
        p.op("vector", lambda v: v.tensor_copy(kvb[:, :], kvs[:, :]),
             reads=[kvs], writes=[kvb])

        for j in range(ntq):
            tw = TT
            t0 = j * TT
            p.dma("sync", bf_.xt[:, 0:NCH * tw].rearrange("p (c t) -> p c t", c=NCH),
                  xo[:, t0:t0 + tw].rearrange("(c p) t -> p c t", p=128),
                  writes=[bf_.xt])
            emit_ln(p, cm, bf_, tw)
            proj_1024(bf_.xnb, wqb, bqs, kfm, tw, (pm0, pm1))
            emit_headnorm_pc(p, cm, bf_, kfm, knb, tw, epsq[0:16, 0:1])
            for h in range(NH):
                ch = h // 2
                base = 64 * (h % 2)
                for fc in range(2):
                    pm = pm0 if fc == 0 else pm1
                    mm_group(p, pm[:, 0:tw],
                             [(pjb[base:base + 64, h * NF + fc * 128:h * NF + (fc + 1) * 128],
                               knb[base:base + 64, ch * tw:(ch + 1) * tw])],
                             reads=[pjb, knb], writes=[pm])
                    p.op("vector", lambda v, fc=fc, pm=pm: v.tensor_scalar(
                        qpb[:, fc * tw:(fc + 1) * tw], pm[:, 0:tw], 0.0, 0.1,
                        op0=ALU.max, op1=ALU.add),
                        reads=[pm], writes=[qpb])
                mm_group(p, pq65[0:65, 0:tw],
                         [(kvb[:, h * 130:h * 130 + 65], qpb[:, 0:tw]),
                          (kvb[:, h * 130 + 65:(h + 1) * 130], qpb[:, tw:2 * tw])],
                         reads=[kvb, qpb], writes=[pq65])
                p.op("vector", lambda v: v.tensor_scalar(
                    bf_.r1[0:1, 0:tw], pq65[64:65, 0:tw], 1e-6, None, op0=ALU.max),
                    reads=[pq65], writes=[bf_.r1])
                p.op("vector", lambda v: v.reciprocal(bf_.r1[0:1, 0:tw], bf_.r1[0:1, 0:tw]),
                     reads=[bf_.r1], writes=[bf_.r1])
                p.op("vector", lambda v: v.tensor_copy(bf_.r1b[0:1, 0:tw], bf_.r1[0:1, 0:tw]),
                     reads=[bf_.r1], writes=[bf_.r1b])
                mm_group(p, bf_.pbc[:, 0:tw], [(cm.onesb[0:1, 0:128], bf_.r1b[0:1, 0:tw])],
                         reads=[bf_.r1b, cm.onesb], writes=[bf_.pbc])
                p.op("scalar", lambda s: s.copy(rbs[0:64, 0:tw], bf_.pbc[0:64, 0:tw]),
                     reads=[bf_.pbc], writes=[rbs])
                p.op("vector", lambda v, h=h, ch=ch, base=base: v.tensor_tensor(
                    att[base:base + 64, ch * tw:(ch + 1) * tw],
                    pq65[0:64, 0:tw], rbs[0:64, 0:tw], op=ALU.mult),
                    reads=[pq65, rbs], writes=[att])
            for m in range(NCH):
                pm = pm0 if m % 2 == 0 else pm1
                mm_group(p, pm[:, 0:tw],
                         [(wob[:, k * D + m * 128:k * D + (m + 1) * 128],
                           att[:, k * tw:(k + 1) * tw]) for k in range(NCH)],
                         reads=[wob, att], writes=[pm])
                p.op("vector", lambda v, m=m, pm=pm: v.scalar_tensor_tensor(
                    bf_.xt[:, m * tw:(m + 1) * tw], pm[:, 0:tw],
                    bos[:, m:m + 1], bf_.xt[:, m * tw:(m + 1) * tw],
                    op0=ALU.add, op1=ALU.add),
                    reads=[pm, bos, bf_.xt], writes=[bf_.xt])
            p.dma("sync", xout[:, t0:t0 + tw].rearrange("(c p) t -> p c t", p=128),
                  bf_.xt[:, 0:NCH * tw].rearrange("p (c t) -> p c t", c=NCH),
                  reads=[bf_.xt], writes=[xout])
        p.finish()
        with nc.Block() as block:
            p.emit(block)
    return nc



def _fm(a):
    return np.ascontiguousarray(np.asarray(a).T)


def _cols(b):
    return np.ascontiguousarray(np.asarray(b).reshape(NCH, 128).T)


def run_performer(x_full, w):
    consts = host_constants()
    projT2 = np.asarray(w["projection"]).transpose(0, 2, 1)
    projT2 = projT2.transpose(1, 0, 2).reshape(HD, NH * NF)
    projT2 = np.ascontiguousarray(np.concatenate([projT2, projT2], axis=0),
                                  dtype=np.float32)
    nc = build_performer(T, S)
    in_maps = []
    for c in range(NCORES):
        b = c // 4
        r = c % 4
        m = {
            "xb_fm": _fm(x_full[b]),
            "xo_fm": _fm(x_full[b][r * T:(r + 1) * T]),
            "Wq": np.asarray(w["Wq"]), "Wk": np.asarray(w["Wk"]),
            "Wv": np.asarray(w["Wv"]), "Wo": np.asarray(w["Wo"]),
            "bq": _cols(w["bq"]),
            "bk": _cols(w["bk"]),
            "bv": np.asarray(w["bv"]).reshape(1, D),
            "bo": _cols(w["bo"]),
            "projT2": projT2,
        }
        m.update(consts)
        in_maps.append(m)
    res = run_bass_kernel_spmd(nc, in_maps, core_ids=list(range(NCORES)))
    out = np.empty((B, S, D), dtype=np.float32)
    for c in range(NCORES):
        b, r = c // 4, c % 4
        out[b, r * T:(r + 1) * T] = res.results[c]["xout_fm"].T
    return out



def build_rwkv(n_own):
    nc = bass.Bass("TRN2", target_bir_lowering=False)
    xo = nc.dram_tensor("xo_fm", [D, n_own], F32, kind="ExternalInput").ap()
    xprev = nc.dram_tensor("xprev_cols", [128, NCH], F32, kind="ExternalInput").ap()
    flag = nc.dram_tensor("flag", [128, 1], F32, kind="ExternalInput").ap()
    WrT = nc.dram_tensor("WrT", [D, D], F32, kind="ExternalInput").ap()
    WkT = nc.dram_tensor("WkT", [D, D], F32, kind="ExternalInput").ap()
    WvT = nc.dram_tensor("WvT", [D, D], F32, kind="ExternalInput").ap()
    mur = nc.dram_tensor("mur", [128, NCH], F32, kind="ExternalInput").ap()
    muk = nc.dram_tensor("muk", [128, NCH], F32, kind="ExternalInput").ap()
    cap = declare_consts(nc)
    xout = nc.dram_tensor("xout_fm", [D, n_own], F32, kind="ExternalOutput").ap()
    nt = cdiv(n_own, TT)

    with contextlib.ExitStack() as ctx:
        def sb(name, shape, dt=F32):
            return ctx.enter_context(nc.sbuf_tensor(name, shape, dt))

        def psd(name, shape):
            return ctx.enter_context(nc.psum_tensor(name, shape, F32))

        bf_ = NS()
        bf_.xt = sb("xt", [128, NCH * TT])
        bf_.dt_ = sb("dt", [128, NCH * TT])
        bf_.sqbf = sb("sqbf", [128, NCH * TT], BF16)
        bf_.xnb = sb("xnb", [128, NCH * TT], BF16)
        bf_.r1 = sb("r1", [1, TT])
        bf_.r1b = sb("r1b", [1, TT], BF16)
        xnf = sb("xnf", [128, NCH * TT])
        xsh = sb("xsh", [128, NCH * TT])
        xrb = sb("xrb", [128, NCH * TT], BF16)
        xkb = sb("xkb", [128, NCH * TT], BF16)
        rfm = sb("rfm", [128, NCH * TT])
        kab = sb("kab", [128, NCH * TT], BF16)
        carry = sb("carry", [128, NCH])
        wst = sb("wst", [128, 4 * D])
        wrb = sb("wrb", [128, NCH * D], BF16)
        wkb = sb("wkb", [128, NCH * D], BF16)
        wvb = sb("wvb", [128, NCH * D], BF16)
        murc = sb("murc", [128, NCH])
        mukc = sb("mukc", [128, NCH])
        flagc = sb("flagc", [128, 1])
        pvt = sb("pvt", [128, NCH])

        bf_.pr1 = psd("pr1", [1, TT])
        bf_.pbc = psd("pbc", [128, TT])
        pm0 = psd("pm0", [128, TT])
        pm1 = psd("pm1", [128, TT])

        p = Prog(nc, ctx)
        cm = load_common(p, sb, cap)
        for w_ap, dstb in [(WrT, wrb), (WkT, wkb), (WvT, wvb)]:
            load_w1024(p, w_ap, wst, dstb)
        p.dma("sync", murc[:, :], mur, writes=[murc])
        p.dma("sync", mukc[:, :], muk, writes=[mukc])
        p.dma("sync", flagc[:, :], flag, writes=[flagc])
        p.dma("sync", pvt[:, :], xprev, writes=[pvt])

        p.op("vector", lambda v: v.tensor_copy(bf_.xt[:, 0:NCH], pvt[:, :]),
             reads=[pvt], writes=[bf_.xt])
        emit_ln(p, cm, bf_, 1, dst=carry)
        p.op("vector", lambda v: v.tensor_scalar(carry[:, :], carry[:, :],
                                                 flagc[:, 0:1], None, op0=ALU.mult),
             reads=[carry, flagc], writes=[carry])

        for j in range(nt):
            tw = TT
            t0 = j * TT
            p.dma("sync", bf_.xt[:, 0:NCH * tw].rearrange("p (c t) -> p c t", c=NCH),
                  xo[:, t0:t0 + tw].rearrange("(c p) t -> p c t", p=128),
                  writes=[bf_.xt])
            emit_ln(p, cm, bf_, tw, dst=xnf)
            p.op("vector", lambda v: v.tensor_copy(
                chunks(xsh[:, 0:NCH * tw], NCH, tw)[:, :, 1:tw],
                chunks(xnf[:, 0:NCH * tw], NCH, tw)[:, :, 0:tw - 1]),
                reads=[xnf], writes=[xsh])
            p.op("vector", lambda v: v.tensor_copy(
                chunks(xsh[:, 0:NCH * tw], NCH, tw)[:, :, 0:1],
                carry[:, :].rearrange("p (c o) -> p c o", c=NCH)),
                reads=[carry], writes=[xsh])
            p.op("vector", lambda v: v.tensor_copy(
                carry[:, :].rearrange("p (c o) -> p c o", c=NCH),
                chunks(xnf[:, 0:NCH * tw], NCH, tw)[:, :, tw - 1:tw]),
                reads=[xnf], writes=[carry])
            p.op("vector", lambda v: v.tensor_tensor(bf_.dt_[:, 0:NCH * tw],
                                                     xnf[:, 0:NCH * tw],
                                                     xsh[:, 0:NCH * tw],
                                                     op=ALU.subtract),
                 reads=[xnf, xsh], writes=[bf_.dt_])
            for c in range(NCH):
                p.op("vector", lambda v, c=c: v.scalar_tensor_tensor(
                    xrb[:, c * tw:(c + 1) * tw], bf_.dt_[:, c * tw:(c + 1) * tw],
                    murc[:, c:c + 1], xsh[:, c * tw:(c + 1) * tw],
                    op0=ALU.mult, op1=ALU.add),
                    reads=[bf_.dt_, murc, xsh], writes=[xrb])
                p.op("vector", lambda v, c=c: v.scalar_tensor_tensor(
                    xkb[:, c * tw:(c + 1) * tw], bf_.dt_[:, c * tw:(c + 1) * tw],
                    mukc[:, c:c + 1], xsh[:, c * tw:(c + 1) * tw],
                    op0=ALU.mult, op1=ALU.add),
                    reads=[bf_.dt_, mukc, xsh], writes=[xkb])
            for m in range(NCH):
                pm = pm0 if m % 2 == 0 else pm1
                mm_group(p, pm[:, 0:tw],
                         [(wrb[:, k * D + m * 128:k * D + (m + 1) * 128],
                           xrb[:, k * tw:(k + 1) * tw]) for k in range(NCH)],
                         reads=[wrb, xrb], writes=[pm])
                p.op("scalar", lambda s, m=m, pm=pm: s.activation(
                    rfm[:, m * tw:(m + 1) * tw], pm[:, 0:tw], AF.Sigmoid),
                    reads=[pm], writes=[rfm])
            for m in range(NCH):
                pm = pm0 if m % 2 == 0 else pm1
                mm_group(p, pm[:, 0:tw],
                         [(wkb[:, k * D + m * 128:k * D + (m + 1) * 128],
                           xkb[:, k * tw:(k + 1) * tw]) for k in range(NCH)],
                         reads=[wkb, xkb], writes=[pm])
                p.op("vector", lambda v, m=m, pm=pm: v.tensor_scalar(
                    bf_.dt_[:, m * tw:(m + 1) * tw], pm[:, 0:tw], 0.0, None,
                    op0=ALU.max),
                    reads=[pm], writes=[bf_.dt_])
            p.op("vector", lambda v: v.tensor_tensor(
                kab[:, 0:NCH * tw], bf_.dt_[:, 0:NCH * tw], bf_.dt_[:, 0:NCH * tw],
                op=ALU.mult),
                reads=[bf_.dt_], writes=[kab])
            for m in range(NCH):
                pm = pm0 if m % 2 == 0 else pm1
                mm_group(p, pm[:, 0:tw],
                         [(wvb[:, k * D + m * 128:k * D + (m + 1) * 128],
                           kab[:, k * tw:(k + 1) * tw]) for k in range(NCH)],
                         reads=[wvb, kab], writes=[pm])
                p.op("vector", lambda v, m=m, pm=pm: v.tensor_tensor(
                    xsh[:, m * tw:(m + 1) * tw], pm[:, 0:tw],
                    rfm[:, m * tw:(m + 1) * tw], op=ALU.mult),
                    reads=[pm, rfm], writes=[xsh])
            p.op("vector", lambda v: v.tensor_tensor(
                bf_.xt[:, 0:NCH * tw], bf_.xt[:, 0:NCH * tw], xsh[:, 0:NCH * tw],
                op=ALU.add),
                reads=[xsh, bf_.xt], writes=[bf_.xt])
            p.dma("sync", xout[:, t0:t0 + tw].rearrange("(c p) t -> p c t", p=128),
                  bf_.xt[:, 0:NCH * tw].rearrange("p (c t) -> p c t", c=NCH),
                  reads=[bf_.xt], writes=[xout])
        p.finish()
        with nc.Block() as block:
            p.emit(block)
    return nc


def run_rwkv(x_cur, w):
    consts = host_constants()
    WrT = np.ascontiguousarray(np.asarray(w["W_r"]).T)
    WkT = np.ascontiguousarray(np.asarray(w["W_k"]).T)
    WvT = np.ascontiguousarray(np.asarray(w["W_v"]).T)
    nc = build_rwkv(T)
    in_maps = []
    xf = np.asarray(x_cur).reshape(B * S, D)
    for c in range(NCORES):
        r = c % 4
        start = c * T
        if r == 0:
            xp = np.zeros((128, NCH), dtype=np.float32)
            fl = np.zeros((128, 1), dtype=np.float32)
        else:
            xp = _cols(xf[start - 1])
            fl = np.ones((128, 1), dtype=np.float32)
        m = {
            "xo_fm": _fm(xf[start:start + T]),
            "xprev_cols": xp, "flag": fl,
            "WrT": WrT, "WkT": WkT, "WvT": WvT,
            "mur": _cols(w["mu_r"]), "muk": _cols(w["mu_k"]),
        }
        m.update(consts)
        in_maps.append(m)
    res = run_bass_kernel_spmd(nc, in_maps, core_ids=list(range(NCORES)))
    out = np.empty((B * S, D), dtype=np.float32)
    for c in range(NCORES):
        out[c * T:(c + 1) * T] = res.results[c]["xout_fm"].T
    return out.reshape(B, S, D)



def build_moe(n_own, nexp, topk, act, extra_residual=False, ln_gains=False):
    FH = 4096
    NQ = 4
    HL = 1024
    nhalf = cdiv(n_own, HL)
    nc = bass.Bass("TRN2", target_bir_lowering=False)
    xo = nc.dram_tensor("xo_fm", [D, n_own], F32, kind="ExternalInput").ap()
    if extra_residual:
        xb2 = nc.dram_tensor("xb2_fm", [D, n_own], F32, kind="ExternalInput").ap()
    w1 = nc.dram_tensor("w1", [nexp, D, FH], BF16, kind="ExternalInput").ap()
    w2 = nc.dram_tensor("w2", [nexp, FH, D], BF16, kind="ExternalInput").ap()
    b1c = nc.dram_tensor("b1c", [nexp, 128, FH // 128], F32, kind="ExternalInput").ap()
    if ln_gains:
        g2c_in = nc.dram_tensor("g2c", [128, NCH], F32, kind="ExternalInput").ap()
        be2c_in = nc.dram_tensor("be2c", [128, NCH], F32, kind="ExternalInput").ap()
    if nexp > 1:
        gw = nc.dram_tensor("gw", [D, nexp], F32, kind="ExternalInput").ap()
        gb = nc.dram_tensor("gb", [nexp, 1], F32, kind="ExternalInput").ap()
        b2r = nc.dram_tensor("b2r", [nexp, D], F32, kind="ExternalInput").ap()
        ident_in = nc.dram_tensor("ident_in", [128, 128], F32, kind="ExternalInput").ap()
        esel_in = nc.dram_tensor("esel_in", [nexp, nexp * 128], BF16, kind="ExternalInput").ap()
    else:
        b2cc = nc.dram_tensor("b2cc", [128, NCH], F32, kind="ExternalInput").ap()
    cap = declare_consts(nc)
    xout = nc.dram_tensor("xout_fm", [D, n_own], F32, kind="ExternalOutput").ap()
    actf = {"gelu": AF.Gelu, "silu": AF.Silu, "relu": AF.Relu}[act]

    with contextlib.ExitStack() as ctx:
        def sb(name, shape, dt=F32):
            return ctx.enter_context(nc.sbuf_tensor(name, shape, dt))

        def psd(name, shape):
            return ctx.enter_context(nc.psum_tensor(name, shape, F32))

        bf_ = NS()
        bf_.xt = sb("xt", [128, NCH * HL])
        bf_.dt_ = sb("dt", [128, NCH * TT])
        bf_.sqbf = sb("sqbf", [128, NCH * TT], BF16)
        bf_.r1 = sb("r1", [1, TT])
        bf_.r1b = sb("r1b", [1, TT], BF16)
        xnbh = sb("xnbh", [128, NCH * HL], BF16)
        hq = sb("hq", [128, NCH * HL], BF16)
        osum = sb("osum", [128, NCH * HL])
        wst = sb("wst", [128, 4 * D])
        w1b = sb("w1b", [128, NCH * HL], BF16)
        w2b = sb("w2b", [128, NCH * HL], BF16)
        b1s = sb("b1s", [128, FH // 128])
        if nexp > 1:
            gwb = sb("gwb", [128, NCH * nexp], BF16)
            gbc = sb("gbc", [nexp, 1])
            lgr = sb("lgr", [nexp, TT])
            ew = sb("ew", [128, nexp])
            sel = sb("sel", [128, nexp])
            cols = sb("cols", [128, 4])
            ewT = sb("ewT", [nexp, HL])
            ewTb = sb("ewTb", [nexp, HL], BF16)
            b2b = sb("b2b", [nexp, D], BF16)
            idf = sb("idf", [128, 128])
            esel = sb("esel", [nexp, nexp * 128], BF16)
        else:
            b2s = sb("b2s", [128, NCH])
        if ln_gains:
            g2s = sb("g2s", [128, NCH])
            be2s = sb("be2s", [128, NCH])

        bf_.pr1 = psd("pr1", [1, TT])
        bf_.pbc = psd("pbc", [128, TT])
        pm0 = psd("pm0", [128, TT])
        pm1 = psd("pm1", [128, TT])
        pewb = psd("pewb", [128, HL])
        pl8 = psd("pl8", [max(nexp, 8), TT])

        p = Prog(nc, ctx)

        def wload(w_ap, dst, ncols):
            p.dma("sync", dst[:, 0:NCH * ncols].rearrange("p (c n) -> p c n", c=NCH),
                  w_ap.rearrange("(c p) n -> p c n", p=128), writes=[dst])
        cm = load_common(p, sb, cap)
        if nexp > 1:
            load_w1024(p, gw, wst, gwb, ncols=nexp)
            p.dma("sync", gbc[:, :], gb, writes=[gbc])
            p.dma("sync", wst[0:nexp, 0:D], b2r, writes=[wst])
            p.op("vector", lambda v: v.tensor_copy(b2b[:, :], wst[0:nexp, 0:D]),
                 reads=[wst], writes=[b2b])
            p.dma("sync", idf[:, :], ident_in, writes=[idf])
            p.dma("sync", esel[:, :], esel_in, writes=[esel])
        else:
            p.dma("sync", b2s[:, :], b2cc, writes=[b2s])
        if ln_gains:
            p.dma("sync", g2s[:, :], g2c_in, writes=[g2s])
            p.dma("sync", be2s[:, :], be2c_in, writes=[be2s])

        for h in range(nhalf):
            h0 = h * HL
            xt3 = chunks(bf_.xt[:, 0:NCH * HL], NCH, HL)
            xn3 = chunks(xnbh[:, 0:NCH * HL], NCH, HL)
            p.dma("sync", xt3, xo[:, h0:h0 + HL].rearrange("(c p) t -> p c t", p=128),
                  writes=[bf_.xt])
            for ts in range(HL // TT):
                emit_ln3(p, cm, bf_, TT,
                         xt3[:, :, ts * TT:(ts + 1) * TT], bf_.xt,
                         xn3[:, :, ts * TT:(ts + 1) * TT], xnbh,
                         gain=g2s if ln_gains else None,
                         beta=be2s if ln_gains else None)
                if nexp > 1:
                    mm_group(p, pl8[0:nexp, 0:TT],
                             [(gwb[:, k * nexp:(k + 1) * nexp],
                               xnbh[:, k * HL + ts * TT:k * HL + (ts + 1) * TT])
                              for k in range(NCH)],
                             reads=[gwb, xnbh], writes=[pl8])
                    p.op("vector", lambda v: v.tensor_scalar(
                        lgr[0:nexp, 0:TT], pl8[0:nexp, 0:TT], gbc[0:nexp, 0:1],
                        None, op0=ALU.add),
                        reads=[pl8, gbc], writes=[lgr])
                    for tc in range(TT // 128):
                        ptr = bf_.pbc[:, 0:nexp]
                        p.op("tensor", lambda t, tc=tc, ptr=ptr: t.transpose(
                            ptr, lgr[0:nexp, tc * 128:(tc + 1) * 128],
                            idf[0:nexp, 0:nexp]),
                            reads=[lgr, idf], writes=[bf_.pbc])
                        p.op("vector", lambda v, ptr=ptr: v.reduce_max(
                            cols[:, 0:1], ptr, axis=AX.X),
                            reads=[bf_.pbc], writes=[cols])
                        p.op("vector", lambda v: v.tensor_scalar(
                            cols[:, 1:2], cols[:, 0:1], -1.0, None, op0=ALU.mult),
                            reads=[cols], writes=[cols])
                        p.op("scalar", lambda s, ptr=ptr: s.activation(
                            ew[:, :], ptr, AF.Exp, bias=cols[:, 1:2]),
                            reads=[bf_.pbc, cols], writes=[ew])
                        p.op("vector", lambda v: v.reduce_sum(
                            cols[:, 2:3], ew[:, :], axis=AX.X),
                            reads=[ew], writes=[cols])
                        if topk == 2:
                            p.op("vector", lambda v, ptr=ptr: v.tensor_scalar(
                                sel[:, :], ptr, cols[:, 0:1], None, op0=ALU.is_ge),
                                reads=[bf_.pbc, cols], writes=[sel])
                            p.op("vector", lambda v, ptr=ptr: v.scalar_tensor_tensor(
                                sel[:, :], sel[:, :], -1e9, ptr,
                                op0=ALU.mult, op1=ALU.add),
                                reads=[sel, bf_.pbc], writes=[sel])
                            p.op("vector", lambda v: v.reduce_max(
                                cols[:, 3:4], sel[:, :], axis=AX.X),
                                reads=[sel], writes=[cols])
                            p.op("vector", lambda v, ptr=ptr: v.tensor_scalar(
                                sel[:, :], ptr, cols[:, 3:4], None, op0=ALU.is_ge),
                                reads=[bf_.pbc, cols], writes=[sel])
                            p.op("vector", lambda v: v.tensor_tensor(
                                ew[:, :], ew[:, :], sel[:, :], op=ALU.mult),
                                reads=[sel, ew], writes=[ew])
                            p.op("vector", lambda v: v.reduce_sum(
                                cols[:, 2:3], ew[:, :], axis=AX.X),
                                reads=[ew], writes=[cols])
                            p.op("vector", lambda v: v.reciprocal(
                                cols[:, 2:3], cols[:, 2:3]),
                                reads=[cols], writes=[cols])
                            p.op("vector", lambda v: v.tensor_scalar(
                                ew[:, :], ew[:, :], cols[:, 2:3], None, op0=ALU.mult),
                                reads=[cols, ew], writes=[ew])
                        else:
                            p.op("vector", lambda v, ptr=ptr: v.tensor_scalar(
                                sel[:, :], ptr, cols[:, 0:1], None, op0=ALU.is_ge),
                                reads=[bf_.pbc, cols], writes=[sel])
                            p.op("vector", lambda v: v.reciprocal(
                                cols[:, 2:3], cols[:, 2:3]),
                                reads=[cols], writes=[cols])
                            p.op("vector", lambda v: v.tensor_tensor(
                                ew[:, :], ew[:, :], sel[:, :], op=ALU.mult),
                                reads=[sel, ew], writes=[ew])
                            p.op("vector", lambda v: v.tensor_scalar(
                                ew[:, :], ew[:, :], cols[:, 2:3], None, op0=ALU.mult),
                                reads=[cols, ew], writes=[ew])
                        p.op("tensor", lambda t, tc=tc: t.transpose(
                            pl8[0:128 if False else nexp, 0:128] if False else pl8[0:nexp, 0:128],
                            ew[:, :], idf[:, :]),
                            reads=[ew, idf], writes=[pl8])
                        p.op("vector", lambda v, ts=ts, tc=tc: v.tensor_copy(
                            ewT[0:nexp, ts * TT + tc * 128:ts * TT + (tc + 1) * 128],
                            pl8[0:nexp, 0:128]),
                            reads=[pl8], writes=[ewT])
                    p.op("vector", lambda v, ts=ts: v.tensor_copy(
                        ewTb[0:nexp, ts * TT:(ts + 1) * TT],
                        ewT[0:nexp, ts * TT:(ts + 1) * TT]),
                        reads=[ewT], writes=[ewTb])

            first = [True]
            for e in range(nexp):
                if nexp > 1:
                    for tt in range(2):
                        mm_group(p, pewb[:, tt * TT:(tt + 1) * TT],
                                 [(esel[0:nexp, e * 128:(e + 1) * 128],
                                   ewTb[0:nexp, tt * TT:(tt + 1) * TT])],
                                 reads=[esel, ewTb], writes=[pewb])
                    p.dma("sync", b1s[:, :], b1c[e:e + 1, :, :]
                          .rearrange("o p q -> (o p) q"), writes=[b1s])
                else:
                    p.dma("sync", b1s[:, :], b1c[0:1, :, :]
                          .rearrange("o p q -> (o p) q"), writes=[b1s])
                for q in range(NQ):
                    w1q = w1[e:e + 1, :, q * HL:(q + 1) * HL].rearrange("o r c -> (o r) c")
                    wload(w1q, w1b, HL)
                    for m in range(NCH):
                        hidx = q * 8 + m
                        for tt in range(2):
                            pm = pm0 if (m * 2 + tt) % 2 == 0 else pm1
                            mm_group(p, pm[:, 0:TT],
                                     [(w1b[:, k * HL + m * 128:k * HL + (m + 1) * 128],
                                       xnbh[:, k * HL + tt * TT:k * HL + (tt + 1) * TT])
                                      for k in range(NCH)],
                                     reads=[w1b, xnbh], writes=[pm])
                            p.op("scalar", lambda s, m=m, tt=tt, pm=pm, hidx=hidx:
                                 s.activation(hq[:, m * HL + tt * TT:m * HL + (tt + 1) * TT],
                                              pm[:, 0:TT], actf,
                                              bias=b1s[:, hidx:hidx + 1]),
                                 reads=[pm, b1s], writes=[hq])
                    if nexp > 1:
                        for tt in range(2):
                            p.op("vector", lambda v, tt=tt: v.tensor_tensor(
                                chunks(hq[:, 0:NCH * HL], NCH, HL)[:, :, tt * TT:(tt + 1) * TT],
                                chunks(hq[:, 0:NCH * HL], NCH, HL)[:, :, tt * TT:(tt + 1) * TT],
                                bcast(pewb[:, tt * TT:(tt + 1) * TT], NCH, TT),
                                op=ALU.mult),
                                reads=[hq, pewb], writes=[hq])
                    w2q = w2[e:e + 1, q * HL:(q + 1) * HL, :].rearrange("o r c -> (o r) c")
                    wload(w2q, w2b, D)
                    for m in range(NCH):
                        for tt in range(2):
                            pm = pm0 if (m * 2 + tt) % 2 == 0 else pm1
                            mm_group(p, pm[:, 0:TT],
                                     [(w2b[:, k * D + m * 128:k * D + (m + 1) * 128],
                                       hq[:, k * HL + tt * TT:k * HL + (tt + 1) * TT])
                                      for k in range(NCH)],
                                     reads=[w2b, hq], writes=[pm])
                            oap = osum[:, m * HL + tt * TT:m * HL + (tt + 1) * TT]
                            if first[0]:
                                p.op("vector", lambda v, oap=oap, pm=pm: v.tensor_copy(
                                    oap, pm[:, 0:TT]), reads=[pm], writes=[osum])
                            else:
                                p.op("vector", lambda v, oap=oap, pm=pm: v.tensor_tensor(
                                    oap, oap, pm[:, 0:TT], op=ALU.add),
                                    reads=[pm, osum], writes=[osum])
                    first[0] = False
            if nexp > 1:
                for m in range(NCH):
                    for tt in range(2):
                        pm = pm0 if (m * 2 + tt) % 2 == 0 else pm1
                        mm_group(p, pm[:, 0:TT],
                                 [(b2b[0:nexp, m * 128:(m + 1) * 128],
                                   ewTb[0:nexp, tt * TT:(tt + 1) * TT])],
                                 reads=[b2b, ewTb], writes=[pm])
                        oap = osum[:, m * HL + tt * TT:m * HL + (tt + 1) * TT]
                        p.op("vector", lambda v, oap=oap, pm=pm: v.tensor_tensor(
                            oap, oap, pm[:, 0:TT], op=ALU.add),
                            reads=[pm, osum], writes=[osum])
                p.op("vector", lambda v: v.tensor_tensor(
                    bf_.xt[:, 0:NCH * HL], bf_.xt[:, 0:NCH * HL],
                    osum[:, 0:NCH * HL], op=ALU.add),
                    reads=[osum, bf_.xt], writes=[bf_.xt])
            else:
                for c in range(NCH):
                    p.op("vector", lambda v, c=c: v.scalar_tensor_tensor(
                        bf_.xt[:, c * HL:(c + 1) * HL], osum[:, c * HL:(c + 1) * HL],
                        b2s[:, c:c + 1], bf_.xt[:, c * HL:(c + 1) * HL],
                        op0=ALU.add, op1=ALU.add),
                        reads=[osum, b2s, bf_.xt], writes=[bf_.xt])
                if extra_residual:
                    p.dma("sync", chunks(osum[:, 0:NCH * HL], NCH, HL),
                          xb2[:, h0:h0 + HL].rearrange("(c p) t -> p c t", p=128),
                          writes=[osum])
                    p.op("vector", lambda v: v.tensor_tensor(
                        bf_.xt[:, 0:NCH * HL], bf_.xt[:, 0:NCH * HL],
                        osum[:, 0:NCH * HL], op=ALU.add),
                        reads=[osum, bf_.xt], writes=[bf_.xt])
            p.dma("sync", xout[:, h0:h0 + HL].rearrange("(c p) t -> p c t", p=128),
                  chunks(bf_.xt[:, 0:NCH * HL], NCH, HL),
                  reads=[bf_.xt], writes=[xout])
        p.finish()
        with nc.Block() as block:
            p.emit(block)
    return nc


def moe_host_consts(nexp):
    ident = np.eye(128, dtype=np.float32)
    esel = np.zeros((nexp, nexp * 128), dtype=BF)
    for e in range(nexp):
        esel[e, e * 128:(e + 1) * 128] = 1
    return ident, esel


def run_moe(x_cur, w, topk, act, names=("gate_w", "gate_b", "w1", "b1", "w2", "b2")):
    gwn, gbn, w1n, b1n, w2n, b2n = names
    nexp = np.asarray(w[w1n]).shape[0]
    consts = host_constants()
    ident, esel = moe_host_consts(nexp)
    b1cols = np.stack([np.asarray(w[b1n][e]).reshape(32, 128).T for e in range(nexp)])
    nc = build_moe(T, nexp, topk, act)
    in_maps = []
    xf = np.asarray(x_cur).reshape(B * S, D)
    for c in range(NCORES):
        m = {
            "xo_fm": _fm(xf[c * T:(c + 1) * T]),
            "w1": np.asarray(w[w1n]), "w2": np.asarray(w[w2n]),
            "b1c": np.ascontiguousarray(b1cols),
            "gw": np.asarray(w[gwn]), "gb": np.asarray(w[gbn]).reshape(nexp, 1),
            "b2r": np.asarray(w[b2n]),
            "ident_in": ident, "esel_in": esel,
        }
        m.update(consts)
        in_maps.append(m)
    res = run_bass_kernel_spmd(nc, in_maps, core_ids=list(range(NCORES)))
    out = np.empty((B * S, D), dtype=np.float32)
    for c in range(NCORES):
        out[c * T:(c + 1) * T] = res.results[c]["xout_fm"].T
    return out.reshape(B, S, D)



def rope_perm():
    perm = np.zeros(D, dtype=np.int64)
    for c in range(NCH):
        base = 128 * c
        cols = ([base + 2 * i for i in range(32)]
                + [base + 64 + 2 * i for i in range(32)]
                + [base + 1 + 2 * i for i in range(32)]
                + [base + 64 + 1 + 2 * i for i in range(32)])
        perm[base:base + 128] = cols
    return perm


def rope_tables():
    inv = 1.0 / (10000.0 ** (np.arange(0, HD, 2, dtype=np.float64) / HD))
    ang = np.arange(S, dtype=np.float64)[:, None] * inv[None, :]
    cos = np.cos(ang).T.astype(np.float32)
    sin = np.sin(ang).T.astype(np.float32)
    return (np.ascontiguousarray(np.concatenate([cos, cos, cos, cos], axis=0)),
            np.ascontiguousarray(np.concatenate([sin, sin, sin, sin], axis=0)))


def hselp_consts():
    sel = np.zeros((128, 8 * 16), dtype=BF)
    for ch in range(8):
        for pp in range(128):
            j = 2 * ch + (pp // 32) % 2
            sel[pp, ch * 16 + j] = 1
    return sel


def emit_rope(p, kfm, robuf, cost, sint, sA, sB, tw):
    for c in range(NCH):
        ev = kfm[0:64, c * tw:(c + 1) * tw]
        od = kfm[64:128, c * tw:(c + 1) * tw]
        a0 = sA[0:64, 0:tw]
        b0 = sB[0:64, 0:tw]
        a1 = sA[64:128, 0:tw]
        b1 = sB[64:128, 0:tw]
        oe = robuf[0:64, c * tw:(c + 1) * tw]
        oo = robuf[64:128, c * tw:(c + 1) * tw]
        p.op("vector", lambda v, ev=ev, a0=a0: v.tensor_tensor(
            a0, ev, cost[0:64, 0:tw], op=ALU.mult), reads=[kfm, cost], writes=[sA])
        p.op("vector", lambda v, od=od, b0=b0: v.tensor_tensor(
            b0, od, sint[64:128, 0:tw], op=ALU.mult), reads=[kfm, sint], writes=[sB])
        p.op("vector", lambda v, a0=a0, b0=b0, oe=oe: v.tensor_tensor(
            oe, a0, b0, op=ALU.subtract), reads=[sA, sB], writes=[robuf])
        p.op("vector", lambda v, ev=ev, a1=a1: v.tensor_tensor(
            a1, ev, sint[0:64, 0:tw], op=ALU.mult), reads=[kfm, sint], writes=[sA])
        p.op("vector", lambda v, od=od, b1=b1: v.tensor_tensor(
            b1, od, cost[64:128, 0:tw], op=ALU.mult), reads=[kfm, cost], writes=[sB])
        p.op("vector", lambda v, a1=a1, b1=b1, oo=oo: v.tensor_tensor(
            oo, a1, b1, op=ALU.add), reads=[sA, sB], writes=[robuf])


def build_l1a_prep(n_own):
    nc = bass.Bass("TRN2", target_bir_lowering=False)
    xo = nc.dram_tensor("xo_fm", [D, n_own], F32, kind="ExternalInput").ap()
    WkP = nc.dram_tensor("WkP", [D, D], F32, kind="ExternalInput").ap()
    Wv = nc.dram_tensor("Wv", [D, D], F32, kind="ExternalInput").ap()
    bkP = nc.dram_tensor("bkP", [128, NCH], F32, kind="ExternalInput").ap()
    bv = nc.dram_tensor("bv", [1, D], F32, kind="ExternalInput").ap()
    g1c = nc.dram_tensor("g1c", [128, NCH], F32, kind="ExternalInput").ap()
    be1c = nc.dram_tensor("be1c", [128, NCH], F32, kind="ExternalInput").ap()
    cosd = nc.dram_tensor("cosd", [128, n_own], F32, kind="ExternalInput").ap()
    sind = nc.dram_tensor("sind", [128, n_own], F32, kind="ExternalInput").ap()
    hselp_in = nc.dram_tensor("hselp_in", [128, 8 * 16], BF16, kind="ExternalInput").ap()
    cap = declare_consts(nc)
    kout = nc.dram_tensor("kout", [D, n_own], BF16, kind="ExternalOutput").ap()
    vout = nc.dram_tensor("vout", [n_own, D], BF16, kind="ExternalOutput").ap()
    mkout = nc.dram_tensor("mkout", [16, 1], F32, kind="ExternalOutput").ap()
    nt = cdiv(n_own, TT)

    with contextlib.ExitStack() as ctx:
        def sb(name, shape, dt=F32):
            return ctx.enter_context(nc.sbuf_tensor(name, shape, dt))

        def psd(name, shape):
            return ctx.enter_context(nc.psum_tensor(name, shape, F32))

        bf_ = NS()
        bf_.xt = sb("xt", [128, NCH * TT])
        bf_.dt_ = sb("dt", [128, NCH * TT])
        bf_.sqbf = sb("sqbf", [128, NCH * TT], BF16)
        bf_.xnb = sb("xnb", [128, NCH * TT], BF16)
        bf_.r1 = sb("r1", [1, TT])
        bf_.r1b = sb("r1b", [1, TT], BF16)
        kfm = sb("kfm", [128, NCH * TT])
        krb = sb("krb", [128, NCH * TT], BF16)
        sA = sb("sA", [128, TT])
        sB = sb("sB", [128, TT])
        vtok = sb("vtok", [128, D], BF16)
        rows = sb("rows", [16, TT])
        mkc = sb("mkc", [16, 1])
        cost = sb("cost", [128, TT])
        sint = sb("sint", [128, TT])
        wst = sb("wst", [128, 4 * D])
        wkb = sb("wkb", [128, NCH * D], BF16)
        wvb = sb("wvb", [128, NCH * D], BF16)
        bks = sb("bks", [128, NCH])
        g1s = sb("g1s", [128, NCH])
        be1s = sb("be1s", [128, NCH])
        bvb = sb("bvb", [1, D], BF16)
        hselp = sb("hselp", [128, 8 * 16], BF16)

        bf_.pr1 = psd("pr1", [1, TT])
        bf_.pbc = psd("pbc", [128, TT])
        pm0 = psd("pm0", [128, TT])
        pm1 = psd("pm1", [128, TT])
        pv = psd("pv", [128, D])
        prow = psd("prow", [16, TT])

        p = Prog(nc, ctx)
        cm = load_common(p, sb, cap)
        load_w1024(p, WkP, wst, wkb)
        load_w1024(p, Wv, wst, wvb)
        p.dma("sync", bks[:, :], bkP, writes=[bks])
        p.dma("sync", g1s[:, :], g1c, writes=[g1s])
        p.dma("sync", be1s[:, :], be1c, writes=[be1s])
        p.dma("sync", wst[0:1, 0:D], bv, writes=[wst])
        p.op("vector", lambda v: v.tensor_copy(bvb[:, :], wst[0:1, 0:D]),
             reads=[wst], writes=[bvb])
        p.dma("sync", hselp[:, :], hselp_in, writes=[hselp])
        p.op("vector", lambda v: v.memset(mkc[:, :], 0.0), writes=[mkc])

        for j in range(nt):
            tw = TT
            t0 = j * TT
            p.dma("sync", chunks(bf_.xt[:, 0:NCH * tw], NCH, tw),
                  xo[:, t0:t0 + tw].rearrange("(c p) t -> p c t", p=128),
                  writes=[bf_.xt])
            p.dma("sync", cost[:, 0:tw], cosd[:, t0:t0 + tw], writes=[cost])
            p.dma("sync", sint[:, 0:tw], sind[:, t0:t0 + tw], writes=[sint])
            emit_ln(p, cm, bf_, tw, gain=g1s, beta=be1s)
            for m in range(NCH):
                pm = pm0 if m % 2 == 0 else pm1
                mm_group(p, pm[:, 0:tw],
                         [(wkb[:, k * D + m * 128:k * D + (m + 1) * 128],
                           bf_.xnb[:, k * tw:(k + 1) * tw]) for k in range(NCH)],
                         reads=[wkb, bf_.xnb], writes=[pm])
                p.op("vector", lambda v, m=m, pm=pm: v.tensor_scalar(
                    kfm[:, m * tw:(m + 1) * tw], pm[:, 0:tw],
                    bks[:, m:m + 1], None, op0=ALU.add),
                    reads=[pm, bks], writes=[kfm])
            emit_rope(p, kfm, krb, cost, sint, sA, sB, tw)
            p.dma("sync", kout[:, t0:t0 + tw].rearrange("(c p) t -> p c t", p=128),
                  chunks(krb[:, 0:NCH * tw], NCH, tw), reads=[krb], writes=[kout])
            p.op("scalar", lambda s: s.activation(bf_.sqbf[:, 0:NCH * tw],
                                                  krb[:, 0:NCH * tw], AF.Square),
                 reads=[krb], writes=[bf_.sqbf])
            mm_group(p, prow[0:16, 0:tw],
                     [(hselp[:, ch * 16:(ch + 1) * 16], bf_.sqbf[:, ch * tw:(ch + 1) * tw])
                      for ch in range(NCH)],
                     reads=[bf_.sqbf, hselp], writes=[prow])
            p.op("scalar", lambda s: s.activation(rows[0:16, 0:tw], prow[0:16, 0:tw], AF.Ln),
                 reads=[prow], writes=[rows])
            p.op("scalar", lambda s: s.activation(rows[0:16, 0:tw], rows[0:16, 0:tw],
                                                  AF.Exp, scale=0.5),
                 reads=[rows], writes=[rows])
            p.op("vector", lambda v: v.reduce_max(rows[0:16, 0:1], rows[0:16, 0:tw],
                                                  axis=AX.X),
                 reads=[rows], writes=[rows])
            p.op("vector", lambda v: v.tensor_tensor(mkc[0:16, 0:1], mkc[0:16, 0:1],
                                                     rows[0:16, 0:1], op=ALU.max),
                 reads=[rows, mkc], writes=[mkc])
            for tc in range(tw // 128):
                for half in range(2):
                    mm_group(p, pv[:, half * 512:(half + 1) * 512],
                             [(bf_.xnb[:, k * tw + tc * 128:k * tw + tc * 128 + 128],
                               wvb[:, k * D + half * 512:k * D + (half + 1) * 512])
                              for k in range(NCH)]
                             + [(cm.onesb[0:1, 0:128], bvb[0:1, half * 512:(half + 1) * 512])],
                             reads=[bf_.xnb, wvb, cm.onesb, bvb], writes=[pv])
                p.op("vector", lambda v: v.tensor_copy(vtok[:, :], pv[:, 0:D]),
                     reads=[pv], writes=[vtok])
                p.dma("sync", vout[t0 + tc * 128:t0 + (tc + 1) * 128, :], vtok[:, :],
                      reads=[vtok], writes=[vout])
        p.dma("sync", mkout, mkc[0:16, 0:1], reads=[mkc], writes=[mkout])
        p.finish()
        with nc.Block() as block:
            p.emit(block)
    return nc


def build_l1a_attn(n_own, n_seq):
    nc = bass.Bass("TRN2", target_bir_lowering=False)
    xo = nc.dram_tensor("xo_fm", [D, n_own], F32, kind="ExternalInput").ap()
    kfull = nc.dram_tensor("kfull", [D, n_seq], BF16, kind="ExternalInput").ap()
    vgrp = nc.dram_tensor("vgrp", [NH, n_seq, HD], BF16, kind="ExternalInput").ap()
    mkin = nc.dram_tensor("mkin", [16, 4], F32, kind="ExternalInput").ap()
    WqP = nc.dram_tensor("WqP", [D, D], F32, kind="ExternalInput").ap()
    Wo = nc.dram_tensor("Wo", [D, D], F32, kind="ExternalInput").ap()
    bqP = nc.dram_tensor("bqP", [128, NCH], F32, kind="ExternalInput").ap()
    bo = nc.dram_tensor("bo", [128, NCH], F32, kind="ExternalInput").ap()
    g1c = nc.dram_tensor("g1c", [128, NCH], F32, kind="ExternalInput").ap()
    be1c = nc.dram_tensor("be1c", [128, NCH], F32, kind="ExternalInput").ap()
    cosd = nc.dram_tensor("cosd", [128, n_own], F32, kind="ExternalInput").ap()
    sind = nc.dram_tensor("sind", [128, n_own], F32, kind="ExternalInput").ap()
    sel65_in = nc.dram_tensor("sel65_in", [128, NH * 65], BF16, kind="ExternalInput").ap()
    ident_in = nc.dram_tensor("ident_in", [128, 128], F32, kind="ExternalInput").ap()
    cap = declare_consts(nc)
    xout = nc.dram_tensor("xout_fm", [D, n_own], F32, kind="ExternalOutput").ap()
    nt = cdiv(n_own, TT)
    nkc = n_seq // 128

    with contextlib.ExitStack() as ctx:
        def sb(name, shape, dt=F32):
            return ctx.enter_context(nc.sbuf_tensor(name, shape, dt))

        def psd(name, shape, dt=F32):
            return ctx.enter_context(nc.psum_tensor(name, shape, dt))

        bf_ = NS()
        bf_.xt = sb("xt", [128, NCH * TT])
        bf_.dt_ = sb("dt", [128, NCH * TT])
        bf_.sqbf = sb("sqbf", [128, NCH * TT], BF16)
        bf_.xnb = sb("xnb", [128, NCH * TT], BF16)
        bf_.r1 = sb("r1", [1, TT])
        bf_.r1b = sb("r1b", [1, TT], BF16)
        zt = sb("zt", [128, NCH * TT])
        kfm = bf_.dt_
        krb = sb("krb", [128, NCH * TT])
        sA = sb("sA", [128, TT])
        sB = sb("sB", [128, TT])
        qodd = sb("qodd", [64, TT])
        qaug = sb("qaug", [65, NH * TT], BF16)
        kaug = sb("kaug", [65, 0 + n_seq], BF16)
        vh = sb("vh", [128, (n_seq // 128) * 65], BF16)
        ptA = sb("ptA", [128, TT], BF16)
        ptB = sb("ptB", [128, TT], BF16)
        att = sb("att", [128, NCH * TT], BF16)
        rbs = sb("rbs", [64, TT])
        cost = sb("cost", [128, TT])
        sint = sb("sint", [128, TT])
        wst = sb("wst", [128, 4 * D])
        wqb = sb("wqb", [128, NCH * D], BF16)
        wob = sb("wob", [128, NCH * D], BF16)
        bqs = sb("bqs", [128, NCH])
        bos = sb("bos", [128, NCH])
        g1s = sb("g1s", [128, NCH])
        be1s = sb("be1s", [128, NCH])
        sel65 = sb("sel65", [128, NH * 65], BF16)
        mkrow = sb("mkrow", [1, 16])
        mks = sb("mks", [16, 4])
        idf = sb("idf", [128, 128])

        bf_.pr1 = psd("pr1", [1, TT])
        bf_.pbc = psd("pbc", [128, TT])
        pm0 = psd("pm0", [128, TT])
        pm1 = psd("pm1", [128, TT])
        pv65 = psd("pv65", [65, TT])

        p = Prog(nc, ctx)
        cm = load_common(p, sb, cap)
        load_w1024(p, WqP, wst, wqb)
        load_w1024(p, Wo, wst, wob)
        p.dma("sync", bqs[:, :], bqP, writes=[bqs])
        p.dma("sync", bos[:, :], bo, writes=[bos])
        p.dma("sync", g1s[:, :], g1c, writes=[g1s])
        p.dma("sync", be1s[:, :], be1c, writes=[be1s])
        p.dma("sync", sel65[:, :], sel65_in, writes=[sel65])
        p.dma("sync", idf[:, :], ident_in, writes=[idf])
        p.dma("sync", mks[:, :], mkin, writes=[mks])
        p.op("vector", lambda v: v.tensor_reduce(mks[0:16, 0:1], mks[0:16, 0:4],
                                                 axis=AX.X, op=ALU.max),
             reads=[mks], writes=[mks])
        p.op("tensor", lambda t: t.transpose(bf_.pbc[0:1, 0:16], mks[0:16, 0:1],
                                             idf[0:16, 0:16]),
             reads=[mks, idf], writes=[bf_.pbc])
        p.op("vector", lambda v: v.tensor_copy(mkrow[0:1, 0:16], bf_.pbc[0:1, 0:16]),
             reads=[bf_.pbc], writes=[mkrow])
        p.op("vector", lambda v: v.memset(kaug[64:65, :], 1.0), writes=[kaug])
        p.op("vector", lambda v: v.memset(vh[:, :], 1.0), writes=[vh])


        for j in range(nt):
            tw = TT
            t0 = j * TT
            p.dma("sync", chunks(bf_.xt[:, 0:NCH * tw], NCH, tw),
                  xo[:, t0:t0 + tw].rearrange("(c p) t -> p c t", p=128),
                  writes=[bf_.xt])
            p.dma("sync", cost[:, 0:tw], cosd[:, t0:t0 + tw], writes=[cost])
            p.dma("sync", sint[:, 0:tw], sind[:, t0:t0 + tw], writes=[sint])
            emit_ln(p, cm, bf_, tw, dst=zt)
            emit_ln(p, cm, bf_, tw, src=zt, gain=g1s, beta=be1s)
            for m in range(NCH):
                pm = pm0 if m % 2 == 0 else pm1
                mm_group(p, pm[:, 0:tw],
                         [(wqb[:, k * D + m * 128:k * D + (m + 1) * 128],
                           bf_.xnb[:, k * tw:(k + 1) * tw]) for k in range(NCH)],
                         reads=[wqb, bf_.xnb], writes=[pm])
                p.op("vector", lambda v, m=m, pm=pm: v.tensor_scalar(
                    kfm[:, m * tw:(m + 1) * tw], pm[:, 0:tw],
                    bqs[:, m:m + 1], None, op0=ALU.add),
                    reads=[pm, bqs], writes=[kfm])
            emit_rope(p, kfm, krb, cost, sint, sA, sB, tw)
            p.op("scalar", lambda s: s.activation(bf_.sqbf[:, 0:NCH * tw],
                                                  krb[:, 0:NCH * tw], AF.Square),
                 reads=[krb], writes=[bf_.sqbf])
            for h in range(NH):
                ch, par = h // 2, h % 2
                if par == 0:
                    p.op("vector", lambda v, ch=ch: v.tensor_copy(
                        qodd[0:64, 0:tw], krb[64:128, ch * tw:(ch + 1) * tw]),
                        reads=[krb], writes=[qodd])
                p.op("vector", lambda v, h=h, ch=ch, par=par: v.tensor_scalar(
                    qaug[0:32, h * tw:(h + 1) * tw],
                    krb[32 * par:32 * par + 32, ch * tw:(ch + 1) * tw],
                    0.125, None, op0=ALU.mult),
                    reads=[krb], writes=[qaug])
                p.op("vector", lambda v, h=h, par=par: v.tensor_scalar(
                    qaug[32:64, h * tw:(h + 1) * tw],
                    qodd[32 * par:32 * par + 32, 0:tw],
                    0.125, None, op0=ALU.mult),
                    reads=[qodd], writes=[qaug])
                mm_group(p, pv65[0:65, 0:tw],
                         [(sel65[:, h * 65:(h + 1) * 65],
                           bf_.sqbf[:, ch * tw:(ch + 1) * tw])],
                         reads=[sel65, bf_.sqbf], writes=[pv65])
                p.op("scalar", lambda s: s.activation(rbs[0:1, 0:tw],
                                                      pv65[64:65, 0:tw], AF.Ln),
                     reads=[pv65], writes=[rbs])
                p.op("scalar", lambda s: s.activation(rbs[0:1, 0:tw], rbs[0:1, 0:tw],
                                                      AF.Exp, scale=0.5),
                     reads=[rbs], writes=[rbs])
                p.op("vector", lambda v, h=h: v.tensor_scalar(
                    qaug[64:65, h * tw:(h + 1) * tw], rbs[0:1, 0:tw],
                    mkrow[0:1, h:h + 1], -0.125, op0=ALU.mult, op1=ALU.mult),
                    reads=[rbs, mkrow], writes=[qaug])
            for h in range(NH):
                ch, par = h // 2, h % 2
                p.dma("sync", kaug[0:32, :],
                      kfull[ch * 128 + 32 * par:ch * 128 + 32 * par + 32, :],
                      writes=[kaug], slot="kaug")
                p.dma("sync", kaug[32:64, :],
                      kfull[ch * 128 + 64 + 32 * par:ch * 128 + 96 + 32 * par, :],
                      writes=[kaug], slot="kaug")
                for qsp in range(4):
                    nq = nkc // 4
                    p.dma("sync",
                          vh[:, :].rearrange("p (kc g) -> p kc g", g=65)[
                              :, qsp * nq:(qsp + 1) * nq, 0:64],
                          vgrp[h, :, :].rearrange("(kc p) g -> p kc g", p=128)[
                              :, qsp * nq:(qsp + 1) * nq, :],
                          writes=[vh], slot="vh")

                def emit_qk(kc, h=h):
                    pX = pm0 if kc % 2 == 0 else pm1
                    def mmsc(t, kc=kc, pX=pX, h=h):
                        return t.matmul(pX[:, 0:tw],
                                        kaug[0:65, kc * 128:(kc + 1) * 128],
                                        qaug[0:65, h * tw:(h + 1) * tw],
                                        start=True, stop=True)
                    p.op("tensor", mmsc, reads=[kaug, qaug], writes=[pX])

                def emit_exp(kc):
                    pX = pm0 if kc % 2 == 0 else pm1
                    ptX = ptA if kc % 2 == 0 else ptB
                    p.op("scalar", lambda s, pX=pX, ptX=ptX: s.activation(
                        ptX[:, 0:tw], pX[:, 0:tw], AF.Exp),
                        reads=[pX], writes=[ptX])

                def emit_av(kc):
                    ptX = ptA if kc % 2 == 0 else ptB
                    first = kc == 0
                    last = kc == nkc - 1
                    def mmpv(t, kc=kc, ptX=ptX, first=first, last=last):
                        return t.matmul(pv65[0:65, 0:tw],
                                        vh[:, kc * 65:(kc + 1) * 65],
                                        ptX[:, 0:tw],
                                        start=first, stop=last)
                    p.op("tensor", mmpv, reads=[vh, ptX], writes=[pv65])

                emit_qk(0)
                for kc in range(nkc):
                    if kc + 1 < nkc:
                        emit_qk(kc + 1)
                    emit_exp(kc)
                    emit_av(kc)
                p.op("vector", lambda v: v.reciprocal(bf_.r1[0:1, 0:tw],
                                                      pv65[64:65, 0:tw]),
                     reads=[pv65], writes=[bf_.r1])
                p.op("vector", lambda v: v.tensor_copy(bf_.r1b[0:1, 0:tw],
                                                       bf_.r1[0:1, 0:tw]),
                     reads=[bf_.r1], writes=[bf_.r1b])
                mm_group(p, bf_.pbc[0:64, 0:tw], [(cm.onesb[0:1, 0:64], bf_.r1b[0:1, 0:tw])],
                         reads=[bf_.r1b, cm.onesb], writes=[bf_.pbc])
                p.op("scalar", lambda s: s.copy(rbs[0:64, 0:tw], bf_.pbc[0:64, 0:tw]),
                     reads=[bf_.pbc], writes=[rbs])
                p.op("vector", lambda v, h=h, ch=ch, par=par: v.tensor_tensor(
                    att[64 * par:64 * par + 64, ch * tw:(ch + 1) * tw],
                    pv65[0:64, 0:tw], rbs[0:64, 0:tw], op=ALU.mult),
                    reads=[pv65, rbs], writes=[att])
            for m in range(NCH):
                pm = pm0 if m % 2 == 0 else pm1
                mm_group(p, pm[:, 0:tw],
                         [(wob[:, k * D + m * 128:k * D + (m + 1) * 128],
                           att[:, k * tw:(k + 1) * tw]) for k in range(NCH)],
                         reads=[wob, att], writes=[pm])
                p.op("vector", lambda v, m=m, pm=pm: v.scalar_tensor_tensor(
                    zt[:, m * tw:(m + 1) * tw], pm[:, 0:tw],
                    bos[:, m:m + 1], zt[:, m * tw:(m + 1) * tw],
                    op0=ALU.add, op1=ALU.add),
                    reads=[pm, bos, zt], writes=[zt])
            p.dma("sync", xout[:, t0:t0 + tw].rearrange("(c p) t -> p c t", p=128),
                  chunks(zt[:, 0:NCH * tw], NCH, tw), reads=[zt], writes=[xout])
        p.finish()
        with nc.Block() as block:
            p.emit(block)
    return nc


def sel65_consts():
    sel = np.zeros((128, NH * 65), dtype=BF)
    for h in range(NH):
        par = h % 2
        for blk in (32 * par, 64 + 32 * par):
            sel[blk:blk + 32, h * 65 + 64] = 1
    return sel


def run_l1a(x_cur, w):
    consts = host_constants()
    perm = rope_perm()
    cos64, sin64 = rope_tables()
    WqP = np.ascontiguousarray(np.asarray(w["Wq"])[:, perm])
    WkP = np.ascontiguousarray(np.asarray(w["Wk"])[:, perm])
    bqP = _cols(np.asarray(w["bq"])[perm])
    bkP = _cols(np.asarray(w["bk"])[perm])
    xf = np.asarray(x_cur).reshape(B * S, D)

    nc1 = build_l1a_prep(T)
    in_maps = []
    for c in range(NCORES):
        t0g = c * T
        pos0 = (c % 4) * T
        m = {
            "xo_fm": _fm(xf[t0g:t0g + T]),
            "WkP": WkP, "Wv": np.asarray(w["Wv"]), "bkP": bkP,
            "bv": np.asarray(w["bv"]).reshape(1, D),
            "g1c": _cols(w["g1"]), "be1c": _cols(w["beta1"]),
            "cosd": np.ascontiguousarray(cos64[:, pos0:pos0 + T]),
            "sind": np.ascontiguousarray(sin64[:, pos0:pos0 + T]),
            "hselp_in": hselp_consts(),
        }
        m.update(consts)
        in_maps.append(m)
    res1 = run_bass_kernel_spmd(nc1, in_maps, core_ids=list(range(NCORES)))
    kparts = [res1.results[c]["kout"] for c in range(NCORES)]
    vparts = [res1.results[c]["vout"] for c in range(NCORES)]
    mkparts = [res1.results[c]["mkout"] for c in range(NCORES)]
    kfulls = [np.concatenate(kparts[b * 4:(b + 1) * 4], axis=1) for b in range(B)]
    vfulls = [np.ascontiguousarray(
        np.concatenate(vparts[b * 4:(b + 1) * 4], axis=0)
        .reshape(S, NH, HD).transpose(1, 0, 2)) for b in range(B)]
    mkfulls = [np.concatenate(mkparts[b * 4:(b + 1) * 4], axis=1) for b in range(B)]

    nc2 = build_l1a_attn(T, S)
    in_maps = []
    for c in range(NCORES):
        b = c // 4
        t0g = c * T
        pos0 = (c % 4) * T
        m = {
            "xo_fm": _fm(xf[t0g:t0g + T]),
            "kfull": kfulls[b], "vgrp": vfulls[b], "mkin": mkfulls[b],
            "WqP": WqP, "Wo": np.asarray(w["Wo"]),
            "bqP": bqP, "bo": _cols(w["bo"]),
            "g1c": _cols(w["g1"]), "be1c": _cols(w["beta1"]),
            "cosd": np.ascontiguousarray(cos64[:, pos0:pos0 + T]),
            "sind": np.ascontiguousarray(sin64[:, pos0:pos0 + T]),
            "sel65_in": sel65_consts(),
            "ident_in": np.eye(128, dtype=np.float32),
        }
        m.update(consts)
        in_maps.append(m)
    res2 = run_bass_kernel_spmd(nc2, in_maps, core_ids=list(range(NCORES)))
    z2 = np.empty((B * S, D), dtype=np.float32)
    for c in range(NCORES):
        z2[c * T:(c + 1) * T] = res2.results[c]["xout_fm"].T

    consts2 = host_constants()
    nc3 = build_moe(T, 1, 1, "relu", extra_residual=True, ln_gains=True)
    b1cols = np.asarray(w["b1"]).reshape(32, 128).T[None]
    in_maps = []
    for c in range(NCORES):
        t0g = c * T
        m = {
            "xo_fm": _fm(z2[t0g:t0g + T]),
            "xb2_fm": _fm(xf[t0g:t0g + T]),
            "w1": np.asarray(w["W1"])[None], "w2": np.asarray(w["W2"])[None],
            "b1c": np.ascontiguousarray(b1cols),
            "b2cc": _cols(w["b2"]),
            "g2c": _cols(w["g2"]), "be2c": _cols(w["beta2"]),
        }
        m.update(consts2)
        in_maps.append(m)
    res3 = run_bass_kernel_spmd(nc3, in_maps, core_ids=list(range(NCORES)))
    out = np.empty((B * S, D), dtype=np.float32)
    for c in range(NCORES):
        out[c * T:(c + 1) * T] = res3.results[c]["xout_fm"].T
    return out.reshape(B, S, D)



def scan_consts():
    c = {}
    c["sc_iota_j"] = np.arange(128, dtype=np.float32).reshape(1, 128)
    c["sc_iota127"] = (127.0 - np.arange(128, dtype=np.float32)).reshape(128, 1)
    deltaT = np.zeros((128, 128), dtype=np.float32)
    maskT = np.zeros((128, 128), dtype=np.float32)
    for i in range(128):
        for j_ in range(128):
            if i < j_:
                deltaT[i, j_] = j_ - 1 - i
                maskT[i, j_] = 1.0
    c["sc_deltaT"] = deltaT
    c["sc_maskT"] = maskT
    c["sc_ones_f32"] = np.ones((1, 128), dtype=np.float32)
    return c


def emit_scan_wprep(p, nc, sb, psd, bf_, cm, basd, coefd, wst, pm0, pm1, pbt,
                    names=("q", "k", "v", "o")):
    bb = sb("bb", [128, NCH * 64], BF16)
    ct = sb("ct", [128, 64], BF16)
    basT = sb("basT", [64, D], BF16)
    cqT = sb("cqT", [64, 128], BF16)
    lgs = sb("lgs", [128, D])
    colm = sb("colm", [128, 2])
    out = {}
    p.dma("sync", chunks(wst[:, 0:NCH * 64], NCH, 64),
          basd.rearrange("(c p) n -> p c n", p=128), writes=[wst])
    p.op("vector", lambda v: v.tensor_copy(bb[:, :], wst[:, 0:NCH * 64]),
         reads=[wst], writes=[bb])
    for c in range(NCH):
        p.op("tensor", lambda t, c=c: t.transpose(
            pbt[0:64, 0:128], bb[:, c * 64:(c + 1) * 64], cm.identb[:, 0:128]),
            reads=[bb, cm.identb], writes=[pbt])
        p.op("vector", lambda v, c=c: v.tensor_copy(
            basT[0:64, c * 128:(c + 1) * 128], pbt[0:64, 0:128]),
            reads=[pbt], writes=[basT])
    for nm in names:
        cd = coefd[nm]
        p.dma("sync", wst[:, 0:64], cd, writes=[wst])
        p.op("vector", lambda v: v.tensor_copy(ct[:, :], wst[:, 0:64]),
             reads=[wst], writes=[ct])
        p.op("tensor", lambda t: t.transpose(pbt[0:64, 0:128], ct[:, :],
                                             cm.identb[:, 0:128]),
             reads=[ct, cm.identb], writes=[pbt])
        p.op("vector", lambda v: v.tensor_copy(cqT[0:64, 0:128], pbt[0:64, 0:128]),
             reads=[pbt], writes=[cqT])
        for half in range(2):
            pm = pm0 if half == 0 else pm1
            mm_group(p, pm[:, 0:TT],
                     [(cqT[0:64, 0:128], basT[0:64, half * TT:(half + 1) * TT])],
                     reads=[cqT, basT], writes=[pm])
        if nm == "o":
            owT = sb("owT", [128, D], BF16)
            for half in range(2):
                pm = pm0 if half == 0 else pm1
                p.op("vector", lambda v, half=half, pm=pm, owT=owT: v.tensor_copy(
                    owT[:, half * TT:(half + 1) * TT], pm[:, 0:TT]),
                    reads=[pm], writes=[owT])
            out["o"] = owT
            continue
        p.op("vector", lambda v: v.reduce_max(colm[:, 0:1], pm0[:, 0:TT], axis=AX.X),
             reads=[pm0], writes=[colm])
        p.op("vector", lambda v: v.reduce_max(colm[:, 1:2], pm1[:, 0:TT], axis=AX.X),
             reads=[pm1], writes=[colm])
        p.op("vector", lambda v: v.tensor_tensor(colm[:, 0:1], colm[:, 0:1],
                                                 colm[:, 1:2], op=ALU.max),
             reads=[colm], writes=[colm])
        p.op("vector", lambda v: v.tensor_scalar(colm[:, 0:1], colm[:, 0:1],
                                                 -1.0, None, op0=ALU.mult),
             reads=[colm], writes=[colm])
        for half in range(2):
            pm = pm0 if half == 0 else pm1
            p.op("scalar", lambda s, half=half, pm=pm: s.activation(
                lgs[:, half * TT:(half + 1) * TT], pm[:, 0:TT], AF.Exp,
                bias=colm[:, 0:1]),
                reads=[pm, colm], writes=[lgs])
        p.op("vector", lambda v: v.reduce_sum(colm[:, 1:2], lgs[:, 0:D], axis=AX.X),
             reads=[lgs], writes=[colm])
        p.op("vector", lambda v: v.reciprocal(colm[:, 1:2], colm[:, 1:2]),
             reads=[colm], writes=[colm])
        wsm = sb(f"wsm_{nm}", [128, D], BF16)
        p.op("vector", lambda v, wsm=wsm: v.tensor_scalar(wsm[:, 0:D], lgs[:, 0:D],
                                                 colm[:, 1:2], None, op0=ALU.mult),
             reads=[lgs, colm], writes=[wsm])
        wnb = sb(f"w_{nm}", [128, NCH * 128], BF16)
        p.op("vector", lambda v, wnb=wnb: v.memset(wnb[:, :], 0.0), writes=[wnb])
        for c in range(NCH):
            p.op("tensor", lambda t, c=c, wsm=wsm: t.transpose(
                pbt[:, 0:128], wsm[:, c * 128:(c + 1) * 128], cm.identb[:, 0:128]),
                reads=[wsm, cm.identb], writes=[pbt])
            p.op("vector", lambda v, c=c, wnb=wnb: v.tensor_copy(
                wnb[:, c * 128:(c + 1) * 128], pbt[:, 0:128]),
                reads=[pbt], writes=[wnb])
        out[nm] = wnb
    return out


def emit_decay(p, nc, sb, bf_, cm, decd, iota_j, iota127, deltaT_t, maskT_t):
    dd = NS()
    dd.sc = sb("dsc", [1, 2])
    dd.ldcol = sb("ldcol", [128, 1])
    dd.dprow = sb("dprow", [1, 128], BF16)
    dd.d127col = sb("d127col", [128, 1])
    dd.dp128col = sb("dp128col", [128, 1])
    dd.dmaskT = sb("dmaskT", [128, 128], BF16)
    dd.dmf = sb("dmf", [128, 128])
    p.dma("sync", dd.sc[0:1, 0:1], decd, writes=[dd.sc])
    p.op("scalar", lambda s: s.activation(dd.sc[0:1, 0:1], dd.sc[0:1, 0:1],
                                          AF.Sigmoid),
         reads=[dd.sc], writes=[dd.sc])
    p.op("scalar", lambda s: s.activation(dd.sc[0:1, 1:2], dd.sc[0:1, 0:1], AF.Ln),
         reads=[dd.sc], writes=[dd.sc])
    ld = dd.sc[0:1, 1:2]
    def bcast_ld(t):
        return t.matmul(bf_.pbc[:, 0:1], cm.onesf[0:1, 0:128], ld, start=True,
                        stop=True)
    p.op("tensor", bcast_ld, reads=[dd.sc, cm.onesf], writes=[bf_.pbc])
    p.op("vector", lambda v: v.tensor_copy(dd.ldcol[:, 0:1], bf_.pbc[:, 0:1]),
         reads=[bf_.pbc], writes=[dd.ldcol])
    p.op("scalar", lambda s: s.activation(dd.dprow[0:1, 0:128], iota_j[0:1, 0:128],
                                          AF.Exp, scale=ld),
         reads=["iota_j", dd.sc], writes=[dd.dprow])
    p.op("scalar", lambda s: s.activation(dd.d127col[:, 0:1], iota127[:, 0:1],
                                          AF.Exp, scale=dd.ldcol[:, 0:1]),
         reads=["iota127", dd.ldcol], writes=[dd.d127col])
    p.op("scalar", lambda s: s.activation(dd.dp128col[:, 0:1], dd.ldcol[:, 0:1],
                                          AF.Exp, scale=128.0),
         reads=[dd.ldcol], writes=[dd.dp128col])
    p.op("scalar", lambda s: s.activation(dd.dmf[:, :], deltaT_t[:, :], AF.Exp,
                                          scale=dd.ldcol[:, 0:1]),
         reads=["deltaT", dd.ldcol], writes=[dd.dmf])
    p.op("vector", lambda v: v.tensor_tensor(dd.dmaskT[:, :], dd.dmf[:, :],
                                             maskT_t[:, :], op=ALU.mult),
         reads=[dd.dmf, "maskT"], writes=[dd.dmaskT])
    return dd


def build_scan_a(n_own):
    nc = bass.Bass("TRN2", target_bir_lowering=False)
    xo = nc.dram_tensor("xo_fm", [D, n_own], F32, kind="ExternalInput").ap()
    basd = nc.dram_tensor("basis", [D, 64], F32, kind="ExternalInput").ap()
    qcd = nc.dram_tensor("qcoef", [128, 64], F32, kind="ExternalInput").ap()
    kcd = nc.dram_tensor("kcoef", [128, 64], F32, kind="ExternalInput").ap()
    vcd = nc.dram_tensor("vcoef", [128, 64], F32, kind="ExternalInput").ap()
    ocd = nc.dram_tensor("ocoef", [128, 64], F32, kind="ExternalInput").ap()
    decd = nc.dram_tensor("decay", [1, 1], F32, kind="ExternalInput").ap()
    oscd = nc.dram_tensor("oscol", [128, 1], F32, kind="ExternalInput").ap()
    cap = declare_consts(nc)
    scap = {}
    for name, arr in scan_consts().items():
        scap[name] = nc.dram_tensor(name, list(arr.shape), F32, kind="ExternalInput").ap()
    identb_in = nc.dram_tensor("identb_in", [128, 128], BF16, kind="ExternalInput").ap()
    xout = nc.dram_tensor("xout_fm", [D, n_own], F32, kind="ExternalOutput").ap()
    sout = nc.dram_tensor("sout", [128, 128], F32, kind="ExternalOutput").ap()
    qout = nc.dram_tensor("qout", [128, n_own], BF16, kind="ExternalOutput").ap()
    nt = cdiv(n_own, TT)
    ncc = n_own // 128

    with contextlib.ExitStack() as ctx:
        def sb(name, shape, dt=F32):
            return ctx.enter_context(nc.sbuf_tensor(name, shape, dt))

        def psd(name, shape, dt=F32):
            return ctx.enter_context(nc.psum_tensor(name, shape, dt))

        bf_ = NS()
        bf_.xt = sb("xt", [128, NCH * TT])
        bf_.dt_ = sb("dt", [128, NCH * TT])
        bf_.sqbf = sb("sqbf", [128, NCH * TT], BF16)
        bf_.xnb = sb("xnb", [128, NCH * TT], BF16)
        bf_.r1 = sb("r1", [1, TT])
        bf_.r1b = sb("r1b", [1, TT], BF16)
        qT = sb("qT", [128, n_own], BF16)
        kT = sb("kT", [128, n_own], BF16)
        vT = sb("vT", [128, n_own], BF16)
        rt = sb("rt", [128, n_own], BF16)
        vtk = sb("vtk", [128, 128], BF16)
        ktk = sb("ktk", [128, 128], BF16)
        kdec = sb("kdec", [128, 128], BF16)
        qd = sb("qd", [128, 128], BF16)
        amb = sb("amb", [128, 128], BF16)
        Mst = sb("Mst", [128, 128])
        Mstb = sb("Mstb", [128, 128], BF16)
        wst = sb("wst", [128, 4 * D])
        iotaj = sb("iotaj", [1, 128])
        iota127 = sb("iota127", [128, 1])
        deltaT_t = sb("deltaT_t", [128, 128])
        maskT_t = sb("maskT_t", [128, 128])
        oscol = sb("oscol_sb", [128, 1])

        bf_.pr1 = psd("pr1", [1, TT])
        bf_.pbc = psd("pbc", [128, TT])
        pm0 = psd("pm0", [128, TT])
        pm1 = psd("pm1", [128, TT])
        pbt2 = psd("pbt2", [128, 128], BF16)
        pA = psd("pA", [128, 128])
        pS = psd("pS", [128, 128])
        prt = psd("prt", [128, 128])

        p = Prog(nc, ctx)
        cm = load_common(p, sb, cap)
        cm.identb = sb("identb", [128, 128], BF16)
        cm.onesf = sb("onesf", [1, 128])
        p.dma("sync", cm.identb[:, :], identb_in, writes=[cm.identb])
        p.dma("sync", cm.onesf[:, :], scap["sc_ones_f32"], writes=[cm.onesf])
        p.dma("sync", iotaj[:, :], scap["sc_iota_j"], writes=["iota_j"])
        p.dma("sync", iota127[:, :], scap["sc_iota127"], writes=["iota127"])
        p.dma("sync", deltaT_t[:, :], scap["sc_deltaT"], writes=["deltaT"])
        p.dma("sync", maskT_t[:, :], scap["sc_maskT"], writes=["maskT"])
        p.dma("sync", oscol[:, :], oscd, writes=[oscol])

        wd = emit_scan_wprep(p, nc, sb, psd, bf_, cm, basd,
                             {"q": qcd, "k": kcd, "v": vcd, "o": ocd}, wst,
                             pm0, pm1, pbt2)
        dd = emit_decay(p, nc, sb, bf_, cm, decd, iotaj, iota127,
                        deltaT_t, maskT_t)

        for j in range(nt):
            tw = TT
            t0 = j * TT
            p.dma("sync", chunks(bf_.xt[:, 0:NCH * tw], NCH, tw),
                  xo[:, t0:t0 + tw].rearrange("(c p) t -> p c t", p=128),
                  writes=[bf_.xt])
            emit_ln(p, cm, bf_, tw)
            for nm, dst in [("q", qT), ("k", kT), ("v", vT)]:
                wnb = wd[nm]
                pm = pm0 if nm != "k" else pm1
                mm_group(p, pm[:, 0:tw],
                         [(wnb[:, k * 128:(k + 1) * 128],
                           bf_.xnb[:, k * tw:(k + 1) * tw]) for k in range(NCH)],
                         reads=[wnb, bf_.xnb], writes=[pm])
                p.op("vector", lambda v, dst=dst, t0=t0, tw=tw, pm=pm: v.tensor_copy(
                    dst[:, t0:t0 + tw], pm[:, 0:tw]),
                    reads=[pm], writes=[dst])
        p.op("vector", lambda v: v.memset(Mst[:, :], 0.0), writes=[Mst])
        p.op("vector", lambda v: v.memset(Mstb[:, :], 0.0), writes=[Mstb])
        for cc in range(ncc):
            c0 = cc * 128
            p.op("tensor", lambda t, c0=c0: t.transpose(
                pbt2[:, 0:128], vT[:, c0:c0 + 128], cm.identb[:, 0:128]),
                reads=[vT, cm.identb], writes=[pbt2])
            p.op("vector", lambda v: v.tensor_copy(vtk[:, :], pbt2[:, 0:128]),
                 reads=[pbt2], writes=[vtk])
            p.op("tensor", lambda t, c0=c0: t.transpose(
                pbt2[:, 0:128], kT[:, c0:c0 + 128], cm.identb[:, 0:128]),
                reads=[kT, cm.identb], writes=[pbt2])
            p.op("vector", lambda v: v.tensor_copy(ktk[:, :], pbt2[:, 0:128]),
                 reads=[pbt2], writes=[ktk])
            p.op("vector", lambda v: v.tensor_scalar(
                kdec[:, :], ktk[:, :], dd.d127col[:, 0:1], None, op0=ALU.mult),
                reads=[ktk, dd.d127col], writes=[kdec])
            mm_group(p, pA[:, 0:128],
                     [(kT[:, c0:c0 + 128], qT[:, c0:c0 + 128])],
                     reads=[kT, qT], writes=[pA])
            p.op("vector", lambda v: v.tensor_tensor(
                amb[:, :], pA[:, 0:128], dd.dmaskT[:, :], op=ALU.mult),
                reads=[pA, dd.dmaskT], writes=[amb])
            mm_group(p, bf_.pbc[:, 0:128],
                     [(cm.onesb[0:1, 0:128], dd.dprow[0:1, 0:128])],
                     reads=[cm.onesb, dd.dprow], writes=[bf_.pbc])
            p.op("vector", lambda v, c0=c0: v.tensor_tensor(
                qd[:, :], qT[:, c0:c0 + 128], bf_.pbc[:, 0:128], op=ALU.mult),
                reads=[qT, bf_.pbc], writes=[qd])
            mm_group(p, prt[:, 0:128],
                     [(Mstb[:, :], qd[:, :]), (vtk[:, :], amb[:, :])],
                     reads=[Mstb, qd, vtk, amb], writes=[prt])
            p.op("vector", lambda v, c0=c0: v.tensor_copy(
                rt[:, c0:c0 + 128], prt[:, 0:128]),
                reads=[prt], writes=[rt])
            mm_group(p, pS[:, 0:128], [(kdec[:, :], vtk[:, :])],
                     reads=[kdec, vtk], writes=[pS])
            p.op("vector", lambda v: v.scalar_tensor_tensor(
                Mst[:, :], Mst[:, :], dd.dp128col[:, 0:1], pS[:, 0:128],
                op0=ALU.mult, op1=ALU.add),
                reads=[pS, Mst, dd.dp128col], writes=[Mst])
            p.op("vector", lambda v: v.tensor_copy(Mstb[:, :], Mst[:, :]),
                 reads=[Mst], writes=[Mstb])
        for j in range(nt):
            tw = TT
            t0 = j * TT
            p.dma("sync", chunks(bf_.xt[:, 0:NCH * tw], NCH, tw),
                  xo[:, t0:t0 + tw].rearrange("(c p) t -> p c t", p=128),
                  writes=[bf_.xt])
            for m in range(NCH):
                pm = pm0 if m % 2 == 0 else pm1
                mm_group(p, pm[:, 0:tw],
                         [(wd["o"][:, m * 128:(m + 1) * 128], rt[:, t0:t0 + tw])],
                         reads=[wd["o"], rt], writes=[pm])
                p.op("vector", lambda v, m=m, tw=tw, pm=pm: v.scalar_tensor_tensor(
                    bf_.xt[:, m * tw:(m + 1) * tw], pm[:, 0:tw],
                    oscol[:, 0:1], bf_.xt[:, m * tw:(m + 1) * tw],
                    op0=ALU.mult, op1=ALU.add),
                    reads=[pm, oscol, bf_.xt], writes=[bf_.xt])
            p.dma("sync", xout[:, t0:t0 + tw].rearrange("(c p) t -> p c t", p=128),
                  chunks(bf_.xt[:, 0:NCH * tw], NCH, tw),
                  reads=[bf_.xt], writes=[xout])
        p.dma("sync", sout, Mst[:, :], reads=[Mst], writes=[sout])
        p.dma("sync", qout, qT[:, :], reads=[qT], writes=[qout])
        p.finish()
        with nc.Block() as block:
            p.emit(block)
    return nc


def build_scan_b(n_own):
    nc = bass.Bass("TRN2", target_bir_lowering=False)
    xo = nc.dram_tensor("xo_fm", [D, n_own], F32, kind="ExternalInput").ap()
    qin = nc.dram_tensor("qin", [128, n_own], BF16, kind="ExternalInput").ap()
    sstk = nc.dram_tensor("sstk", [3, 128, 128], F32, kind="ExternalInput").ap()
    basd = nc.dram_tensor("basis", [D, 64], F32, kind="ExternalInput").ap()
    ocd = nc.dram_tensor("ocoef", [128, 64], F32, kind="ExternalInput").ap()
    decd = nc.dram_tensor("decay", [1, 1], F32, kind="ExternalInput").ap()
    oscd = nc.dram_tensor("oscol", [128, 1], F32, kind="ExternalInput").ap()
    iott = nc.dram_tensor("iota_t", [1, n_own], F32, kind="ExternalInput").ap()
    cap = declare_consts(nc)
    scap = {}
    for name, arr in scan_consts().items():
        scap[name] = nc.dram_tensor(name, list(arr.shape), F32, kind="ExternalInput").ap()
    identb_in = nc.dram_tensor("identb_in", [128, 128], BF16, kind="ExternalInput").ap()
    xout = nc.dram_tensor("xout_fm", [D, n_own], F32, kind="ExternalOutput").ap()
    nt = cdiv(n_own, TT)

    with contextlib.ExitStack() as ctx:
        def sb(name, shape, dt=F32):
            return ctx.enter_context(nc.sbuf_tensor(name, shape, dt))

        def psd(name, shape, dt=F32):
            return ctx.enter_context(nc.psum_tensor(name, shape, dt))

        bf_ = NS()
        bf_.xt = sb("xt", [128, NCH * TT])
        bf_.dt_ = sb("dt", [128, NCH * TT])
        bf_.sqbf = sb("sqbf", [128, NCH * TT], BF16)
        bf_.xnb = sb("xnb", [128, NCH * TT], BF16)
        bf_.r1 = sb("r1", [1, TT])
        bf_.r1b = sb("r1b", [1, TT], BF16)
        qT = sb("qT", [128, n_own], BF16)
        qd = sb("qd", [128, TT], BF16)
        rc = sb("rc", [128, TT], BF16)
        Mi = sb("Mi", [128, 128])
        Mib = sb("Mib", [128, 128], BF16)
        stile = sb("stile", [128, 3 * 128])
        wst = sb("wst", [128, 4 * D])
        iotat = sb("iotat", [1, n_own])
        dgrow = sb("dgrow", [1, n_own], BF16)
        ecol = sb("ecol", [128, 3])
        oscol = sb("oscol_sb", [128, 1])

        bf_.pr1 = psd("pr1", [1, TT])
        bf_.pbc = psd("pbc", [128, TT])
        pm0 = psd("pm0", [128, TT])
        pm1 = psd("pm1", [128, TT])
        pbt = psd("pbt", [128, 128], BF16)

        p = Prog(nc, ctx)
        cm = load_common(p, sb, cap)
        cm.identb = sb("identb", [128, 128], BF16)
        cm.onesf = sb("onesf", [1, 128])
        p.dma("sync", cm.identb[:, :], identb_in, writes=[cm.identb])
        p.dma("sync", cm.onesf[:, :], scap["sc_ones_f32"], writes=[cm.onesf])
        p.dma("sync", oscol[:, :], oscd, writes=[oscol])
        p.dma("sync", iotat[:, :], iott, writes=[iotat])
        p.dma("sync", qT[:, :], qin, writes=[qT])
        p.dma("sync", stile[:, :].rearrange("p (s q) -> p s q", s=3),
              sstk.rearrange("s p q -> p s q"), writes=[stile])

        wd = emit_scan_wprep(p, nc, sb, psd, bf_, cm, basd, {"o": ocd}, wst,
                             pm0, pm1, pbt, names=("o",))
        dsc = sb("dsc2", [1, 2])
        ldcol = sb("ldcol2", [128, 1])
        p.dma("sync", dsc[0:1, 0:1], decd, writes=[dsc])
        p.op("scalar", lambda s: s.activation(dsc[0:1, 0:1], dsc[0:1, 0:1], AF.Sigmoid),
             reads=[dsc], writes=[dsc])
        p.op("scalar", lambda s: s.activation(dsc[0:1, 1:2], dsc[0:1, 0:1], AF.Ln),
             reads=[dsc], writes=[dsc])
        p.op("tensor", lambda t: t.matmul(bf_.pbc[:, 0:1], cm.onesf[0:1, 0:128],
                                          dsc[0:1, 1:2], start=True, stop=True),
             reads=[dsc, cm.onesf], writes=[bf_.pbc])
        p.op("vector", lambda v: v.tensor_copy(ldcol[:, 0:1], bf_.pbc[:, 0:1]),
             reads=[bf_.pbc], writes=[ldcol])
        for kk in range(3):
            p.op("scalar", lambda s, kk=kk: s.activation(
                ecol[:, kk:kk + 1], ldcol[:, 0:1], AF.Exp, scale=2048.0 * (kk + 1)),
                reads=[ldcol], writes=[ecol])
        p.op("scalar", lambda s: s.activation(dgrow[0:1, 0:n_own], iotat[0:1, 0:n_own],
                                              AF.Exp, scale=dsc[0:1, 1:2]),
             reads=[iotat, dsc], writes=[dgrow])
        p.op("vector", lambda v: v.tensor_scalar(Mi[:, :], stile[:, 0:128],
                                                 ecol[:, 0:1], None, op0=ALU.mult),
             reads=[stile, ecol], writes=[Mi])
        p.op("vector", lambda v: v.scalar_tensor_tensor(
            Mi[:, :], stile[:, 128:256], ecol[:, 1:2], Mi[:, :],
            op0=ALU.mult, op1=ALU.add),
            reads=[stile, ecol, Mi], writes=[Mi])
        p.op("vector", lambda v: v.scalar_tensor_tensor(
            Mi[:, :], stile[:, 256:384], ecol[:, 2:3], Mi[:, :],
            op0=ALU.mult, op1=ALU.add),
            reads=[stile, ecol, Mi], writes=[Mi])
        p.op("vector", lambda v: v.tensor_copy(Mib[:, :], Mi[:, :]),
             reads=[Mi], writes=[Mib])

        for j in range(nt):
            tw = TT
            t0 = j * TT
            p.dma("sync", chunks(bf_.xt[:, 0:NCH * tw], NCH, tw),
                  xo[:, t0:t0 + tw].rearrange("(c p) t -> p c t", p=128),
                  writes=[bf_.xt])
            mm_group(p, bf_.pbc[:, 0:tw],
                     [(cm.onesb[0:1, 0:128], dgrow[0:1, t0:t0 + tw])],
                     reads=[cm.onesb, dgrow], writes=[bf_.pbc])
            p.op("vector", lambda v, t0=t0, tw=tw: v.tensor_tensor(
                qd[:, 0:tw], qT[:, t0:t0 + tw], bf_.pbc[:, 0:tw], op=ALU.mult),
                reads=[qT, bf_.pbc], writes=[qd])
            mm_group(p, pm0[:, 0:tw], [(Mib[:, :], qd[:, 0:tw])],
                     reads=[Mib, qd], writes=[pm0])
            p.op("vector", lambda v: v.tensor_copy(rc[:, 0:tw], pm0[:, 0:tw]),
                 reads=[pm0], writes=[rc])
            for m in range(NCH):
                pm = pm1
                mm_group(p, pm[:, 0:tw],
                         [(wd["o"][:, m * 128:(m + 1) * 128], rc[:, 0:tw])],
                         reads=[wd["o"], rc], writes=[pm])
                p.op("vector", lambda v, m=m, tw=tw, pm=pm: v.scalar_tensor_tensor(
                    bf_.xt[:, m * tw:(m + 1) * tw], pm[:, 0:tw],
                    oscol[:, 0:1], bf_.xt[:, m * tw:(m + 1) * tw],
                    op0=ALU.mult, op1=ALU.add),
                    reads=[pm, oscol, bf_.xt], writes=[bf_.xt])
            p.dma("sync", xout[:, t0:t0 + tw].rearrange("(c p) t -> p c t", p=128),
                  chunks(bf_.xt[:, 0:NCH * tw], NCH, tw),
                  reads=[bf_.xt], writes=[xout])
        p.finish()
        with nc.Block() as block:
            p.emit(block)
    return nc


def run_scan(x_cur, w):
    consts = host_constants()
    sconsts = scan_consts()
    identb = np.eye(128, dtype=BF)
    xf = np.asarray(x_cur).reshape(B * S, D)
    oscol = np.broadcast_to(np.asarray(w["out_scale"]).reshape(1, 1),
                            (128, 1)).astype(np.float32).copy()
    nc1 = build_scan_a(T)
    in_maps = []
    base = {
        "basis": np.asarray(w["basis"]),
        "qcoef": np.asarray(w["q_coeffs"]), "kcoef": np.asarray(w["k_coeffs"]),
        "vcoef": np.asarray(w["v_coeffs"]), "ocoef": np.asarray(w["o_coeffs"]),
        "decay": np.asarray(w["decay"]).reshape(1, 1),
        "oscol": oscol, "identb_in": identb,
    }
    for c in range(NCORES):
        m = {"xo_fm": _fm(xf[c * T:(c + 1) * T])}
        m.update(base)
        m.update(consts)
        m.update(sconsts)
        in_maps.append(m)
    res1 = run_bass_kernel_spmd(nc1, in_maps, core_ids=list(range(NCORES)))
    xpart = [res1.results[c]["xout_fm"] for c in range(NCORES)]
    souts = [res1.results[c]["sout"] for c in range(NCORES)]
    qouts = [res1.results[c]["qout"] for c in range(NCORES)]

    nc2 = build_scan_b(T)
    in_maps = []
    iota_t = np.arange(T, dtype=np.float32).reshape(1, T)
    for c in range(NCORES):
        r = c % 4
        stk = np.zeros((3, 128, 128), dtype=np.float32)
        for kk in range(1, 4):
            if r - kk >= 0:
                stk[kk - 1] = souts[c - kk]
        m = {
            "xo_fm": np.ascontiguousarray(xpart[c]),
            "qin": qouts[c], "sstk": stk,
            "basis": base["basis"], "ocoef": base["ocoef"],
            "decay": base["decay"], "oscol": oscol,
            "iota_t": iota_t, "identb_in": identb,
        }
        m.update(consts)
        m.update(sconsts)
        in_maps.append(m)
    res2 = run_bass_kernel_spmd(nc2, in_maps, core_ids=list(range(NCORES)))
    out = np.empty((B * S, D), dtype=np.float32)
    for c in range(NCORES):
        out[c * T:(c + 1) * T] = res2.results[c]["xout_fm"].T
    return out.reshape(B, S, D)



def build_wcast(ncols):
    nc = bass.Bass("TRN2", target_bir_lowering=False)
    src = nc.dram_tensor("wsrc", [128, ncols], F32, kind="ExternalInput").ap()
    dst = nc.dram_tensor("wdst", [128, ncols], BF16, kind="ExternalOutput").ap()
    CW = 16384
    nt = cdiv(ncols, CW)
    with contextlib.ExitStack() as ctx:
        def sb(name, shape, dt=F32):
            return ctx.enter_context(nc.sbuf_tensor(name, shape, dt))
        tA = sb("tA", [128, CW])
        tB = sb("tB", [128, CW])
        oA = sb("oA", [128, CW], BF16)
        oB = sb("oB", [128, CW], BF16)
        p = Prog(nc, ctx)
        for j in range(nt):
            t = tA if j % 2 == 0 else tB
            o = oA if j % 2 == 0 else oB
            c0 = j * CW
            cw = min(CW, ncols - c0)
            hw = cw // 2
            p.dma("sync", t[:, 0:cw], src[:, c0:c0 + cw], writes=[t])
            p.op("vector", lambda v, t=t, o=o, hw=hw: v.tensor_copy(
                o[:, 0:hw], t[:, 0:hw]), reads=[t], writes=[o])
            p.op("scalar", lambda s, t=t, o=o, hw=hw, cw=cw: s.copy(
                o[:, hw:cw], t[:, hw:cw]), reads=[t], writes=[o])
            p.dma("gpsimd", dst[:, c0:c0 + cw], o[:, 0:cw], reads=[o],
                  writes=[dst])
        p.finish()
        with nc.Block() as block:
            p.emit(block)
    return nc


def run_wcast(arrs):
    flat = np.concatenate([np.asarray(a, np.float32).reshape(-1) for a in arrs])
    per = cdiv(flat.size, NCORES * 128)
    per = cdiv(per, 256) * 256
    buf = np.zeros(per * 128 * NCORES, np.float32)
    buf[:flat.size] = flat
    shards = buf.reshape(NCORES, 128, per)
    nc = build_wcast(per)
    in_maps = [{"wsrc": np.ascontiguousarray(shards[c])} for c in range(NCORES)]
    res = run_bass_kernel_spmd(nc, in_maps, core_ids=list(range(NCORES)))
    out = np.concatenate([np.asarray(res.results[c]["wdst"]).reshape(-1)
                          for c in range(NCORES)])
    outs = []
    off = 0
    for a in arrs:
        outs.append(out[off:off + a.size].reshape(a.shape))
        off += a.size
    return outs


def _retry(fn):
    try:
        return fn()
    except Exception:
        return fn()


def kernel(**inputs):
    g = {k: np.ascontiguousarray(np.asarray(v, dtype=np.float32))
         for k, v in inputs.items()}
    x = g["L0_attn_a_x"]
    w = {n: g[f"L0_attn_a_{n}"] for n in
         ["Wq", "bq", "Wk", "bk", "Wv", "bv", "Wo", "bo", "projection"]}
    x = _retry(lambda: run_performer(x, w))
    wf = {"W_r": g["L0_ffn_W_r"], "W_k": g["L0_ffn_W_k"], "W_v": g["L0_ffn_W_v"],
          "mu_r": g["L0_ffn_mu_r"], "mu_k": g["L0_ffn_mu_k"]}
    x = _retry(lambda: run_rwkv(x, wf))
    cast6 = _retry(lambda: run_wcast(
        [g["L0_attn_b_w1"], g["L0_attn_b_w2"], g["L1_ffn_W1"], g["L1_ffn_W2"],
         g["L1_attn_a_W1"], g["L1_attn_a_W2"]]))
    wm0 = {"gate_w": g["L0_attn_b_gate_w"], "gate_b": g["L0_attn_b_gate_b"],
           "w1": cast6[0], "b1": g["L0_attn_b_b1"],
           "w2": cast6[1], "b2": g["L0_attn_b_b2"]}
    x = _retry(lambda: run_moe(x, wm0, topk=2, act="gelu"))
    x = _retry(lambda: run_rwkv(x, wf))
    wa = {n: g[f"L1_attn_a_{n}"] for n in
          ["Wq", "bq", "Wk", "bk", "Wv", "bv", "Wo", "bo", "W1", "b1",
           "W2", "b2", "g1", "beta1", "g2", "beta2"]}
    wa["W1"] = cast6[4]
    wa["W2"] = cast6[5]
    x = _retry(lambda: run_l1a(x, wa))
    wm1 = {"gate_w": g["L1_ffn_Wg"], "gate_b": g["L1_ffn_bg"],
           "w1": cast6[2], "b1": g["L1_ffn_b1"],
           "w2": cast6[3], "b2": g["L1_ffn_b2"]}
    x = _retry(lambda: run_moe(x, wm1, topk=1, act="silu"))
    ws = {"basis": g["L1_attn_b_basis"], "q_coeffs": g["L1_attn_b_q_coeffs"],
          "k_coeffs": g["L1_attn_b_k_coeffs"], "v_coeffs": g["L1_attn_b_v_coeffs"],
          "o_coeffs": g["L1_attn_b_o_coeffs"], "decay": g["L1_attn_b_decay"],
          "out_scale": g["L1_attn_b_out_scale"]}
    x = _retry(lambda: run_scan(x, ws))
    x = _retry(lambda: run_moe(x, wm1, topk=1, act="silu"))
    return np.asarray(x, dtype=np.float32)
